# version 07
# baseline (speedup 1.0000x reference)
import numpy as np
from contextlib import ExitStack

import concourse.bass as bass
import concourse.mybir as mybir
from concourse.bass_utils import run_bass_kernel_spmd
from concourse.alu_op_type import AluOpType as ALU

F32 = mybir.dt.float32
BF16 = mybir.dt.bfloat16
U32 = mybir.dt.uint32
U8 = mybir.dt.uint8
I32 = mybir.dt.int32
AF = mybir.ActivationFunctionType
AX = mybir.AxisListType

P = 128
T = 2048
NT = T // P
D = 1024
DC = D // P
NHEADS = 16
DHEAD = 64
NCORES = 8

BLOCK_ORDER = ["L0_attn", "L0_ffn", "L1_attn", "L1_ffn",
               "L2_attn", "L2_ffn", "L3_attn", "L3_ffn"]

BLOCK_WEIGHTS = {
    "L0_attn": [("L0_attn_qkv_weights", (1024, 3072))],
    "L0_ffn": [("L0_ffn_ln_g", (1024,)), ("L0_ffn_ln_b", (1024,)),
               ("L0_ffn_aw1", (1024, 4096)), ("L0_ffn_ab1", (4096,)),
               ("L0_ffn_aw2", (4096, 1024)), ("L0_ffn_ab2", (1024,)),
               ("L0_ffn_gate_w", (1024, 6)), ("L0_ffn_gate_b", (6,)),
               ("L0_ffn_ew1", (6, 1024, 2048)), ("L0_ffn_eb1", (6, 2048)),
               ("L0_ffn_ew2", (6, 2048, 1024)), ("L0_ffn_eb2", (6, 1024)),
               ("L0_ffn_gens", (4, 1024, 1024)), ("L0_ffn_gc_w", (2048, 1)),
               ("L0_ffn_gc_b", (1,))],
    "L1_attn": [("L1_attn_W_attn", (1024, 3072)), ("L1_attn_b_attn", (3072,)),
                ("L1_attn_W_proj", (1024, 1024)), ("L1_attn_b_proj", (1024,))],
    "L1_ffn": [("L1_ffn_Wg", (1024, 8)), ("L1_ffn_bg", (8,)),
               ("L1_ffn_W1", (8, 1024, 4096)), ("L1_ffn_W2", (8, 4096, 1024)),
               ("L1_ffn_W3", (8, 1024, 4096))],
    "L2_attn": [("L2_attn_sf_w", (1024, 1024)), ("L2_attn_sf_b", (1024,)),
                ("L2_attn_mask", (1024, 1024)), ("L2_attn_wq", (1024, 1024)),
                ("L2_attn_bq", (1024,)), ("L2_attn_wk", (1024, 1024)),
                ("L2_attn_bk", (1024,)), ("L2_attn_wv", (1024, 1024)),
                ("L2_attn_bv", (1024,)), ("L2_attn_wo", (1024, 1024)),
                ("L2_attn_bo", (1024,)), ("L2_attn_ff1_w", (4096, 1024)),
                ("L2_attn_ff1_b", (4096,)), ("L2_attn_ff2_w", (1024, 4096)),
                ("L2_attn_ff2_b", (1024,)), ("L2_attn_g1", (1024,)),
                ("L2_attn_b1", (1024,)), ("L2_attn_g2", (1024,)),
                ("L2_attn_b2", (1024,)), ("L2_attn_g3", (1024,)),
                ("L2_attn_b3", (1024,))],
    "L2_ffn": [("L2_ffn_Wg", (1024, 32)), ("L2_ffn_W1", (32, 1024, 512)),
               ("L2_ffn_W2", (32, 512, 1024)), ("L2_ffn_W3", (32, 1024, 512)),
               ("L2_ffn_Ws1", (1024, 1024)), ("L2_ffn_Ws2", (1024, 1024)),
               ("L2_ffn_Ws3", (1024, 1024))],
    "L3_attn": [("L3_attn_W_qkva", (3072, 1024)), ("L3_attn_gamma", (1024,))],
    "L3_ffn": [("L3_ffn_proj_w", (6, 1024)), ("L3_ffn_anchors", (6, 6)),
               ("L3_ffn_routing_temp", ()), ("L3_ffn_w1", (6, 1024, 512)),
               ("L3_ffn_w2", (6, 512, 1024))],
}



class Ctx:

    ENG = {"q": "sync", "g": "gpsimd", "t": "tensor", "s": "scalar", "v": "vector"}

    def __init__(self, nc, st):
        self.nc = nc
        self.sems = {}
        self.cnt = {}
        for name in ("qs", "qg", "pe", "act", "dve", "pool"):
            self.sems[name] = st.enter_context(nc.semaphore(name))
            self.cnt[name] = 0
        self.L = {k: [] for k in self.ENG}

    def op(self, lst, emit, inc=None, by=1):
        if inc is None:
            self.L[lst].append(emit)
            return None
        sem = self.sems[inc]

        def f(eng, emit=emit, sem=sem, by=by):
            r = emit(eng)
            r.then_inc(sem, by)
        self.L[lst].append(f)
        self.cnt[inc] += by
        return self.cnt[inc]

    def wait(self, lst, sem, target=None):
        t = self.cnt[sem] if target is None else target
        assert t <= self.cnt[sem], f"forward wait on {sem}: {t} > {self.cnt[sem]}"
        s = self.sems[sem]
        self.L[lst].append(lambda eng, s=s, t=t: eng.wait_ge(s, t))

    def dma(self, lst, out, in_, **kw):
        sem = "qs" if lst == "q" else "qg"
        return self.op(lst, lambda e, out=out, in_=in_, kw=kw:
                       e.dma_start(out=out, in_=in_, **kw), inc=sem, by=16)

    def flush(self, blk):
        for name, attr in self.ENG.items():
            lst = self.L[name]
            if lst:
                def body(eng, lst=lst):
                    for f in lst:
                        f(eng)
                getattr(blk, attr)(body)


def pipelined(ctx, n, mm, ev, ev_lst="s", depth=2):
    ev_sem = {"s": "act", "v": "dve"}[ev_lst]
    ev_done = []
    ctx.wait("t", ev_sem)
    for g in range(n):
        if g >= depth:
            ctx.wait("t", ev_sem, ev_done[g - depth])
        pe_t = ctx.op("t", mm(g), inc="pe")
        ctx.wait(ev_lst, "pe", pe_t)
        ev_done.append(ctx.op(ev_lst, ev(g), inc=ev_sem))


DBG_TAGS = set()
L2FFN_PHASE = 5



class Prog:
    def __init__(self, blocks):
        self.blocks = list(blocks)
        nc = self.nc = bass.Bass("TRN2", target_bir_lowering=False)
        self.inp = {}
        self.x_in = nc.dram_tensor("x_in", [T, D], F32, kind="ExternalInput").ap()
        self.x_out = nc.dram_tensor("x_out", [T, D], F32, kind="ExternalOutput").ap()
        for b in self.blocks:
            for name, shape in BLOCK_WEIGHTS[b]:
                shp = list(shape) if shape else [1]
                self.inp[name] = nc.dram_tensor(name, shp, F32,
                                                kind="ExternalInput").ap()
        self.dbg = {}

    def barrier(self, ctx):
        for lst in ("q", "g", "t", "s", "v"):
            for sem in ("pe", "act", "dve", "pool", "qs", "qg"):
                ctx.wait(lst, sem)

    def dbg_out(self, name, shape, dt=F32):
        ap = self.nc.dram_tensor(name, list(shape), dt, kind="ExternalOutput").ap()
        self.dbg[name] = ap
        return ap

    def snap(self, ctx, tag, src_ap, dt=BF16):
        if tag not in DBG_TAGS:
            return
        n = src_ap.shape[-1]
        dst = self.dbg_out("dbg_" + tag, [P, n], src_ap.dtype)
        for sem in ("pe", "act", "dve", "pool"):
            ctx.wait("q", sem)
        ctx.dma("q", out=dst[:, :], in_=src_ap)
        for lst in ("t", "s", "v", "g"):
            ctx.wait(lst, "qs")

    def build(self):
        nc = self.nc
        with ExitStack() as st:
            def sb(name, shape, dt):
                return st.enter_context(nc.sbuf_tensor(name, shape, dt))

            def ps(name, shape, dt=F32):
                return st.enter_context(nc.psum_tensor(name, shape, dt))
            s = self
            s.x_cur = sb("x_cur", [P, NT * D], F32)
            s.sbA = sb("sbA", [P, NT * D], BF16)
            s.sbB = sb("sbB", [P, NT * D], BF16)
            s.xt = sb("xt", [P, DC * T], BF16)
            s.wbuf = sb("wbuf", [P, 16 * 1024], BF16)
            s.sf = sb("sf", [P, 896], F32)
            s.cb = sb("cb", [P, 1152], F32)
            s.cc = sb("cc", [P, 2048 + 384], BF16)
            s.su = sb("su", [P, 64], U32)
            s.idf = sb("idf", [P, P], F32)
            s.ccf = sb("ccf", [P, 512], F32)
            s.PA = ps("PA", [P, 2048])
            s.PBb = ps("PBb", [P, 1024], BF16)
            s.PCb = ps("PCb", [P, 1024], BF16)
            s.PD = ps("PD", [P, 512])
            s.PE_ = ps("PE_", [P, 512])
            s.dscr_a = nc.dram_tensor("dscr_a", [33024, 1024], BF16, kind="Internal").ap()
            s.dscr_b = nc.dram_tensor("dscr_b", [33024, 1024], BF16, kind="Internal").ap()
            s.dscr_w = nc.dram_tensor("dscr_w", [8192, 4096], BF16, kind="Internal").ap()
            s.dscr_f = nc.dram_tensor("dscr_f", [T, D], F32, kind="Internal").ap()
            s.dscr_wt = nc.dram_tensor("dscr_wt", [1024, 3072], BF16,
                                       kind="Internal").ap()
            s.dscr_s = nc.dram_tensor("dscr_s", [64, 16], F32, kind="Internal").ap()

            ctx = self.ctx = Ctx(nc, st)
            blk = st.enter_context(nc.Block())
            self.emit_consts(ctx)
            self.emit_load_x(ctx)
            for b in self.blocks:
                self.barrier(ctx)
                getattr(self, "emit_" + b)(ctx)
            self.barrier(ctx)
            self.emit_store_x(ctx)
            ctx.flush(blk)
        return nc

    def emit_consts(self, ctx):
        s = self
        ctx.op("g", lambda g: g.iota(s.ccf[:, 0:128].bitcast(I32),
                                     pattern=[[-1, 128]], base=0,
                                     channel_multiplier=1), inc="pool")
        ctx.op("g", lambda g: g.iota(s.ccf[:, 128:128 + 384].bitcast(I32),
                                     pattern=[[1, 384]], base=0,
                                     channel_multiplier=-1), inc="pool")
        ctx.op("g", lambda g: g.iota(s.sf[:, 0:512].bitcast(I32),
                                     pattern=[[1, 512]], base=0,
                                     channel_multiplier=-1), inc="pool")
        ctx.wait("v", "pool")
        ctx.op("v", lambda v: v.tensor_scalar(s.idf[:], s.ccf[:, 0:128].bitcast(I32),
                                              0.0, None, op0=ALU.is_equal), inc="dve")
        ctx.op("v", lambda v: v.tensor_scalar(s.cc[:, 2048:2176],
                                              s.ccf[:, 0:128].bitcast(I32),
                                              0.0, None, op0=ALU.is_equal), inc="dve")
        ctx.op("v", lambda v: v.tensor_scalar(s.cc[:, 2176:2304],
                                              s.ccf[:, 0:128].bitcast(I32),
                                              -1.0, None, op0=ALU.is_le), inc="dve")
        for m in range(4):
            ctx.op("v", lambda v, m=m: v.tensor_scalar(
                s.cc[:, m * 512:(m + 1) * 512], s.sf[:, 0:512].bitcast(I32),
                float(128 * m), None, op0=ALU.is_ge), inc="dve")
        ctx.op("v", lambda v: v.memset(s.cc[:, 2304:2312], 1.0), inc="dve")
        ctx.op("g", lambda g: g.iota(s.ccf[:, 272:273].bitcast(I32),
                                     pattern=[[0, 1]], base=0,
                                     channel_multiplier=1), inc="pool")
        ctx.wait("v", "pool")
        ctx.op("v", lambda v: v.tensor_copy(s.sf[:, 272:273],
                                            s.ccf[:, 272:273].bitcast(I32)),
               inc="dve")
        ctx.op("v", lambda v: v.tensor_copy(s.ccf[:, 273:274],
                                            s.ccf[:, 272:273].bitcast(I32)),
               inc="dve")
        ctx.op("g", lambda g: g.iota(s.ccf[:, 264:272].bitcast(I32),
                                     pattern=[[1, 8]], base=0,
                                     channel_multiplier=0), inc="pool")
        ctx.wait("v", "pool")
        ctx.op("v", lambda v: v.tensor_copy(s.sf[:, 264:272],
                                            s.ccf[:, 264:272].bitcast(I32)),
               inc="dve")
        ctx.op("v", lambda v: v.memset(s.ccf[:, 256:257], 1e-5), inc="dve")
        self.c_dve_consts = ctx.cnt["dve"]

    def emit_load_x(self, ctx):
        s = self
        ctx.dma("q", out=s.x_cur[:].rearrange("p (i c) -> p i c", c=D),
                in_=s.x_in.rearrange("(i p) c -> p i c", p=P))
        self.c_x_loaded = ctx.cnt["qs"]

    def emit_store_x(self, ctx):
        s = self
        ctx.wait("q", "dve")
        ctx.wait("q", "act")
        ctx.dma("q", out=s.x_out.rearrange("(i p) c -> p i c", p=P),
                in_=s.x_cur[:].rearrange("p (i c) -> p i c", c=D))
        ctx.wait("q", "qs")

    def transpose_to_xt(self, ctx, src_f32, dst=None, src_tile_off=None):
        s = self
        dst = dst if dst is not None else s.xt
        quads = [(i, j4) for i in range(NT) for j4 in range(2)]

        def mm(g):
            i, j4 = quads[g]
            bank = [s.PD, s.PE_][g % 2]

            def f(t_, i=i, j4=j4, bank=bank):
                last = None
                for jj in range(4):
                    j = j4 * 4 + jj
                    last = t_.transpose(bank[:, jj * 128:(jj + 1) * 128],
                                        src_f32[:, i * D + j * 128:i * D + j * 128 + 128],
                                        s.idf[:])
                return last
            return f

        def ev(g):
            i, j4 = quads[g]
            bank = [s.PD, s.PE_][g % 2]

            def f(e_, i=i, j4=j4, bank=bank):
                last = None
                for jj in range(4):
                    j = j4 * 4 + jj
                    last = e_.copy(dst[:, j * T + i * 128:j * T + i * 128 + 128],
                                   bank[:, jj * 128:(jj + 1) * 128])
                return last
            return f
        pipelined(ctx, len(quads), mm, ev, "s")

    def ln_plain(self, ctx, make_sum_tile, out_buf=None):
        s = self
        out_buf = out_buf if out_buf is not None else s.x_cur
        for i in range(NT):
            ctx.op("v", lambda v, i=i: make_sum_tile(v, i))
            ctx.op("v", lambda v: v.reduce_sum(s.sf[:, 520:521], s.cb[:, 0:D],
                                               axis=AX.X, negate=True), inc="dve")
            ctx.wait("v", "dve")
            d1 = ctx.op("v", lambda v: v.tensor_scalar(
                s.sf[:, 512:513], s.sf[:, 520:521], 1.0 / D, None, op0=ALU.mult),
                inc="dve")
            ctx.wait("s", "dve", d1)
            ctx.op("s", lambda a: a.activation(s.sbB[:, 0:D],
                                               s.cb[:, 0:D],
                                               AF.Square, bias=s.sf[:, 512:513],
                                               scale=1.0,
                                               accum_out=s.sf[:, 516:517]), inc="act")
            ctx.wait("s", "act")
            a1 = ctx.op("s", lambda a: a.activation(
                s.sf[:, 514:515], s.sf[:, 516:517], AF.Sqrt,
                bias=s.ccf[:, 256:257], scale=1.0 / D), inc="act")
            ctx.wait("v", "act", a1)
            ctx.op("v", lambda v: v.reciprocal(s.sf[:, 515:516], s.sf[:, 514:515]),
                   inc="dve")
            ctx.wait("v", "dve")
            ctx.op("v", lambda v, i=i: v.tensor_scalar(
                out_buf[:, i * D:(i + 1) * D], s.cb[:, 0:D], s.sf[:, 512:513],
                s.sf[:, 515:516], op0=ALU.add, op1=ALU.mult), inc="dve")


    def load_bcast(self, ctx, vec_ap, dst, n, lst="q"):
        src = vec_ap
        if len(src.shape) == 1:
            src = src.unsqueeze(0)
        ctx.dma(lst, out=dst, in_=src.broadcast_to([P, n]))

    def load_chunkcol(self, ctx, vec_ap, dst, nch, lst="q"):
        ctx.dma(lst, out=dst, in_=vec_ap.rearrange("(a p) -> p a", p=P),
                allow_slow_non_contiguous=True)

    def bf_transpose_batch(self, ctx, pairs, depth=2):
        s = self
        groups = [pairs[i:i + 8] for i in range(0, len(pairs), 8)]

        def mm(g):
            def f(t_, g=g):
                bank = [s.PBb, s.PCb][g % 2]
                last = None
                for k, (inap, _outap) in enumerate(groups[g]):
                    last = t_.transpose(bank[0:inap.shape[-1], k * 128:(k + 1) * 128],
                                        inap, s.cc[:, 2048:2176])
                return last
            return f

        def ev(g):
            def f(e_, g=g):
                bank = [s.PBb, s.PCb][g % 2]
                last = None
                for k, (inap, outap) in enumerate(groups[g]):
                    np_ = outap.shape[0] if len(outap.shape) == 2 else P
                    last = e_.copy(outap, bank[0:np_, k * 128:k * 128 + outap.shape[-1]])
                return last
            return f
        pipelined(ctx, len(groups), mm, ev, "s", depth=depth)

    def ln_tile(self, ctx, out_ap, sq_scratch=None):
        s = self
        sq = sq_scratch if sq_scratch is not None else s.sbB[:, 4096:5120]
        ctx.op("v", lambda v: v.reduce_sum(s.sf[:, 520:521], s.cb[:, 0:1024],
                                           axis=AX.X, negate=True), inc="dve")
        ctx.wait("v", "dve")
        d1 = ctx.op("v", lambda v: v.tensor_scalar(
            s.sf[:, 512:513], s.sf[:, 520:521], 1.0 / D, None, op0=ALU.mult),
            inc="dve")
        ctx.wait("s", "dve", d1)
        ctx.op("s", lambda a: a.activation(sq, s.cb[:, 0:1024], AF.Square,
                                           bias=s.sf[:, 512:513], scale=1.0,
                                           accum_out=s.sf[:, 516:517]), inc="act")
        ctx.wait("s", "act")
        a2 = ctx.op("s", lambda a: a.activation(
            s.sf[:, 514:515], s.sf[:, 516:517], AF.Sqrt,
            bias=s.ccf[:, 256:257], scale=1.0 / D), inc="act")
        ctx.wait("v", "act", a2)
        ctx.op("v", lambda v: v.reciprocal(s.sf[:, 515:516], s.sf[:, 514:515]),
               inc="dve")
        ctx.wait("v", "dve")
        ctx.op("v", lambda v, out_ap=out_ap: v.tensor_scalar(
            out_ap, s.cb[:, 0:1024], s.sf[:, 512:513], s.sf[:, 515:516],
            op0=ALU.add, op1=ALU.mult), inc="dve")

    def ln_batch(self, ctx, tiles, sq_scratch):
        s = self
        n = len(tiles)
        assert n <= 16
        for i, (src, _o) in enumerate(tiles):
            ctx.op("v", lambda v, i=i, src=src: v.reduce_sum(
                s.sf[:, 600 + i:601 + i], src, axis=AX.X, negate=True), inc="dve")
        ctx.wait("v", "dve")
        d1 = ctx.op("v", lambda v, n=n: v.tensor_scalar(
            s.sf[:, 616:616 + n], s.sf[:, 600:600 + n], 1.0 / D, None,
            op0=ALU.mult), inc="dve")
        ctx.wait("s", "dve", d1)
        for i, (src, _o) in enumerate(tiles):
            ctx.op("s", lambda a, i=i, src=src: a.activation(
                sq_scratch, src, AF.Square, bias=s.sf[:, 616 + i:617 + i],
                scale=1.0, accum_out=s.sf[:, 526 + i:527 + i]), inc="act")
        ctx.wait("s", "act")
        a2 = ctx.op("s", lambda a, n=n: a.activation(
            s.sf[:, 542:542 + n], s.sf[:, 526:526 + n], AF.Sqrt,
            bias=s.ccf[:, 256:257], scale=1.0 / D), inc="act")
        ctx.wait("v", "act", a2)
        d2 = ctx.op("v", lambda v, n=n: v.reciprocal(
            s.sf[:, 600:600 + n], s.sf[:, 542:542 + n]), inc="dve")
        ctx.wait("v", "dve", d2)
        for i, (src, out) in enumerate(tiles):
            ctx.op("v", lambda v, i=i, src=src, out=out: v.tensor_scalar(
                out, src, s.sf[:, 616 + i:617 + i], s.sf[:, 600 + i:601 + i],
                op0=ALU.add, op1=ALU.mult), inc="dve")

    def l1_w13_load(self, ctx, W1, W3, e, q):
        s = self
        ctx.dma("g", out=s.wbuf[:, 0:8192]
                .rearrange("p (j nt c) -> p j nt c", nt=8, c=128),
                in_=W1[e][:, q * 1024:(q + 1) * 1024]
                .rearrange("(j p) (nt c) -> p j nt c", p=P, c=128))
        return ctx.dma("g", out=s.wbuf[:, 8192:16384]
                       .rearrange("p (j nt c) -> p j nt c", nt=8, c=128),
                       in_=W3[e][:, q * 1024:(q + 1) * 1024]
                       .rearrange("(j p) (nt c) -> p j nt c", p=P, c=128))

    def emit_L0_attn(self, ctx):
        self.emit_attn_std(ctx, self.inp["L0_attn_qkv_weights"], bias=None,
                           kprime=True, proj=None)

    def emit_L1_attn(self, ctx):
        self.emit_attn_std(ctx, self.inp["L1_attn_W_attn"],
                           bias=self.inp["L1_attn_b_attn"], kprime=False,
                           proj=(self.inp["L1_attn_W_proj"],
                                 self.inp["L1_attn_b_proj"]))

    PADQ, PADK, PADKP, STQK, VON0, VON1, VT2, CTXB = (
        0, 2048, 4096, 6144, 8192, 10240, 12288, 14336)
    ATB = 6144

    def attn_head_core(self, ctx, h, hh, kprime, score, O, relu_eps=None,
                       qbase=None, kbase=None):
        s = self
        PADQ, PADK, PADKP, STQK, VON0, VON1, VT2, CTXB = (
            self.PADQ, self.PADK, self.PADKP, self.STQK, self.VON0, self.VON1,
            self.VT2, self.CTXB)
        ATB = self.ATB
        qbase = PADQ if qbase is None else qbase
        kbase = PADK if kbase is None else kbase
        SCORE_K = PADKP if kprime else kbase
        def kmm(g):
            def f(t_, g=g):
                bank = [s.PBb, s.PCb][g % 2]
                last = None
                for k in range(8):
                    sb = g * 8 + k
                    last = t_.transpose(bank[0:64, k * 128:(k + 1) * 128],
                                        s.sbB[:, kbase + sb * 64:kbase + sb * 64 + 64],
                                        s.cc[:, 2048:2176])
                return last
            return f

        def kev(g):
            def f(e_, g=g):
                bank = [s.PBb, s.PCb][g % 2]
                kp = s.sbB[0:64, PADKP:PADKP + 2048].rearrange(
                    "p (d sb) -> p d sb", sb=32)
                last = None
                for k in range(8):
                    sb = g * 8 + k
                    last = e_.copy(kp[:, :, sb], bank[0:64, k * 128:k * 128 + 64])
                return last
            return f
        if kprime:
            pipelined(ctx, 4, kmm, kev, "s")

        if h == 0:
            self.snap(ctx, "padKp0", s.sbB[:, PADKP:PADKP + 2048])
        von = (VON0, VON1)[hh]
        pe_av, avbank, a_ct_par = {}, {}, {}
        prev_sev_last = ctx.cnt["act"]
        for qci in range(5):
            if qci < 4:
                qc = qci
                nk = 4 * (qc + 1)
                if qc >= 1:
                    ctx.wait("s", "pe", pe_av[qc - 1])
                sev_cnt = {}
                ctx.wait("t", "act", prev_sev_last)
                for g in range(nk):
                    if g >= 4:
                        ctx.wait("t", "act", sev_cnt[g - 4])
                    pe_g = ctx.op("t", lambda t_, g=g, qc=qc: t_.matmul(
                        s.PA[:, (g % 4) * 512:((g % 4) + 1) * 512],
                        lhsT=s.sbB[:, SCORE_K + g * 128:SCORE_K + (g + 1) * 128],
                        rhs=s.sbB[:, qbase + qc * 512:qbase + (qc + 1) * 512],
                        start=True, stop=True), inc="pe")
                    ctx.wait("s", "pe", pe_g)
                    sev_cnt[g] = ctx.op("s", lambda e_, g=g: e_.activation(
                        s.wbuf[:, ATB + g * 512:ATB + (g + 1) * 512],
                        s.PA[:, (g % 4) * 512:((g % 4) + 1) * 512],
                        AF.Exp if score == "exp" else AF.Relu,
                        bias=0.0, scale=0.125), inc="act")
                prev_sev_last = sev_cnt[nk - 1]
                ctx.wait("v", "act", sev_cnt[nk - 1])
                for mm_ in range(4):
                    kt = 4 * qc + mm_
                    ctx.op("v", lambda v, kt=kt, mm_=mm_: v.tensor_tensor(
                        s.wbuf[:, ATB + kt * 512:ATB + (kt + 1) * 512],
                        s.wbuf[:, ATB + kt * 512:ATB + (kt + 1) * 512],
                        s.cc[:, mm_ * 512:(mm_ + 1) * 512], op=ALU.mult),
                        inc="dve")
                mask_dve = ctx.cnt["dve"]
                bank = s.PD if qc % 2 == 0 else s.PE_
                avbank[qc] = bank
                if nk > 4:
                    ctx.wait("t", "act", sev_cnt[nk - 5])

                    def av1(t_, nk=nk, von=von, bank=bank):
                        last = None
                        for kt in range(nk - 4):
                            last = t_.matmul(
                                bank[:, :],
                                lhsT=s.sbB[:, von + kt * 128:von + (kt + 1) * 128],
                                rhs=s.wbuf[:, ATB + kt * 512:ATB + (kt + 1) * 512],
                                start=(kt == 0), stop=False)
                        return last
                    ctx.op("t", av1)
                ctx.wait("t", "dve", mask_dve)

                def av2(t_, nk=nk, von=von, bank=bank):
                    last = None
                    for kt in range(max(0, nk - 4), nk):
                        last = t_.matmul(
                            bank[:, :],
                            lhsT=s.sbB[:, von + kt * 128:von + (kt + 1) * 128],
                            rhs=s.wbuf[:, ATB + kt * 512:ATB + (kt + 1) * 512],
                            start=(kt == 0), stop=(kt == nk - 1))
                    return last
                pe_av[qc] = ctx.op("t", av2, inc="pe")
            if qci >= 1:
                qc = qci - 1
                bank = avbank[qc]
                CT = CTXB + (qc % 2) * 1024
                sb_ = 512 + (qc % 2) * 8
                pb = s.PBb if qc % 2 == 0 else s.PCb
                ctx.wait("s", "pe", pe_av[qc])
                a_ctx = ctx.op("s", lambda a, CT=CT, bank=bank: a.copy(
                    s.sbB[:, CT:CT + 512], bank[:, :]), inc="act")
                if h == 0 and qc == 3:
                    self.snap(ctx, "ctx0", s.sbB[:, CT:CT + 512])
                if (qc % 2) in a_ct_par:
                    ctx.wait("t", "act", a_ct_par[qc % 2])
                ctx.wait("t", "act", a_ctx)
                pe_ct = ctx.op("t", lambda t_, CT=CT, pb=pb: [
                    t_.transpose(pb[:, u * 128:(u + 1) * 128],
                                 s.sbB[:, CT + u * 128:CT + (u + 1) * 128],
                                 s.cc[:, 2048:2176]) for u in range(4)][-1],
                    inc="pe")
                ctx.wait("s", "pe", pe_ct)
                a_ct = ctx.op("s", lambda a, CT=CT, pb=pb: a.copy(
                    s.sbB[:, CT + 512:CT + 1024], pb[0:128, 0:512]), inc="act")
                a_ct_par[qc % 2] = a_ct
                ctx.wait("v", "act", a_ct)
                rs_ap = s.sbB[:, CT + 512:CT + 1024].rearrange(
                    "p (u c) -> p u c", c=128)[:, :, 64]
                if relu_eps is not None:
                    ctx.op("v", lambda v, rs_ap=rs_ap, sb_=sb_: v.tensor_scalar(
                        s.sf[:, sb_ + 4:sb_ + 8], rs_ap, relu_eps, None,
                        op0=ALU.add), inc="dve")
                    ctx.wait("v", "dve")
                    ctx.op("v", lambda v, sb_=sb_: v.reciprocal(
                        s.sf[:, sb_:sb_ + 4], s.sf[:, sb_ + 4:sb_ + 8]),
                        inc="dve")
                else:
                    ctx.op("v", lambda v, rs_ap=rs_ap, sb_=sb_: v.reciprocal(
                        s.sf[:, sb_:sb_ + 4], rs_ap), inc="dve")
                ctx.wait("v", "dve")
                for u in range(4):
                    itile = qc * 4 + u
                    ctx.op("v", lambda v, u=u, itile=itile, h=h, CT=CT, sb_=sb_:
                           v.tensor_scalar(
                               O[:, itile * D + h * 64:itile * D + h * 64 + 64],
                               s.sbB[:, CT + 512 + u * 128:CT + 512 + u * 128 + 64],
                               s.sf[:, sb_ + u:sb_ + u + 1], None,
                               op0=ALU.mult), inc="dve")

    def emit_attn_std(self, ctx, wq, bias, kprime, proj):
        s = self
        ctx.wait("t", "qs", self.c_x_loaded)
        ctx.wait("t", "dve", self.c_dve_consts)
        self.transpose_to_xt(ctx, s.x_cur)
        if bias is not None:
            self.load_chunkcol(ctx, bias, s.sf[:, 0:24], 24, lst="q")
            ctx.wait("v", "qs")

        PADQ, PADK, PADKP, STQK, VON0, VON1, VT2, CTXB = (
            0, 2048, 4096, 6144, 8192, 10240, 12288, 14336)
        ATB = 6144

        ctx.op("v", lambda v: v.memset(s.sbB[:, 0:8192], 0.0), inc="dve")
        for h in range(2):
            von = (VON0, VON1)[h]
            ctx.op("v", lambda v, von=von: v.memset(s.sbB[:, von:von + 2048], 0.0),
                   inc="dve")
            for m in range(NT):
                ctx.op("v", lambda v, von=von, m=m: v.memset(
                    s.sbB[:, von + m * 128 + 64:von + m * 128 + 65], 1.0), inc="dve")
        pads_ready = ctx.cnt["dve"]
        ctx.wait("t", "dve", pads_ready)

        def wsl(hp, m):
            o = (hp % 2) * 3072 + m * 1024
            return s.wbuf[:, o:o + 1024]

        O = s.sbA
        ctx.wait("t", "act")
        self.snap(ctx, "xt", s.xt[:, :])
        pair_pe_done = {}

        for hp in range(8):
            if hp >= 2:
                ctx.wait("g", "pe", pair_pe_done[hp - 2])
            for m in range(3):
                c0 = m * D + hp * 128
                ctx.dma("g", out=wsl(hp, m).rearrange("p (j n) -> p j n", n=128),
                        in_=wq[:, c0:c0 + 128].rearrange("(j p) n -> p j n", p=P))
            ctx.wait("t", "qg")

            ev_ring = {}
            for m in range(3):
                for tc in range(4):
                    gl = m * 4 + tc
                    if gl >= 4:
                        if bias is None:
                            ctx.wait("t", "act", ev_ring[gl - 4])
                        else:
                            ctx.wait("t", "dve", ev_ring[gl - 4])

                    def mmg(t_, hp=hp, m=m, tc=tc):
                        last = None
                        for j in range(DC):
                            last = t_.matmul(
                                s.PA[:, tc * 512:(tc + 1) * 512],
                                lhsT=wsl(hp, m)[:, j * 128:(j + 1) * 128],
                                rhs=s.xt[:, j * T + tc * 512:j * T + (tc + 1) * 512],
                                start=(j == 0), stop=(j == DC - 1))
                        return last
                    pe_t = ctx.op("t", mmg, inc="pe")
                    c0, c1 = tc * 512, (tc + 1) * 512
                    dsts = {0: [(s.sbB[0:64, PADQ + c0:PADQ + c1],
                                 s.PA[0:64, c0:c1], 0, 64),
                                (s.sbB[64:128, STQK + c0:STQK + c1],
                                 s.PA[64:128, c0:c1], 64, 64)],
                            1: [(s.sbB[0:64, PADK + c0:PADK + c1],
                                 s.PA[0:64, c0:c1], 0, 64),
                                (s.sbB[64:128, VT2 + c0:VT2 + c1],
                                 s.PA[64:128, c0:c1], 64, 64)],
                            2: [(s.wbuf[:, 14336 + c0:14336 + c1],
                                 s.PA[:, c0:c1], 0, 128)]}[m]
                    if bias is None:
                        ctx.wait("s", "pe", pe_t)
                        for dst, src, p0, pn in dsts:
                            ev_ring[gl] = ctx.op(
                                "s", lambda a, dst=dst, src=src: a.copy(dst, src),
                                inc="act")
                    else:
                        ctx.wait("v", "pe", pe_t)
                        for dst, src, p0, pn in dsts:
                            bcol = s.sf[p0:p0 + pn, m * 8 + hp:m * 8 + hp + 1]
                            ev_ring[gl] = ctx.op(
                                "v", lambda v, dst=dst, src=src, bcol=bcol:
                                v.tensor_scalar(dst, src, bcol, None, op0=ALU.add),
                                inc="dve")

            qkv_act_done = ctx.cnt["act"]
            if bias is None:
                ctx.wait("q", "act", ev_ring[11])
            else:
                ctx.wait("q", "dve", ev_ring[11])
            ctx.dma("q", out=s.sbB[0:64, STQK:STQK + 2048],
                    in_=s.sbB[64:128, STQK:STQK + 2048])
            h1q = ctx.dma("q", out=s.sbB[0:64, VT2:VT2 + 2048],
                          in_=s.sbB[64:128, VT2:VT2 + 2048])
            ctx.wait("v", "qs", h1q)
            ctx.op("v", lambda v: v.memset(s.sbB[64:128, STQK:STQK + 2048], 0.0))
            h1z = ctx.op("v", lambda v: v.memset(
                s.sbB[64:128, VT2:VT2 + 2048], 0.0), inc="dve")

            if bias is None:
                ctx.wait("t", "act", ev_ring[11])
            else:
                ctx.wait("t", "dve", ev_ring[11])

            def vmm(g):
                def f(t_, g=g):
                    bank = [s.PBb, s.PCb][g % 2]
                    last = None
                    for k in range(8):
                        mtile = g * 8 + k
                        last = t_.transpose(bank[:, k * 128:(k + 1) * 128],
                                            s.wbuf[:, 14336 + mtile * 128:
                                                   14336 + (mtile + 1) * 128],
                                            s.cc[:, 2048:2176])
                    return last
                return f

            def vev(g):
                def f(e_, g=g):
                    bank = [s.PBb, s.PCb][g % 2]
                    last = None
                    for k in range(8):
                        mtile = g * 8 + k
                        last = e_.copy(s.sbB[:, VON0 + mtile * 128:VON0 + mtile * 128 + 64],
                                       bank[:, k * 128:k * 128 + 64])
                        last = e_.copy(s.sbB[:, VON1 + mtile * 128:VON1 + mtile * 128 + 64],
                                       bank[:, k * 128 + 64:k * 128 + 128])
                    return last
                return f
            pipelined(ctx, 2, vmm, vev, "s")

            if hp == 0:
                self.snap(ctx, "padQ0", s.sbB[:, PADQ:PADQ + 2048])
                self.snap(ctx, "padK0", s.sbB[:, PADK:PADK + 2048])
                self.snap(ctx, "von0", s.sbB[:, VON0:VON0 + 2048])
            for hh in range(2):
                h = 2 * hp + hh
                if hh == 1:
                    ctx.wait("t", "dve", h1z)
                self.attn_head_core(ctx, h, hh, kprime=kprime, score="exp", O=O,
                                    qbase=PADQ if hh == 0 else STQK,
                                    kbase=PADK if hh == 0 else VT2)
            pair_pe_done[hp] = ctx.cnt["pe"]

        self.barrier(ctx)
        self.snap(ctx, "O", O[:, :])
        if proj is not None:
            wp, bp = proj
            ctx.wait("t", "dve")
            pairs = []
            for i in range(NT):
                for j in range(DC):
                    pairs.append((O[:, i * D + j * 128:i * D + (j + 1) * 128],
                                  s.xt[:, j * T + i * 128:j * T + i * 128 + 128]))
            self.bf_transpose_batch(ctx, pairs, depth=2)
            ctx.wait("g", "pe")
            ctx.dma("g", out=s.wbuf[:, 0:8192].rearrange("p (j n) -> p j n", n=1024),
                    in_=wp.rearrange("(j p) n -> p j n", p=P))
            self.load_bcast(ctx, bp, s.cb[:, 0:1024], D, lst="g")
            ctx.wait("t", "qg")
            ctx.wait("t", "act")
            ctx.wait("v", "qg")

            def pmm(g):
                def f(t_, g=g):
                    tt, nh = g // 2, g % 2
                    last = None
                    for j in range(8):
                        last = t_.matmul(
                            s.PA[:, (g % 4) * 512:((g % 4) + 1) * 512],
                            lhsT=s.xt[:, j * T + tt * 128:j * T + (tt + 1) * 128],
                            rhs=s.wbuf[:, j * 1024 + nh * 512:j * 1024 + (nh + 1) * 512],
                            start=(j == 0), stop=(j == 7))
                    return last
                return f

            def pev(g):
                def f(v_, g=g):
                    tt, nh = g // 2, g % 2
                    return v_.scalar_tensor_tensor(
                        O[:, tt * D + nh * 512:tt * D + (nh + 1) * 512],
                        s.PA[:, (g % 4) * 512:((g % 4) + 1) * 512], 1.0,
                        s.cb[:, nh * 512:(nh + 1) * 512], op0=ALU.mult, op1=ALU.add)
                return f
            pipelined(ctx, 32, pmm, pev, "v", depth=4)
            self.snap(ctx, "Oproj", O[:, :])

        self.barrier(ctx)
        ctx.wait("v", "act")
        for i in range(NT):
            ctx.op("v", lambda v, i=i: v.tensor_tensor(
                s.x_cur[:, i * D:(i + 1) * D], s.x_cur[:, i * D:(i + 1) * D],
                O[:, i * D:(i + 1) * D], op=ALU.add), inc="dve")
        self.ln_batch(ctx, [(s.x_cur[:, i * D:(i + 1) * D],
                             s.x_cur[:, i * D:(i + 1) * D]) for i in range(NT)],
                      sq_scratch=s.sbB[:, 0:D])

    def emit_L0_ffn(self, ctx):
        s = self
        inp = self.inp
        import math

        self.barrier(ctx)
        NTAY = 12
        gens = inp["L0_ffn_gens"]

        def mat_dma_in(lst, dram_mat, dst_off, buf=None):
            buf = buf if buf is not None else s.wbuf
            ctx.wait(lst, "pe")
            ctx.wait(lst, "qs")
            ctx.wait(lst, "qg")
            ctx.dma(lst, out=buf[:, dst_off:dst_off + 8192]
                    .rearrange("p (j n) -> p j n", n=1024),
                    in_=dram_mat.rearrange("(j p) n -> p j n", p=P))

        def matmul_1024(ctx, lhsT_buf, lhsT_off, rhs_buf, rhs_off, ev_emit):
            def mm(g):
                def f(t_, g=g):
                    last = None
                    po = (g % 2) * 1024
                    for nh in range(2):
                        for j in range(8):
                            last = t_.matmul(
                                s.PA[:, po + nh * 512:po + (nh + 1) * 512],
                                lhsT=lhsT_buf[:, lhsT_off + j * 1024 + g * 128:
                                              lhsT_off + j * 1024 + (g + 1) * 128],
                                rhs=rhs_buf[:, rhs_off + j * 1024 + nh * 512:
                                            rhs_off + j * 1024 + (nh + 1) * 512],
                                start=(j == 0), stop=(j == 7))
                    return last
                return f

            def ev(g):
                def f(e_, g=g):
                    return ev_emit(e_, g, (g % 2) * 1024)
                return f
            pipelined(ctx, 8, mm, ev, "s", depth=2)

        for gi in range(4):
            mat_dma_in("g", gens[gi], 0, buf=s.xt)
            ctx.wait("t", "qg")
            ctx.wait("v", "qg")
            ctx.op("v", lambda v: v.memset(s.sbA[:, 0:8192], 0.0), inc="dve")
            for c in range(8):
                ctx.op("v", lambda v, c=c: v.tensor_copy(
                    s.sbA[:, c * 1024 + c * 128:c * 1024 + c * 128 + 128],
                    s.cc[:, 2048:2176]))
            for c in range(8):
                ctx.op("v", lambda v, c=c: v.scalar_tensor_tensor(
                    s.sbA[:, c * 1024:(c + 1) * 1024],
                    s.xt[:, c * 1024:(c + 1) * 1024], 1.0,
                    s.sbA[:, c * 1024:(c + 1) * 1024],
                    op0=ALU.mult, op1=ALU.add), inc="dve")
            ctx.wait("t", "dve")
            prev_buf, prev_off = s.xt, 0
            cur_half = 0
            for k in range(2, NTAY + 1):
                dst_off = cur_half * 8192
                sk = (1.0 if (k % 2 == 1) else -1.0) / math.factorial(k)

                def ev_emit(e_, m, po, dst_off=dst_off):
                    return e_.copy(s.wbuf[:, dst_off + m * 1024:dst_off + (m + 1) * 1024],
                                   s.PA[:, po:po + 1024])
                matmul_1024(ctx, s.xt, 0, prev_buf, prev_off, ev_emit)
                ctx.wait("v", "act")
                for c in range(8):
                    ctx.op("v", lambda v, c=c, sk=sk, dst_off=dst_off:
                           v.scalar_tensor_tensor(
                               s.sbA[:, c * 1024:(c + 1) * 1024],
                               s.wbuf[:, dst_off + c * 1024:dst_off + (c + 1) * 1024],
                               sk, s.sbA[:, c * 1024:(c + 1) * 1024],
                               op0=ALU.mult, op1=ALU.add), inc="dve")
                prev_buf, prev_off = s.wbuf, dst_off
                cur_half ^= 1
                ctx.wait("t", "dve")
            ctx.wait("q", "dve")
            ctx.dma("q", out=self.dscr_w[0:1024, gi * 1024:(gi + 1) * 1024]
                    .rearrange("(j p) n -> p j n", p=P),
                    in_=s.sbA[:, 0:8192].rearrange("p (j n) -> p j n", n=1024))
            ctx.wait("t", "qs")
            ctx.wait("v", "qs")

        def prod(left_from_dram, left_dram_col, right_col, out_is_R):
            ctx.wait("t", "qg")
            pairs = []
            for j in range(8):
                for m in range(8):
                    pairs.append((s.wbuf[:, j * 1024 + m * 128:j * 1024 + (m + 1) * 128],
                                  s.wbuf[:, 8192 + m * 1024 + j * 128:
                                         8192 + m * 1024 + (j + 1) * 128]))
            self.bf_transpose_batch(ctx, pairs)
            mat_dma_in("g", self.dscr_w[0:1024, right_col:right_col + 1024], 0, buf=s.xt)
            ctx.wait("t", "qg")
            ctx.wait("t", "act")

            def ev_emit(e_, m, po):
                return e_.copy(s.sbA[:, m * 1024:(m + 1) * 1024], s.PA[:, po:po + 1024])
            matmul_1024(ctx, s.wbuf, 8192, s.xt, 0, ev_emit)
            ctx.wait("q", "act")
            row0 = 1024 if out_is_R else 2048
            ctx.dma("q", out=self.dscr_w[row0:row0 + 1024, 0:1024]
                    .rearrange("(j p) n -> p j n", p=P),
                    in_=s.sbA[:, 0:8192].rearrange("p (j n) -> p j n", n=1024))
            ctx.wait("t", "qs")
            ctx.wait("g", "qs")

        mat_dma_in("g", self.dscr_w[0:1024, 0:1024], 0)
        prod(False, 0, 1024, False)
        mat_dma_in("g", self.dscr_w[2048:3072, 0:1024], 0)
        prod(False, 0, 2048, False)
        mat_dma_in("g", self.dscr_w[2048:3072, 0:1024], 0)
        prod(False, 0, 3072, True)
        self.snap(ctx, "R", s.sbA[:, 0:8192])

        self.barrier(ctx)
        self.emit_gained_ln_to_xt(ctx, inp["L0_ffn_ln_g"], inp["L0_ffn_ln_b"])
        ctx.wait("t", "act")
        self.snap(ctx, "xnT", s.xt[:, :])

        self.barrier(ctx)
        ctx.dma("g", out=s.wbuf[:, 0:48].rearrange("p (j c) -> p j c", c=6),
                in_=inp["L0_ffn_gate_w"].rearrange("(j p) c -> p j c", p=P))
        self.load_bcast(ctx, inp["L0_ffn_gate_b"], s.sf[:, 256:262], 6)
        ctx.wait("t", "qg")
        ctx.wait("v", "qs")

        def gmm2(g):
            def f(t_, g=g):
                bank = [s.PD, s.PE_][g % 2]
                last = None
                for j in range(8):
                    last = t_.matmul(bank[:, 0:6],
                                     lhsT=s.xt[:, j * T + g * 128:j * T + (g + 1) * 128],
                                     rhs=s.wbuf[:, j * 6:(j + 1) * 6],
                                     start=(j == 0), stop=(j == 7))
                return last
            return f

        ctx.op("v", lambda v: v.memset(s.sf[:, 0:128], -1e30), inc="dve")
        ctx.wait("v", "dve")

        def gev(g):
            def f(e_, g=g):
                bank = [s.PD, s.PE_][g % 2]
                return e_.tensor_tensor(s.sf[:, g * 8:g * 8 + 6], bank[:, 0:6],
                                        s.sf[:, 256:262], op=ALU.add)
            return f
        pipelined(ctx, NT, gmm2, gev, "v", depth=2)
        self.snap(ctx, "lgL0", s.sf[:, 0:128])
        self.emit_topk_softmax(ctx, nexp=6, logit_off=0, wfull_off=128, kk=2)
        self.snap(ctx, "wfullL0", s.sf[:, 128:256])

        self.barrier(ctx)
        ctx.wait("v", "act")
        ctx.op("v", lambda v: v.memset(s.sbA[:], 0.0), inc="dve")
        ctx.wait("t", "dve")
        units = [(e, hh2) for e in range(6) for hh2 in range(2)]

        def px_w1(u):
            e, hh2 = units[u]
            return ctx.dma("g", out=s.wbuf[:, 0:8192]
                           .rearrange("p (j nt c) -> p j nt c", nt=8, c=128),
                           in_=inp["L0_ffn_ew1"][e, :, hh2 * 1024:(hh2 + 1) * 1024]
                           .rearrange("(j p) (nt c) -> p j nt c", p=P, c=128))

        def px_w2(u):
            e, hh2 = units[u]
            return ctx.dma("g", out=s.wbuf[:, 8192:16384]
                           .rearrange("p (nt c) -> p nt c", c=1024),
                           in_=inp["L0_ffn_ew2"][e, hh2 * 1024:(hh2 + 1) * 1024, :]
                           .rearrange("(nt p) c -> p nt c", p=P))

        def px_b(u):
            e, hh2 = units[u]
            self.load_chunkcol(ctx, inp["L0_ffn_eb1"][e, hh2 * 1024:(hh2 + 1) * 1024],
                               s.sf[:, (u % 2) * 8:(u % 2) * 8 + 8], 8, lst="g")
            if hh2 == 0:
                ctx.wait("g", "dve")
                self.load_bcast(ctx, inp["L0_ffn_eb2"][e], s.cb[:, 0:1024], D,
                                lst="g")
            return ctx.cnt["qg"]

        ctx.wait("g", "pe")
        ctx.wait("g", "dve")
        c1 = px_w1(0)
        cb_ = px_b(0)
        c2 = px_w2(0)
        for u, (e, hh2) in enumerate(units):
            c1u, c2u, cbu = c1, c2, cb_
            ctx.wait("t", "qg", c1u)
            ctx.wait("s", "qg", cbu)

            def amm(g):
                def f(t_, g=g):
                    tc, gg = g // 8, g % 8
                    bank = [s.PD, s.PE_][g % 2]
                    last = None
                    for j in range(8):
                        last = t_.matmul(
                            bank[:, :],
                            lhsT=s.wbuf[:, j * 1024 + gg * 128:j * 1024 + (gg + 1) * 128],
                            rhs=s.xt[:, j * T + tc * 512:j * T + (tc + 1) * 512],
                            start=(j == 0), stop=(j == 7))
                    return last
                return f

            def aev(g):
                def f(e_, g=g, u=u):
                    tc, gg = g // 8, g % 8
                    bank = [s.PD, s.PE_][g % 2]
                    return e_.activation(s.sbB[:, tc * 4096 + gg * 512:
                                               tc * 4096 + (gg + 1) * 512],
                                         bank[:, :], AF.Gelu,
                                         bias=s.sf[:, (u % 2) * 8 + gg:
                                                   (u % 2) * 8 + gg + 1], scale=1.0)
                return f
            pipelined(ctx, 32, amm, aev, "s", depth=2)
            if u + 1 < 12:
                ctx.wait("g", "pe")
                c1 = px_w1(u + 1)
                cb_ = px_b(u + 1)
            ctx.wait("t", "act")
            ctx.wait("t", "qg", c2u)
            ctx.wait("v", "qg", cbu)

            def bmm(g):
                def f(t_, g=g):
                    tc, gg = g // 8, g % 8
                    last = None
                    for nt in range(8):
                        last = t_.matmul(
                            s.PA[:, (g % 4) * 512:((g % 4) + 1) * 512],
                            lhsT=s.sbB[:, tc * 4096 + nt * 512 + (gg // 2) * 128:
                                       tc * 4096 + nt * 512 + (gg // 2) * 128 + 128],
                            rhs=s.wbuf[:, 8192 + nt * 1024 + (gg % 2) * 512:
                                       8192 + nt * 1024 + (gg % 2) * 512 + 512],
                            start=(nt == 0), stop=(nt == 7))
                    return last
                return f

            def bev(g):
                def f(v_, g=g, e=e, hh2=hh2):
                    tc, gg = g // 8, g % 8
                    tt = tc * 4 + gg // 2
                    nh = gg % 2
                    wcol = s.sf[:, 128 + tt * 8 + e:128 + tt * 8 + e + 1]
                    dst = s.sbA[:, tt * 1024 + nh * 512:tt * 1024 + (nh + 1) * 512]
                    last = v_.scalar_tensor_tensor(
                        dst, s.PA[:, (g % 4) * 512:((g % 4) + 1) * 512],
                        wcol, dst, op0=ALU.mult, op1=ALU.add)
                    if hh2 == 0:
                        last = v_.scalar_tensor_tensor(
                            dst, s.cb[:, nh * 512:(nh + 1) * 512], wcol, dst,
                            op0=ALU.mult, op1=ALU.add)
                    return last
                return f
            pipelined(ctx, 32, bmm, bev, "v", depth=4)
            if u + 1 < 12:
                ctx.wait("g", "pe")
                c2 = px_w2(u + 1)
        self.snap(ctx, "nonabel", s.sbA[:, :])

        self.barrier(ctx)
        pairs = []
        for i in range(NT):
            for j in range(DC):
                pairs.append((s.sbA[:, i * 1024 + j * 128:i * 1024 + (j + 1) * 128],
                              s.sbB[:, j * T + i * 128:j * T + i * 128 + 128]))
        self.bf_transpose_batch(ctx, pairs, depth=2)
        ctx.wait("g", "pe")
        mat_dma_in("g", self.dscr_w[1024:2048, 0:1024], 0)
        ctx.wait("t", "qg")
        ctx.wait("t", "act")

        def rmm(g):
            def f(t_, g=g):
                tt, nh = g // 2, g % 2
                last = None
                for j in range(8):
                    last = t_.matmul(
                        s.PA[:, (g % 4) * 512:((g % 4) + 1) * 512],
                        lhsT=s.sbB[:, j * T + tt * 128:j * T + (tt + 1) * 128],
                        rhs=s.wbuf[:, j * 1024 + nh * 512:j * 1024 + (nh + 1) * 512],
                        start=(j == 0), stop=(j == 7))
                return last
            return f
        rot_seq = []

        def rev(g):
            def f(e_, g=g):
                slot = 8192 + (g % 2) * 1024
                tt, nh = g // 2, g % 2
                return e_.copy(s.wbuf[:, slot + nh * 0:slot + 512]
                               if False else s.wbuf[:, 8192 + (g % 4) * 512:
                                                    8192 + (g % 4) * 512 + 512],
                               s.PA[:, (g % 4) * 512:((g % 4) + 1) * 512])
            return f
        def rot_after(g, act_cnt):
            tt, nh = g // 2, g % 2
            ctx.wait("q", "act", act_cnt)
            ctx.dma("q", out=self.dscr_b[tt * 128:(tt + 1) * 128,
                                         nh * 512:(nh + 1) * 512],
                    in_=s.wbuf[:, 8192 + (g % 4) * 512:8192 + (g % 4) * 512 + 512])
        ev_done = []
        ctx.wait("t", "act")
        for g in range(32):
            if g >= 4:
                ctx.wait("t", "act", ev_done[g - 4])
                ctx.wait("s", "qs", self._rot_q[g - 4])
            pe_t = ctx.op("t", rmm(g), inc="pe")
            ctx.wait("s", "pe", pe_t)
            a_t = ctx.op("s", rev(g), inc="act")
            ev_done.append(a_t)
            rot_after(g, a_t)
            if not hasattr(self, "_rot_q"):
                self._rot_q = []
            self._rot_q.append(ctx.cnt["qs"])
        self._rot_q = []

        self.barrier(ctx)
        ctx.wait("g", "qs")
        ctx.dma("g", out=self.dscr_a[0:4096, :], in_=inp["L0_ffn_aw2"])
        self.load_chunkcol(ctx, inp["L0_ffn_ab1"], s.sf[:, 0:32], 32, lst="g")
        self.load_bcast(ctx, inp["L0_ffn_ab2"], s.cb[:, 0:1024], D, lst="g")
        aw2_ready = ctx.cnt["qg"]
        aw1 = inp["L0_ffn_aw1"]
        a1_loaded = {}
        for tc in range(4):
            for ng in range(8):
                slot = (ng % 2) * 4096
                if tc == 0 or True:
                    if ng >= 2 or tc > 0:
                        ctx.wait("g", "pe")
                    ctx.dma("g", out=s.wbuf[:, slot:slot + 4096]
                            .rearrange("p (j nt c) -> p j nt c", nt=4, c=128),
                            in_=aw1[:, ng * 512:(ng + 1) * 512]
                            .rearrange("(j p) (nt c) -> p j nt c", p=P, c=128))
                ctx.wait("t", "qg")

                def amm(g):
                    def f(t_, g=g, tc=tc, slot=slot):
                        bank = [s.PD, s.PE_][g % 2]
                        last = None
                        for j in range(8):
                            last = t_.matmul(
                                bank[:, :],
                                lhsT=s.wbuf[:, slot + j * 512 + g * 128:
                                            slot + j * 512 + (g + 1) * 128],
                                rhs=s.xt[:, j * T + tc * 512:j * T + (tc + 1) * 512],
                                start=(j == 0), stop=(j == 7))
                        return last
                    return f

                def aev(g):
                    def f(e_, g=g, ng=ng):
                        bank = [s.PD, s.PE_][g % 2]
                        nt = ng * 4 + g
                        return e_.activation(s.sbB[:, nt * 512:(nt + 1) * 512],
                                             bank[:, :], AF.Gelu,
                                             bias=s.sf[:, nt:nt + 1], scale=1.0)
                    return f
                pipelined(ctx, 4, amm, aev, "s", depth=2)
            ctx.wait("t", "act")
            for tth in range(2):
                for ntc in range(4):
                    ctx.wait("g", "pe")
                    if tth == 0 and tc == 0 and ntc == 0:
                        ctx.wait("g", "qg", aw2_ready)
                    ctx.dma("g", out=s.wbuf[:, 8192:16384]
                            .rearrange("p (nt c) -> p nt c", c=1024),
                            in_=self.dscr_a[ntc * 1024:(ntc + 1) * 1024, :]
                            .rearrange("(nt p) c -> p nt c", p=P))
                    ctx.wait("t", "qg")

                    def bmm(t_, tc=tc, tth=tth, ntc=ntc):
                        last = None
                        for grp in range(4):
                            ttr = tth * 2 + grp // 2
                            nh = grp % 2
                            for ni in range(8):
                                nt = ntc * 8 + ni
                                last = t_.matmul(
                                    s.PA[:, grp * 512:(grp + 1) * 512],
                                    lhsT=s.sbB[:, nt * 512 + ttr * 128:
                                               nt * 512 + ttr * 128 + 128],
                                    rhs=s.wbuf[:, 8192 + ni * 1024 + nh * 512:
                                               8192 + ni * 1024 + nh * 512 + 512],
                                    start=(nt == 0), stop=(nt == 31))
                        return last
                    pe_b = ctx.op("t", bmm, inc="pe")
                ctx.wait("v", "pe", pe_b)
                for grp in range(4):
                    ttr = tth * 2 + grp // 2
                    tt = tc * 4 + ttr
                    nh = grp % 2
                    ctx.op("v", lambda v, grp=grp, tt=tt, nh=nh:
                           v.scalar_tensor_tensor(
                               s.sbA[:, tt * 1024 + nh * 512:tt * 1024 + (nh + 1) * 512],
                               s.PA[:, grp * 512:(grp + 1) * 512], 1.0,
                               s.cb[:, nh * 512:(nh + 1) * 512],
                               op0=ALU.mult, op1=ALU.add), inc="dve")
                ctx.wait("t", "dve")
        self.snap(ctx, "abel", s.sbA[:, :])

        self.barrier(ctx)
        gcw = inp["L0_ffn_gc_w"]
        ctx.wait("g", "dve")
        ctx.dma("g", out=s.sbB[:, 0:1024],
                in_=gcw[0:1024, 0:1].rearrange("n 1 -> 1 n").broadcast_to([P, 1024]))
        ctx.dma("g", out=s.sbB[:, 1024:2048],
                in_=gcw[1024:2048, 0:1].rearrange("n 1 -> 1 n").broadcast_to([P, 1024]))
        self.load_bcast(ctx, inp["L0_ffn_gc_b"], s.sf[:, 893:894], 1, lst="g")
        ctx.wait("v", "qg")
        ctx.wait("q", "qg")
        ctx.wait("q", "qs")
        ctx.wait("q", "pe")

        def rot_load(i):
            slot = 2048 + (i % 4) * 1024
            return ctx.dma("q", out=s.sbB[:, slot:slot + 1024],
                           in_=self.dscr_b[i * 128:(i + 1) * 128, :]), slot

        ql, dva = {}, {}
        for i in range(3):
            ql[i] = rot_load(i)
        for i in range(NT):
            if i + 3 < NT:
                if i >= 1:
                    ctx.wait("q", "dve", dva[i - 1])
                ql[i + 3] = rot_load(i + 3)
            ctx.wait("v", "qs", ql[i][0])
            ctx.op("v", lambda v, i=i: v.scalar_tensor_tensor(
                s.sbB[:, 6144:7168], s.sbA[:, i * 1024:(i + 1) * 1024], 1.0,
                s.sbB[:, 0:1024], op0=ALU.mult, op1=ALU.mult,
                accum_out=s.sf[:, 560 + i:561 + i]), inc="dve")
            ctx.op("v", lambda v, i=i, slot=ql[i][1]: v.scalar_tensor_tensor(
                s.sbB[:, 6144:7168], s.sbB[:, slot:slot + 1024], 1.0,
                s.sbB[:, 1024:2048], op0=ALU.mult, op1=ALU.mult,
                accum_out=s.sf[:, 576 + i:577 + i]), inc="dve")
            dva[i] = ctx.cnt["dve"]
        ctx.wait("v", "dve")
        d1 = ctx.op("v", lambda v: v.tensor_tensor(
            s.sf[:, 560:576], s.sf[:, 560:576], s.sf[:, 576:592], op=ALU.add),
            inc="dve")
        ctx.wait("s", "dve", d1)
        a1 = ctx.op("s", lambda a: a.activation(
            s.sf[:, 576:592], s.sf[:, 560:576], AF.Sigmoid,
            bias=s.sf[:, 893:894], scale=1.0), inc="act")
        ctx.wait("v", "act", a1)
        d2 = ctx.op("v", lambda v: v.tensor_scalar(
            s.sf[:, 560:576], s.sf[:, 576:592], -1.0, 1.0,
            op0=ALU.mult, op1=ALU.add), inc="dve")
        ctx.wait("v", "dve", d2)
        ctx.wait("q", "dve")
        ql2, dv2 = {}, {}
        for i in range(3):
            ql2[i] = rot_load(i)
        for i in range(NT):
            if i + 3 < NT:
                if i >= 1:
                    ctx.wait("q", "dve", dv2[i - 1])
                ql2[i + 3] = rot_load(i + 3)
            ctx.wait("v", "qs", ql2[i][0])
            ctx.op("v", lambda v, i=i, slot=ql2[i][1]: v.tensor_scalar(
                s.cb[:, 0:1024], s.sbB[:, slot:slot + 1024],
                s.sf[:, 576 + i:577 + i], None, op0=ALU.mult))
            ctx.op("v", lambda v, i=i: v.scalar_tensor_tensor(
                s.cb[:, 0:1024], s.sbA[:, i * 1024:(i + 1) * 1024],
                s.sf[:, 560 + i:561 + i], s.cb[:, 0:1024],
                op0=ALU.mult, op1=ALU.add))
            ctx.op("v", lambda v, i=i: v.scalar_tensor_tensor(
                s.x_cur[:, i * D:(i + 1) * D], s.x_cur[:, i * D:(i + 1) * D],
                2.0, s.cb[:, 0:1024], op0=ALU.mult, op1=ALU.add), inc="dve")
            dv2[i] = ctx.cnt["dve"]
        self.ln_batch(ctx, [(s.x_cur[:, i * D:(i + 1) * D],
                             s.x_cur[:, i * D:(i + 1) * D]) for i in range(NT)],
                      sq_scratch=s.sbB[:, 6144:7168])

    def emit_topk_softmax(self, ctx, nexp, logit_off, wfull_off, kk,
                          stride=None, slot_store=None):
        assert kk == 2
        s = self
        ne = stride if stride is not None else max(8, nexp)
        B0 = 600
        for i in range(NT):
            lg = s.sf[:, logit_off + i * ne:logit_off + i * ne + ne]
            oh1 = s.sf[:, B0:B0 + ne]
            cum = s.sf[:, B0 + 32:B0 + 32 + ne]
            oh1f = s.sf[:, B0 + 64:B0 + 64 + ne]
            l2 = s.sf[:, B0 + 96:B0 + 96 + ne]
            oh2 = s.sf[:, B0 + 128:B0 + 128 + ne]
            oh2f = s.sf[:, B0 + 160:B0 + 160 + ne]
            m1 = s.sf[:, B0 + 192:B0 + 193]
            m2 = s.sf[:, B0 + 193:B0 + 194]
            ee = s.sf[:, B0 + 194:B0 + 195]
            den = s.sf[:, B0 + 195:B0 + 196]
            w1 = s.sf[:, B0 + 196:B0 + 197]
            w2 = s.sf[:, B0 + 197:B0 + 198]
            ee2 = s.sf[:, B0 + 198:B0 + 199]
            wf = (s.sf[:, wfull_off + i * ne:wfull_off + i * ne + ne]
                  if wfull_off is not None else None)

            def ve(emit):
                ctx.op("v", emit, inc="dve")
                ctx.wait("v", "dve")
            ve(lambda v, a=m1, b=lg: v.reduce_max(a, b, axis=AX.X))
            ve(lambda v, a=oh1, b=lg, c=m1: v.tensor_scalar(
                a, b, c, None, op0=ALU.is_equal))
            ve(lambda v, a=cum, b=oh1: v.tensor_tensor_scan(
                a, b, b, 0.0, op0=ALU.add, op1=ALU.bypass))
            ve(lambda v, a=cum: v.tensor_scalar(a, a, 1.0, None, op0=ALU.is_equal))
            ve(lambda v, a=oh1f, b=oh1, c=cum: v.tensor_tensor(a, b, c, op=ALU.mult))
            ve(lambda v, a=l2, b=oh1f, c=lg: v.scalar_tensor_tensor(
                a, b, -1e30, c, op0=ALU.mult, op1=ALU.add))
            ve(lambda v, a=m2, b=l2: v.reduce_max(a, b, axis=AX.X))
            ve(lambda v, a=oh2, b=l2, c=m2: v.tensor_scalar(
                a, b, c, None, op0=ALU.is_equal))
            ve(lambda v, a=cum, b=oh2: v.tensor_tensor_scan(
                a, b, b, 0.0, op0=ALU.add, op1=ALU.bypass))
            ve(lambda v, a=cum: v.tensor_scalar(a, a, 1.0, None, op0=ALU.is_equal))
            ve(lambda v, a=oh2f, b=oh2, c=cum: v.tensor_tensor(a, b, c, op=ALU.mult))
            d = ctx.op("v", lambda v, a=ee, b=m2, c=m1: v.tensor_tensor(
                a, b, c, op=ALU.subtract), inc="dve")
            ctx.wait("s", "dve", d)
            a_ = ctx.op("s", lambda a2, a=ee2, b=ee: a2.activation(a, b, AF.Exp),
                        inc="act")
            ctx.wait("v", "act", a_)
            ve(lambda v, a=den, b=ee2: v.tensor_scalar(a, b, 1.0, None, op0=ALU.add))
            ve(lambda v, a=w1, b=den: v.reciprocal(a, b))
            ve(lambda v, a=w2, b=ee2, c=w1: v.tensor_tensor(a, b, c, op=ALU.mult))
            if slot_store is None:
                ve(lambda v, a=wf, b=oh1f, c=w1: v.tensor_scalar(a, b, c, None,
                                                                 op0=ALU.mult))
                ve(lambda v, a=wf, b=oh2f, c=w2: v.scalar_tensor_tensor(
                    a, b, c, a, op0=ALU.mult, op1=ALU.add))
            else:
                oh_buf, ws0, ws1, es0, es1 = slot_store
                tmp8 = s.sf[:, B0 + 224:B0 + 224 + ne]
                for si, (ohf, wv, wsb, esb) in enumerate(
                        ((oh1f, w1, ws0, es0), (oh2f, w2, ws1, es1))):
                    ve(lambda v, d=oh_buf[:, (si * NT + i) * 8:(si * NT + i) * 8 + 8],
                       b=ohf: v.tensor_copy(d, b[:, 0:8]))
                    ve(lambda v, d=s.sf[:, wsb + i:wsb + i + 1], b=wv:
                       v.tensor_copy(d, b))
                    ve(lambda v, a=tmp8, b=ohf: v.tensor_tensor(
                        a, b[:, 0:8], s.sf[:, 264:272], op=ALU.mult))
                    ve(lambda v, d=s.sf[:, esb + i:esb + i + 1], a=tmp8:
                       v.reduce_sum(d, a, axis=AX.X))
            if i == 0:
                self.snap(ctx, "tk0", s.sf[:, B0:B0 + 200])

    def emit_logits_hl(self, ctx, w_ap, nexp, bias_off, out_stride=8,
                       pad_neg=True):
        s = self
        ne8 = out_stride
        nw = 8 * nexp
        ctx.wait("q", "dve")
        ctx.dma("q", out=s.sf[:, 0:nw].rearrange("p (j c) -> p j c", c=nexp),
                in_=w_ap.rearrange("(j p) c -> p j c", p=P))
        ctx.wait("g", "pe")
        ctx.dma("g", out=s.wbuf[:, 0:nw].rearrange("p (j c) -> p j c", c=nexp),
                in_=w_ap.rearrange("(j p) c -> p j c", p=P))
        ctx.wait("v", "qs")
        ctx.wait("v", "qg")
        ctx.op("v", lambda v: v.tensor_tensor(
            s.wbuf[:, nw:2 * nw], s.sf[:, 0:nw], s.wbuf[:, 0:nw],
            op=ALU.subtract), inc="dve")
        ctx.wait("v", "dve")
        if pad_neg:
            ctx.op("v", lambda v: v.memset(s.sf[:, 0:NT * ne8], -1e30), inc="dve")
        ctx.wait("v", "dve")
        ctx.wait("t", "dve")
        XH, XL, XLT = 9216, 10240, 11264
        for i in range(NT):
            ctx.wait("v", "act")
            ctx.op("v", lambda v, i=i: v.tensor_copy(
                s.sbB[:, XH:XH + 1024], s.x_cur[:, i * D:(i + 1) * D]))
            d1 = ctx.op("v", lambda v, i=i: v.tensor_tensor(
                s.sbB[:, XL:XL + 1024], s.x_cur[:, i * D:(i + 1) * D],
                s.sbB[:, XH:XH + 1024], op=ALU.subtract), inc="dve")
            ctx.wait("t", "dve", d1)
            pairs = [(s.sbB[:, XL + j * 128:XL + (j + 1) * 128],
                      s.sbB[:, XLT + j * 128:XLT + (j + 1) * 128])
                     for j in range(DC)]
            self.bf_transpose_batch(ctx, pairs, depth=2)
            ctx.wait("t", "act")

            def lmm(t_, i=i):
                last = None
                k = 0
                for (lh, rh) in ((0, 0), (0, nw), (1, 0)):
                    for j in range(DC):
                        lhsT = (s.xt[:, j * T + i * 128:j * T + (i + 1) * 128]
                                if lh == 0 else
                                s.sbB[:, XLT + j * 128:XLT + (j + 1) * 128])
                        last = t_.matmul(
                            s.PD[:, 0:nexp], lhsT=lhsT,
                            rhs=s.wbuf[:, rh + j * nexp:rh + (j + 1) * nexp],
                            start=(k == 0), stop=(k == 3 * DC - 1))
                        k += 1
                return last
            pe_t = ctx.op("t", lmm, inc="pe")
            ctx.wait("v", "pe", pe_t)
            if bias_off is not None:
                d2 = ctx.op("v", lambda v, i=i: v.tensor_tensor(
                    s.sf[:, i * ne8:i * ne8 + nexp], s.PD[:, 0:nexp],
                    s.sf[:, bias_off:bias_off + nexp], op=ALU.add), inc="dve")
            else:
                d2 = ctx.op("v", lambda v, i=i: v.tensor_copy(
                    s.sf[:, i * ne8:i * ne8 + nexp], s.PD[:, 0:nexp]), inc="dve")
            ctx.wait("t", "dve", d2)

    def emit_gained_ln_to_xt(self, ctx, g_ap, b_ap):
        s = self
        ctx.wait("v", "qs", self.c_x_loaded)
        ctx.wait("s", "qs", self.c_x_loaded)
        ctx.wait("g", "dve")
        ctx.wait("g", "act")
        ctx.wait("g", "pe")
        self.load_bcast(ctx, g_ap, s.sbB[:, 0:1024], D, lst="g")
        self.load_bcast(ctx, b_ap, s.sbB[:, 1024:2048], D, lst="g")
        for i in range(NT):
            ctx.op("v", lambda v, i=i: v.reduce_sum(
                s.sf[:, 600 + i:601 + i], s.x_cur[:, i * D:(i + 1) * D],
                axis=AX.X, negate=True), inc="dve")
        ctx.wait("v", "dve")
        d1 = ctx.op("v", lambda v: v.tensor_scalar(
            s.sf[:, 616:632], s.sf[:, 600:616], 1.0 / D, None, op0=ALU.mult),
            inc="dve")
        ctx.wait("s", "dve", d1)
        for i in range(NT):
            ctx.op("s", lambda a, i=i: a.activation(
                s.sbB[:, 4096:5120], s.x_cur[:, i * D:(i + 1) * D], AF.Square,
                bias=s.sf[:, 616 + i:617 + i], scale=1.0,
                accum_out=s.sf[:, 526 + i:527 + i]), inc="act")
        ctx.wait("s", "act")
        a2 = ctx.op("s", lambda a: a.activation(
            s.sf[:, 542:558], s.sf[:, 526:542], AF.Sqrt,
            bias=s.ccf[:, 256:257], scale=1.0 / D), inc="act")
        ctx.wait("v", "act", a2)
        d2 = ctx.op("v", lambda v: v.reciprocal(
            s.sf[:, 600:616], s.sf[:, 542:558]), inc="dve")
        ctx.wait("v", "dve", d2)
        ctx.wait("v", "qg")
        pe_ring = {}
        for i in range(NT):
            xs = i % 2
            xn = lambda off=2048 + 1024 * xs: s.sbB[:, off:off + 1024]
            if i >= 2:
                ctx.wait("v", "pe", pe_ring[i - 2])
            ctx.op("v", lambda v, i=i, xn=xn: v.tensor_scalar(
                xn(), s.x_cur[:, i * D:(i + 1) * D], s.sf[:, 616 + i:617 + i],
                s.sf[:, 600 + i:601 + i], op0=ALU.add, op1=ALU.mult))
            ctx.op("v", lambda v, xn=xn: v.tensor_tensor(
                xn(), xn(), s.sbB[:, 0:1024], op=ALU.mult))
            d3 = ctx.op("v", lambda v, xn=xn: v.tensor_tensor(
                xn(), xn(), s.sbB[:, 1024:2048], op=ALU.add), inc="dve")
            ctx.wait("t", "dve", d3)
            pairs = [(xn()[:, j * 128:(j + 1) * 128],
                      s.xt[:, j * T + i * 128:j * T + i * 128 + 128])
                     for j in range(DC)]
            self.bf_transpose_batch(ctx, pairs, depth=2)
            pe_ring[i] = ctx.cnt["pe"]

    def emit_L1_ffn(self, ctx):
        s = self
        inp = self.inp
        CAP = 1280
        E = 8
        self.dscr_e = self.nc.dram_tensor("dscr_e", [E * CAP + 128, 1024],
                                          F32, kind="Internal").ap()
        ctx.wait("t", "qs", self.c_x_loaded)
        ctx.wait("t", "dve", self.c_dve_consts)
        self.transpose_to_xt(ctx, s.x_cur)

        self.barrier(ctx)
        self.load_bcast(ctx, inp["L1_ffn_bg"], s.sf[:, 256:264], 8)
        ctx.wait("v", "qs")
        ctx.wait("t", "act")
        self.emit_logits_hl(ctx, inp["L1_ffn_Wg"], nexp=8, bias_off=256)
        ctx.op("v", lambda v: v.memset(s.sbB[:, 0:256], 0.0), inc="dve")
        ctx.wait("v", "dve")
        self.emit_topk_softmax(ctx, nexp=8, logit_off=0, wfull_off=None, kk=2,
                               slot_store=(s.sbB, 280, 296, 312, 328))
        self.snap(ctx, "l1oh", s.sbB[:, 0:256])

        self.barrier(ctx)
        ctx.wait("t", "dve")
        SC = s.sbA.bitcast(F32)
        ctx.op("g", lambda g: g.iota(s.ccf[:, 352:480].bitcast(I32),
                                     pattern=[[1, 128]], base=0,
                                     channel_multiplier=0), inc="pool")
        ctx.wait("v", "pool")
        ctx.op("v", lambda v: v.tensor_copy(s.cb[:, 0:128],
                                            s.ccf[:, 352:480].bitcast(I32)),
               inc="dve")
        ctx.wait("v", "dve")
        ctx.op("v", lambda v: v.tensor_scalar(
            s.cb[:, 0:128], s.cb[:, 0:128], s.ccf[:, 273:274], None,
            op0=ALU.is_gt), inc="dve")
        ctx.wait("v", "dve")
        ctx.op("v", lambda v: v.tensor_copy(s.sbB[:, 512:640], s.cb[:, 0:128]),
               inc="dve")
        tri_done = ctx.op("v", lambda v: v.memset(s.sbB[:, 640:648], 0.0),
                          inc="dve")
        ctx.op("v", lambda v: v.tensor_scalar(
            s.sf[:, 838:839], s.ccf[:, 273:274], 1.0, float(E * CAP),
            op0=ALU.mult, op1=ALU.add), inc="dve")
        def cmm(t_):
            return t_.matmul(s.PD[0:1, 0:256], lhsT=s.cc[:, 2304:2305],
                             rhs=s.sbB[:, 0:256], start=True, stop=True)
        pe_c = ctx.op("t", cmm, inc="pe")
        ctx.wait("v", "pe", pe_c)
        dc = ctx.op("v", lambda v: v.tensor_copy(
            s.sf[0:1, 344:600], s.PD[0:1, 0:256]), inc="dve")
        self.snap(ctx, "l1cnt", s.sf[:, 344:600])
        ctx.wait("q", "dve", dc)
        cq = ctx.dma("q", out=self.dscr_s.rearrange("a b -> (a b)")[256:512]
                     .unsqueeze(0), in_=s.sf[0:1, 344:600])
        ctx.wait("g", "qs", cq)
        ctx.wait("g", "dve")
        cg = ctx.dma("g", out=s.sbB[0:32, 640:648],
                     in_=self.dscr_s[16:32, 0:16]
                     .rearrange("a (u2 e) -> (a u2) e", e=8))
        ctx.wait("t", "qg", cg)
        ctx.wait("t", "dve", tri_done)
        def bmm_(t_):
            return t_.matmul(s.PE_[:, 0:8], lhsT=s.sbB[:, 512:640],
                             rhs=s.sbB[:, 640:648], start=True, stop=True)
        pe_b = ctx.op("t", bmm_, inc="pe")
        ctx.wait("v", "pe", pe_b)
        db = ctx.op("v", lambda v: v.tensor_copy(
            s.sf[0:32, 884:892], s.PE_[0:32, 0:8]), inc="dve")
        ctx.wait("q", "dve", db)
        bq = ctx.dma("q", out=self.dscr_s[0:32, 0:8], in_=s.sf[0:32, 884:892])
        ctx.wait("q", "qs", bq)
        bb = ctx.dma("q", out=s.sf[:, 344:600].rearrange("p (u e) -> p u e", e=8),
                     in_=self.dscr_s[0:32, 0:8].unsqueeze(0)
                     .broadcast_to([P, 32, 8]))
        self.snap(ctx, "l1base", s.sf[:, 344:600])
        def emm(t_):
            return t_.matmul(s.PA[:, 0:256], lhsT=s.cc[:, 2176:2304],
                             rhs=s.sbB[:, 0:256], start=True, stop=True)
        pe_e = ctx.op("t", emm, inc="pe")
        ctx.wait("v", "pe", pe_e)
        ctx.wait("v", "qs", bb)
        ctx.op("v", lambda v: v.tensor_tensor(
            SC[:, 0:256], s.PA[:, 0:256], s.sf[:, 344:600], op=ALU.add),
            inc="dve")
        ctx.wait("v", "dve")
        ctx.op("v", lambda v: v.tensor_tensor(
            SC[:, 0:256], SC[:, 0:256], s.sbB[:, 0:256], op=ALU.mult), inc="dve")
        ctx.wait("v", "dve")
        ctx.op("v", lambda v: v.reduce_sum(
            SC[:, 256:288], SC[:, 0:256].rearrange("p (u e) -> p u e", e=8),
            axis=AX.X), inc="dve")
        ctx.wait("v", "dve")
        ctx.op("v", lambda v: v.scalar_tensor_tensor(
            SC[:, 288:320], s.sf[:, 312:344], float(CAP), SC[:, 256:288],
            op0=ALU.mult, op1=ALU.add), inc="dve")
        ctx.op("v", lambda v: v.tensor_scalar(
            SC[:, 320:352], SC[:, 256:288], float(CAP), None,
            op0=ALU.is_ge), inc="dve")
        ctx.wait("v", "dve")
        ctx.op("v", lambda v: v.tensor_scalar(
            SC[:, 352:384], SC[:, 320:352], -1.0, 1.0,
            op0=ALU.mult, op1=ALU.add), inc="dve")
        ctx.wait("v", "dve")
        ctx.op("v", lambda v: v.tensor_tensor(
            SC[:, 288:320], SC[:, 288:320], SC[:, 352:384], op=ALU.mult),
            inc="dve")
        ctx.wait("v", "dve")
        ctx.op("v", lambda v: v.scalar_tensor_tensor(
            SC[:, 288:320], SC[:, 320:352], s.sf[:, 838:839], SC[:, 288:320],
            op0=ALU.mult, op1=ALU.add), inc="dve")
        ctx.wait("v", "dve")
        su_done = ctx.op("v", lambda v: v.tensor_copy(
            s.su[:, 0:32], SC[:, 288:320]), inc="dve")

        xe_view = self.dscr_a.rearrange("a b -> a b")
        ctx.wait("g", "dve", su_done)
        self._l1_scat = {}
        for u in range(32):
            si, i = u // 16, u % 16
            slot = 2048 + (u % 2) * 1024
            if u >= 2:
                ctx.wait("v", "qg", self._l1_scat[u - 2])
            xd = ctx.op("v", lambda v, si=si, i=i, slot=slot: v.tensor_scalar(
                s.sbB[:, slot:slot + 1024], s.x_cur[:, i * D:(i + 1) * D],
                s.sf[:, (280 if si == 0 else 296) + i:
                     (280 if si == 0 else 296) + i + 1], None,
                op0=ALU.mult), inc="dve")
            ctx.wait("g", "dve", xd)
            qg_t = ctx.op("g", lambda g, u=u, slot=slot: g.indirect_dma_start(
                out=self.dscr_a[0:6144, :],
                out_offset=bass.IndirectOffsetOnAxis(ap=s.su[:, u:u + 1], axis=0),
                in_=s.sbB[:, slot:slot + 1024], in_offset=None), inc="qg", by=16)
            self._l1_scat[u] = qg_t
        self._l1_scat = {}

        self.snap(ctx, "l1su", s.su[:, 0:32])
        self.snap(ctx, "l1ws", s.sf[:, 280:344])
        self.barrier(ctx)
        ctx.op("v", lambda v: v.memset(s.sbB[:, 4096:6144], 0.0), inc="dve")
        ctx.wait("g", "dve")
        ctx.dma("g", out=self.dscr_e[E * CAP:E * CAP + 128, :],
                in_=s.sbB[:, 4096:6144].bitcast(F32))

        self.barrier(ctx)
        W1, W2, W3 = inp["L1_ffn_W1"], inp["L1_ffn_W2"], inp["L1_ffn_W3"]
        ctx.wait("q", "qg")
        for e in range(E):
            ctx.wait("q", "pe")
            ctx.dma("q", out=s.sbB[:, 0:CAP * 8].rearrange("p (r c) -> p r c", c=1024),
                    in_=self.dscr_a[e * CAP:(e + 1) * CAP, :]
                    .rearrange("(r p) c -> p r c", p=P))
            ctx.wait("t", "qs")
            pairs = []
            for tb in range(CAP // 128):
                for j in range(8):
                    pairs.append((s.sbB[:, tb * 1024 + j * 128:tb * 1024 + (j + 1) * 128],
                                  s.xt[:, j * CAP + tb * 128:j * CAP + (tb + 1) * 128]))
            self.bf_transpose_batch(ctx, pairs, depth=2)
            if e == 0:
                ctx.wait("g", "pe")
                self._l1c13 = self.l1_w13_load(ctx, W1, W3, 0, 0)
            for q in range(4):
                ctx.wait("g", "pe")
                ctx.dma("g", out=s.xt[:, 10240:16384]
                        .rearrange("p (hc c) -> p hc c", c=1024),
                        in_=W2[e][q * 1024:q * 1024 + 768, :]
                        .rearrange("(hc p) c -> p hc c", p=P))
                c2 = ctx.dma("g", out=s.sbB[:, 13312:15360]
                             .rearrange("p (hc c) -> p hc c", c=1024),
                             in_=W2[e][q * 1024 + 768:(q + 1) * 1024, :]
                             .rearrange("(hc p) c -> p hc c", p=P))
                ctx.wait("t", "qg", self._l1c13)
                ctx.wait("t", "act")
                ctx.wait("t", "dve")
                MVS = ((0, 512), (512, 512), (1024, 256))
                units = [(hc, m0, mn) for hc in range(8) for m0, mn in MVS]
                d1ring = {}
                for ui, (hc, m0, mn) in enumerate(units):
                    pr = ui % 2
                    pa_a = s.PA[:, pr * 512:pr * 512 + mn]
                    pa_b = s.PA[:, 1024 + pr * 512:1024 + pr * 512 + mn]
                    if ui >= 2:
                        ctx.wait("t", "dve", d1ring[ui - 2])

                    def h1mm(t_, hc=hc, m0=m0, mn=mn, pa_a=pa_a, pa_b=pa_b):
                        last = None
                        for j in range(8):
                            last = t_.matmul(
                                pa_a,
                                lhsT=s.wbuf[:, j * 1024 + hc * 128:
                                            j * 1024 + (hc + 1) * 128],
                                rhs=s.xt[:, j * CAP + m0:j * CAP + m0 + mn],
                                start=(j == 0), stop=(j == 7))
                        for j in range(8):
                            last = t_.matmul(
                                pa_b,
                                lhsT=s.wbuf[:, 8192 + j * 1024 + hc * 128:
                                            8192 + j * 1024 + (hc + 1) * 128],
                                rhs=s.xt[:, j * CAP + m0:j * CAP + m0 + mn],
                                start=(j == 0), stop=(j == 7))
                        return last
                    p1 = ctx.op("t", h1mm, inc="pe")
                    ctx.wait("s", "pe", p1)
                    a1 = ctx.op("s", lambda a, mn=mn, pa_a=pa_a, pr=pr: a.activation(
                        s.sbB[:, 12288 + pr * 512:12288 + pr * 512 + mn], pa_a,
                        AF.Silu), inc="act")
                    ctx.wait("v", "act", a1)
                    d1 = ctx.op("v", lambda v, hc=hc, m0=m0, mn=mn, pa_b=pa_b, pr=pr:
                                v.tensor_tensor(
                                    s.sbB[:, hc * CAP + m0:hc * CAP + m0 + mn],
                                    s.sbB[:, 12288 + pr * 512:12288 + pr * 512 + mn],
                                    pa_b, op=ALU.mult), inc="dve")
                    d1ring[ui] = d1
                ctx.wait("t", "dve")
                if not (e == E - 1 and q == 3):
                    ne, nq = (e, q + 1) if q < 3 else (e + 1, 0)
                    ctx.wait("g", "pe")
                    self._l1c13 = self.l1_w13_load(ctx, W1, W3, ne, nq)
                ctx.wait("t", "qg", c2)
                if q >= 1:
                    ctx.wait("g", "qg")
                for g in range(2 * (CAP // 128)):
                    tb, nh = g // 2, g % 2

                    def bmm(t_, tb=tb, nh=nh, g=g):
                        last = None
                        for hc in range(8):
                            rhs = (s.xt[:, 10240 + hc * 1024 + nh * 512:
                                        10240 + hc * 1024 + nh * 512 + 512]
                                   if hc < 6 else
                                   s.sbB[:, 13312 + (hc - 6) * 1024 + nh * 512:
                                         13312 + (hc - 6) * 1024 + nh * 512 + 512])
                            last = t_.matmul(
                                s.PA[:, (g % 4) * 512:((g % 4) + 1) * 512],
                                lhsT=s.sbB[:, hc * CAP + tb * 128:
                                           hc * CAP + (tb + 1) * 128],
                                rhs=rhs,
                                start=(hc == 0), stop=(hc == 7))
                        return last
                    if g >= 4:
                        ctx.wait("t", "dve", self._l1_bev[g - 4])
                    pe_t = ctx.op("t", bmm, inc="pe")
                    ctx.wait("v", "pe", pe_t)
                    if g >= 2:
                        ctx.wait("v", "qg", self._l1_bq[g - 2])
                    dv = ctx.op("v", lambda v, g=g: v.tensor_copy(
                        s.cb[:, (g % 2) * 512:((g % 2) + 1) * 512],
                        s.PA[:, (g % 4) * 512:((g % 4) + 1) * 512]), inc="dve")
                    if not hasattr(self, "_l1_bev"):
                        self._l1_bev, self._l1_bq = {}, {}
                    self._l1_bev[g] = dv
                    ctx.wait("g", "dve", dv)
                    if q == 0:
                        qt = ctx.op("g", lambda gg, e=e, tb=tb, nh=nh, g=g:
                                    gg.dma_start(
                                        out=self.dscr_e[e * CAP + tb * 128:
                                                        e * CAP + (tb + 1) * 128,
                                                        nh * 512:(nh + 1) * 512],
                                        in_=s.cb[:, (g % 2) * 512:
                                                 ((g % 2) + 1) * 512]),
                                    inc="qg", by=16)
                    else:
                        qt = ctx.op("g", lambda gg, e=e, tb=tb, nh=nh, g=g:
                                    gg.dma_start(
                                        out=self.dscr_e[e * CAP + tb * 128:
                                                        e * CAP + (tb + 1) * 128,
                                                        nh * 512:(nh + 1) * 512],
                                        in_=s.cb[:, (g % 2) * 512:
                                                 ((g % 2) + 1) * 512],
                                        accum_op=ALU.add), inc="qg", by=16)
                    self._l1_bq[g] = qt
                self._l1_bev, self._l1_bq = {}, {}

        if "l1eo" in DBG_TAGS:
            dbg = self.dbg_out("dbg_l1eo", [1280, 1024], F32)
            ctx.wait("q", "qg")
            ctx.dma("q", out=dbg[:, :], in_=self.dscr_e[0:1280, :])
            for lst in ("t", "s", "v", "g"):
                ctx.wait(lst, "qs")
        if "l1xe" in DBG_TAGS:
            dbg2 = self.dbg_out("dbg_l1xe", [1280, 1024], BF16)
            ctx.wait("q", "qg")
            ctx.dma("q", out=dbg2[:, :], in_=self.dscr_a[0:1280, :])
            for lst in ("t", "s", "v", "g"):
                ctx.wait(lst, "qs")
        if L2FFN_PHASE < 5:
            return
        self.barrier(ctx)
        ctx.wait("g", "qg")
        for i in range(NT):
            for si in range(2):
                u = si * 16 + i
                slot = (2 * i + si) % 4
                gslot = s.sbB[:, slot * 2048:slot * 2048 + 2048].bitcast(F32)
                ctx.wait("g", "dve")
                ctx.op("g", lambda g, u=u, gslot=gslot: g.indirect_dma_start(
                    out=gslot, out_offset=None, in_=self.dscr_e[0:6144, :],
                    in_offset=bass.IndirectOffsetOnAxis(ap=s.su[:, u:u + 1], axis=0)),
                    inc="qg", by=16)
            ctx.wait("v", "qg")
            s0 = s.sbB[:, ((2 * i) % 4) * 2048:((2 * i) % 4) * 2048 + 2048].bitcast(F32)
            s1 = s.sbB[:, ((2 * i + 1) % 4) * 2048:((2 * i + 1) % 4) * 2048 + 2048].bitcast(F32)
            ctx.op("v", lambda v, s0=s0, s1=s1: v.tensor_tensor(
                s.cb[:, 0:1024], s0, s1, op=ALU.add))
            ctx.op("v", lambda v, i=i: v.scalar_tensor_tensor(
                s.x_cur[:, i * D:(i + 1) * D], s.x_cur[:, i * D:(i + 1) * D],
                2.0, s.cb[:, 0:1024], op0=ALU.mult, op1=ALU.add), inc="dve")
        self.ln_batch(ctx, [(s.x_cur[:, i * D:(i + 1) * D],
                             s.x_cur[:, i * D:(i + 1) * D]) for i in range(NT)],
                      sq_scratch=s.sbB[:, 8192:9216])


    def emit_L2_attn(self, ctx):
        s = self
        inp = self.inp
        PADQ, PADK, VON0, VON1, VT2 = (self.PADQ, self.PADK, self.VON0,
                                       self.VON1, self.VT2)
        ctx.wait("t", "qs", self.c_x_loaded)
        ctx.wait("t", "dve", self.c_dve_consts)

        ctx.wait("q", "dve")
        ctx.wait("q", "act")
        ctx.dma("q", out=self.dscr_f.rearrange("(i p) c -> p i c", p=P),
                in_=s.x_cur[:].rearrange("p (i c) -> p i c", c=D))

        self.barrier(ctx)
        self.emit_gained_ln_to_xt(ctx, inp["L2_attn_g1"], inp["L2_attn_b1"])
        ctx.wait("t", "act")

        self.barrier(ctx)
        ctx.wait("g", "pe")
        ctx.dma("g", out=s.wbuf[:, 0:8192].rearrange("p (j n) -> p j n", n=1024),
                in_=inp["L2_attn_sf_w"].rearrange("(j p) n -> p j n", p=P))
        ctx.dma("g", out=s.wbuf[:, 8192:16384].rearrange("p (j n) -> p j n", n=1024),
                in_=inp["L2_attn_mask"].rearrange("(j p) n -> p j n", p=P))
        self.load_bcast(ctx, inp["L2_attn_sf_b"], s.cb[:, 0:1024], D, lst="g")
        ctx.wait("v", "qg")
        ctx.op("v", lambda v: v.tensor_tensor(
            s.wbuf[:, 0:8192], s.wbuf[:, 0:8192], s.wbuf[:, 8192:16384],
            op=ALU.mult), inc="dve")
        ctx.wait("q", "dve")
        ctx.wait("q", "qs")
        ctx.dma("q", out=self.dscr_w[0:1024, 0:1024]
                .rearrange("(j p) n -> p j n", p=P),
                in_=s.wbuf[:, 0:8192].rearrange("p (j n) -> p j n", n=1024))
        ctx.wait("q", "qs")
        for j in range(DC):
            ctx.dma("q", out=s.wbuf[:, j * 1024:(j + 1) * 1024],
                    in_=self.dscr_w[0:1024, j * 128:(j + 1) * 128], transpose=True)
        ctx.wait("t", "qs")
        self.snap(ctx, "l2sfmT", s.wbuf[:, 0:8192])
        self.snap(ctx, "l2xn1", s.xt[:, :])

        def pmm(g):
            def f(t_, g=g):
                tt, nh = g // 2, g % 2
                last = None
                for j in range(DC):
                    last = t_.matmul(
                        s.PA[:, (g % 4) * 512:((g % 4) + 1) * 512],
                        lhsT=s.xt[:, j * T + tt * 128:j * T + (tt + 1) * 128],
                        rhs=s.wbuf[:, j * 1024 + nh * 512:j * 1024 + (nh + 1) * 512],
                        start=(j == 0), stop=(j == DC - 1))
                return last
            return f

        def pev(g):
            def f(v_, g=g):
                tt, nh = g // 2, g % 2
                sl = s.sbB[:, (g % 4) * 512:((g % 4) + 1) * 512]
                v_.scalar_tensor_tensor(sl, s.PA[:, (g % 4) * 512:((g % 4) + 1) * 512],
                                        1.0, s.cb[:, nh * 512:(nh + 1) * 512],
                                        op0=ALU.mult, op1=ALU.add)
                return v_.tensor_tensor(
                    s.x_cur[:, tt * D + nh * 512:tt * D + (nh + 1) * 512],
                    s.x_cur[:, tt * D + nh * 512:tt * D + (nh + 1) * 512],
                    sl, op=ALU.add)
            return f
        pipelined(ctx, 32, pmm, pev, "v", depth=4)

        self.snap(ctx, "l2p2", s.x_cur[:, :])
        self.barrier(ctx)
        self.emit_gained_ln_to_xt(ctx, inp["L2_attn_g2"], inp["L2_attn_b2"])
        ctx.wait("t", "act")

        self.barrier(ctx)
        qkvT = self.dscr_a.rearrange("a b -> (a b)")
        P4 = (("L2_attn_wq", "L2_attn_bq"), ("L2_attn_wk", "L2_attn_bk"),
              ("L2_attn_wv", "L2_attn_bv"))

        def p4_cast(m):
            r0 = 1024 + (m % 2) * 1024
            ctx.dma("g", out=self.dscr_w[r0:r0 + 1024, 0:1024], in_=inp[P4[m][0]])
            self.load_chunkcol(ctx, inp[P4[m][1]], s.sf[:, m * 8:m * 8 + 8], 8,
                               lst="g")
            return ctx.cnt["qg"]

        def p4_trload(m):
            r0 = 1024 + (m % 2) * 1024
            W0 = (m % 2) * 8192
            for j in range(DC):
                ctx.dma("q", out=s.wbuf[:, W0 + j * 1024:W0 + (j + 1) * 1024],
                        in_=self.dscr_w[r0:r0 + 1024, j * 128:(j + 1) * 128],
                        transpose=True)
            return ctx.cnt["qs"]

        ctx.wait("g", "qs")
        cg = {0: p4_cast(0)}
        ctx.wait("q", "qg", cg[0])
        ctx.wait("q", "pe")
        ct = {0: p4_trload(0)}
        self._l2q = {}
        dvring = {}
        for m in range(3):
            W0 = (m % 2) * 8192
            if m + 1 < 3:
                ctx.wait("g", "qs", ct[m - 1] if m > 0 else 0)
                cg[m + 1] = p4_cast(m + 1)
                ctx.wait("q", "qg", cg[m + 1])
                ctx.wait("q", "pe")
                ct[m + 1] = p4_trload(m + 1)
            ctx.wait("t", "qs", ct[m])
            self._p4ct2 = ct[m]
            ctx.wait("v", "qg", cg[m])
            for hp in range(8):
                ctx.wait("t", "dve")
                def qmm(t_, hp=hp, W0=W0):
                    last = None
                    for tc in range(4):
                        for j in range(DC):
                            last = t_.matmul(
                                s.PA[:, tc * 512:(tc + 1) * 512],
                                lhsT=s.wbuf[:, W0 + j * 1024 + hp * 128:
                                            W0 + j * 1024 + (hp + 1) * 128],
                                rhs=s.xt[:, j * T + tc * 512:j * T + (tc + 1) * 512],
                                start=(j == 0), stop=(j == DC - 1))
                    return last
                pe_t = ctx.op("t", qmm, inc="pe")
                ctx.wait("v", "pe", pe_t)
                gl = m * 8 + hp
                if gl >= 2:
                    ctx.wait("v", "qs", self._l2q[gl - 2])
                slot = (hp % 2) * 2048
                dv = ctx.op("v", lambda v, slot=slot, m=m, hp=hp: v.tensor_scalar(
                    s.sbB[:, slot:slot + 2048], s.PA[:, 0:2048],
                    s.sf[:, m * 8 + hp:m * 8 + hp + 1], None, op0=ALU.add),
                    inc="dve")
                ctx.wait("q", "dve", dv)
                qt = ctx.dma("q", out=qkvT[(m * 1024 + hp * 128) * 2048:
                                           (m * 1024 + (hp + 1) * 128) * 2048]
                             .rearrange("(p t) -> p t", p=P),
                             in_=s.sbB[:, slot:slot + 2048])
                self._l2q[gl] = qt
        self._l2q = {}

        self.barrier(ctx)
        ctx.wait("v", "qs")
        ctx.op("v", lambda v: v.memset(s.sbB[:, 0:8192], 0.0), inc="dve")
        for h2 in range(2):
            von = (VON0, VON1)[h2]
            ctx.op("v", lambda v, von=von: v.memset(s.sbB[:, von:von + 2048], 0.0),
                   inc="dve")
            last = None
            for mt in range(NT):
                last = ctx.op("v", lambda v, von=von, mt=mt: v.memset(
                    s.sbB[:, von + mt * 128 + 64:von + mt * 128 + 65], 1.0),
                    inc="dve")
        ctx.wait("q", "dve")
        ctx.wait("q", "qs")
        ctx.wait("t", "dve")
        O = s.sbA
        for hp in range(8):
            ctx.wait("q", "pe")
            ctx.dma("q", out=s.sbB[:, VT2:VT2 + 2048],
                    in_=qkvT[(2048 + hp * 128) * 2048:(2048 + (hp + 1) * 128) * 2048]
                    .rearrange("(p t) -> p t", p=P))
            ctx.wait("t", "qs")

            def vmm(g):
                def f(t_, g=g):
                    bank = [s.PBb, s.PCb][g % 2]
                    last = None
                    for k in range(8):
                        mtile = g * 8 + k
                        last = t_.transpose(bank[:, k * 128:(k + 1) * 128],
                                            s.sbB[:, VT2 + mtile * 128:
                                                   VT2 + (mtile + 1) * 128],
                                            s.cc[:, 2048:2176])
                    return last
                return f

            def vev(g):
                def f(e_, g=g):
                    bank = [s.PBb, s.PCb][g % 2]
                    last = None
                    for k in range(8):
                        mtile = g * 8 + k
                        last = e_.copy(s.sbB[:, VON0 + mtile * 128:VON0 + mtile * 128 + 64],
                                       bank[:, k * 128:k * 128 + 64])
                        last = e_.copy(s.sbB[:, VON1 + mtile * 128:VON1 + mtile * 128 + 64],
                                       bank[:, k * 128 + 64:k * 128 + 128])
                    return last
                return f
            pipelined(ctx, 2, vmm, vev, "s")
            for hh in range(2):
                h = 2 * hp + hh
                ctx.wait("q", "pe")
                ctx.dma("q", out=s.sbB[0:64, PADQ:PADQ + 2048],
                        in_=qkvT[(h * 64) * 2048:(h * 64 + 64) * 2048]
                        .rearrange("(p t) -> p t", p=64))
                ctx.dma("q", out=s.sbB[0:64, PADK:PADK + 2048],
                        in_=qkvT[(1024 + h * 64) * 2048:(1024 + h * 64 + 64) * 2048]
                        .rearrange("(p t) -> p t", p=64))
                ctx.wait("t", "qs")
                if h == 0:
                    self.snap(ctx, "l2padQ", s.sbB[:, PADQ:PADQ + 2048])
                    self.snap(ctx, "l2padK", s.sbB[:, PADK:PADK + 2048])
                    self.snap(ctx, "l2von", s.sbB[:, VON0:VON0 + 2048])
                self.attn_head_core(ctx, h, hh, kprime=False, score="relu", O=O,
                                    relu_eps=1e-9)

        self.snap(ctx, "l2O", O[:, :])
        self.barrier(ctx)
        ctx.wait("t", "dve")
        pairs = []
        for i in range(NT):
            for j in range(DC):
                pairs.append((O[:, i * D + j * 128:i * D + (j + 1) * 128],
                              s.xt[:, j * T + i * 128:j * T + i * 128 + 128]))
        self.bf_transpose_batch(ctx, pairs, depth=2)
        ctx.wait("g", "qs")
        ctx.dma("g", out=self.dscr_w[1024:2048, 0:1024], in_=inp["L2_attn_wo"])
        self.load_bcast(ctx, inp["L2_attn_bo"], s.cb[:, 0:1024], D, lst="g")
        ctx.wait("q", "qg")
        for j in range(DC):
            ctx.dma("q", out=s.wbuf[:, j * 1024:(j + 1) * 1024],
                    in_=self.dscr_w[1024:2048, j * 128:(j + 1) * 128], transpose=True)
        ctx.wait("t", "qs")
        ctx.wait("t", "act")
        ctx.wait("v", "qg")
        pipelined(ctx, 32, pmm, pev, "v", depth=4)

        self.snap(ctx, "l2p6", s.x_cur[:, :])
        self.barrier(ctx)
        self.emit_gained_ln_to_xt(ctx, inp["L2_attn_g3"], inp["L2_attn_b3"])
        ctx.wait("t", "act")

        self.barrier(ctx)
        fl = self.dscr_b.rearrange("a b -> (a b)")
        ff1T = fl[0:1024 * 4096].rearrange("(r c) -> r c", c=4096)
        ff2T = fl[1024 * 4096:2 * 1024 * 4096].rearrange("(r c) -> r c", c=1024)
        ctx.wait("g", "qs")
        ctx.dma("g", out=self.dscr_w[4096:8192, 0:1024], in_=inp["L2_attn_ff1_w"])
        ctx.dma("g", out=self.dscr_w[3072:4096, 0:4096], in_=inp["L2_attn_ff2_w"])
        self._p8cg = ctx.cnt["qg"]
        self.load_chunkcol(ctx, inp["L2_attn_ff1_b"], s.sf[:, 32:64], 32, lst="g")
        self.load_bcast(ctx, inp["L2_attn_ff2_b"], s.cb[:, 0:1024], D, lst="g")
        ctx.wait("q", "pe")
        ctx.wait("q", "dve")
        ctx.wait("q", "act")
        ctx.wait("q", "qg", self._p8cg)
        ctx.wait("g", "qs")
        self._l2f = {}
        for j in range(DC):
            sl = 8192 + (j % 2) * 4096
            if j >= 2:
                ctx.wait("q", "qg", self._l2f[j - 2])
            ctx.dma("q", out=s.sbB[:, sl:sl + 4096],
                    in_=self.dscr_w[4096:8192, j * 128:(j + 1) * 128],
                    transpose=True)
            ctx.wait("g", "qs")
            ft = ctx.dma("g", out=ff1T[j * 128:(j + 1) * 128, :]
                         .rearrange("p c -> p c"), in_=s.sbB[:, sl:sl + 4096])
            self._l2f[j] = ft
        ctx.wait("q", "qg")
        for hc in range(32):
            sl = 8192 + (hc % 2) * 1024
            if hc >= 2:
                ctx.wait("q", "qg", self._l2f[hc - 2])
            ctx.dma("q", out=s.sbB[:, sl:sl + 1024],
                    in_=self.dscr_w[3072:4096, hc * 128:(hc + 1) * 128],
                    transpose=True)
            ctx.wait("g", "qs")
            ft = ctx.dma("g", out=ff2T[hc * 128:(hc + 1) * 128, :],
                         in_=s.sbB[:, sl:sl + 1024])
            self._l2f[hc] = ft
        self._l2f = {}
        ctx.wait("q", "qg")
        ctx.wait("t", "qs")
        ctx.wait("t", "qg")
        ctx.wait("s", "qg")
        ctx.wait("v", "qg")
        ct1 = {}
        pe_after = {-1: ctx.cnt["pe"]}

        def p8_ff1load(k):
            tc, ng = k // 8, k % 8
            slot = (k % 2) * 4096
            return ctx.dma("q", out=s.wbuf[:, slot:slot + 4096]
                           .rearrange("p (j nt c) -> p j nt c", nt=4, c=128),
                           in_=ff1T[:, ng * 512:(ng + 1) * 512]
                           .rearrange("(j p) (nt c) -> p j nt c", p=P, c=128))

        def p8_ff2load(k):
            ntc = k % 4
            dst = (s.wbuf[:, 8192:16384] if k % 2 == 0
                   else s.sbA[:, 8192:16384])
            return ctx.dma("q", out=dst.rearrange("p (nt c) -> p nt c", c=1024),
                           in_=ff2T[ntc * 1024:(ntc + 1) * 1024, :]
                           .rearrange("(nt p) c -> p nt c", p=P))

        ct1[0] = p8_ff1load(0)
        ct2 = {}
        for tc in range(4):
            if tc > 0:
                ctx.wait("s", "pe")
            for ng in range(8):
                k = tc * 8 + ng
                slot = (k % 2) * 4096
                if k + 1 < 32:
                    ctx.wait("q", "pe", pe_after[k - 1])
                    ct1[k + 1] = p8_ff1load(k + 1)
                ctx.wait("t", "qs", ct1[k])
                if k == 0:
                    self.snap(ctx, "l2w1c0", s.wbuf[:, 0:4096])

                def amm(g):
                    def f(t_, g=g, tc=tc, slot=slot):
                        bank = [s.PD, s.PE_][g % 2]
                        last = None
                        for j in range(DC):
                            last = t_.matmul(
                                bank[:, :],
                                lhsT=s.wbuf[:, slot + j * 512 + g * 128:
                                            slot + j * 512 + (g + 1) * 128],
                                rhs=s.xt[:, j * T + tc * 512:j * T + (tc + 1) * 512],
                                start=(j == 0), stop=(j == DC - 1))
                        return last
                    return f

                def aev(g):
                    def f(e_, g=g, ng=ng):
                        bank = [s.PD, s.PE_][g % 2]
                        nt = ng * 4 + g
                        return e_.activation(s.sbB[:, nt * 512:(nt + 1) * 512],
                                             bank[:, :], AF.Relu,
                                             bias=s.sf[:, 32 + nt:32 + nt + 1],
                                             scale=1.0)
                    return f
                pipelined(ctx, 4, amm, aev, "s", depth=2)
                pe_after[k] = ctx.cnt["pe"]
            ctx.wait("t", "act")
            if tc == 0:
                self.snap(ctx, "l2stg0", s.sbB[:, 0:16384])
                ctx.wait("q", "pe", pe_after[-1])
                ct2[0] = p8_ff2load(0)
            for tth in range(2):
                pe_b = None
                for ntc in range(4):
                    k2 = tc * 8 + tth * 4 + ntc
                    if k2 + 1 < 32:
                        ctx.wait("q", "pe")
                        ct2[k2 + 1] = p8_ff2load(k2 + 1)
                    ctx.wait("t", "qs", ct2[k2])
                    ften = s.wbuf if k2 % 2 == 0 else s.sbA

                    def bmm(t_, tc=tc, tth=tth, ntc=ntc, ften=ften):
                        last = None
                        for grp in range(4):
                            ttr = tth * 2 + grp // 2
                            nh = grp % 2
                            for ni in range(8):
                                nt = ntc * 8 + ni
                                last = t_.matmul(
                                    s.PA[:, grp * 512:(grp + 1) * 512],
                                    lhsT=s.sbB[:, nt * 512 + ttr * 128:
                                               nt * 512 + ttr * 128 + 128],
                                    rhs=ften[:, 8192 + ni * 1024 + nh * 512:
                                             8192 + ni * 1024 + nh * 512 + 512],
                                    start=(nt == 0), stop=(nt == 31))
                        return last
                    pe_b = ctx.op("t", bmm, inc="pe")
                ctx.wait("v", "pe", pe_b)
                for grp in range(4):
                    ttr = tth * 2 + grp // 2
                    tt = tc * 4 + ttr
                    nh = grp % 2
                    ctx.op("v", lambda v, grp=grp, tt=tt, nh=nh:
                           v.scalar_tensor_tensor(
                               s.sbA[:, (grp % 2) * 512:((grp % 2) + 1) * 512],
                               s.PA[:, grp * 512:(grp + 1) * 512], 1.0,
                               s.cb[:, nh * 512:(nh + 1) * 512],
                               op0=ALU.mult, op1=ALU.add))
                    ctx.op("v", lambda v, grp=grp, tt=tt, nh=nh: v.tensor_tensor(
                        s.x_cur[:, tt * D + nh * 512:tt * D + (nh + 1) * 512],
                        s.x_cur[:, tt * D + nh * 512:tt * D + (nh + 1) * 512],
                        s.sbA[:, (grp % 2) * 512:((grp % 2) + 1) * 512],
                        op=ALU.add), inc="dve")
                ctx.wait("t", "dve")

        self.snap(ctx, "l2p8", s.x_cur[:, :])
        self.barrier(ctx)
        ctx.wait("q", "qs")
        ctx.wait("q", "act")
        ctx.wait("q", "pe")
        for i in range(NT):
            sl = (i % 2) * 2048
            ctx.wait("q", "dve")
            ctx.dma("q", out=s.sbB[:, sl:sl + 2048].bitcast(F32),
                    in_=self.dscr_f[i * 128:(i + 1) * 128, :])
            ctx.wait("v", "qs")
            ctx.op("v", lambda v, i=i, sl=sl: v.tensor_tensor(
                s.x_cur[:, i * D:(i + 1) * D], s.sbB[:, sl:sl + 2048].bitcast(F32),
                s.x_cur[:, i * D:(i + 1) * D], op=ALU.add), inc="dve")
        self.ln_batch(ctx, [(s.x_cur[:, i * D:(i + 1) * D],
                             s.x_cur[:, i * D:(i + 1) * D]) for i in range(NT)],
                      sq_scratch=s.sbB[:, 8192:9216])


    def emit_L3_attn(self, ctx):
        s = self
        inp = self.inp
        ctx.wait("t", "qs", self.c_x_loaded)
        ctx.wait("t", "dve", self.c_dve_consts)
        ctx.wait("v", "qs", self.c_x_loaded)
        ctx.wait("s", "qs", self.c_x_loaded)

        self.barrier(ctx)
        self.load_bcast(ctx, inp["L3_attn_gamma"], s.cb[:, 0:1024], D, lst="g")
        ctx.wait("v", "qg")
        for i in range(NT):
            xs = 2048 + (i % 2) * 1024
            xn = s.sbB[:, xs:xs + 1024]
            ctx.op("s", lambda a, i=i: a.activation(
                s.sbB[:, 4096:5120], s.x_cur[:, i * D:(i + 1) * D], AF.Square,
                bias=0.0, scale=1.0, accum_out=s.sf[:, 516:517]), inc="act")
            ctx.wait("s", "act")
            a1 = ctx.op("s", lambda a: a.activation(
                s.sf[:, 514:515], s.sf[:, 516:517], AF.Sqrt, bias=0.0, scale=1.0),
                inc="act")
            ctx.wait("v", "act", a1)
            ctx.op("v", lambda v: v.tensor_scalar(
                s.sf[:, 513:514], s.sf[:, 514:515], 1e-12, None, op0=ALU.max),
                inc="dve")
            ctx.wait("v", "dve")
            ctx.op("v", lambda v: v.reciprocal(s.sf[:, 515:516], s.sf[:, 513:514]),
                   inc="dve")
            ctx.wait("v", "dve")
            ctx.op("v", lambda v: v.tensor_scalar(
                s.sf[:, 512:513], s.sf[:, 515:516], 32.0, None, op0=ALU.mult),
                inc="dve")
            ctx.wait("v", "dve")
            ctx.op("v", lambda v, i=i, xn=xn: v.tensor_scalar(
                xn, s.x_cur[:, i * D:(i + 1) * D], s.sf[:, 512:513], None,
                op0=ALU.mult))
            d2 = ctx.op("v", lambda v, xn=xn: v.tensor_tensor(
                xn, xn, s.cb[:, 0:1024], op=ALU.mult), inc="dve")
            ctx.wait("t", "dve", d2)
            pairs = [(xn[:, j * 128:(j + 1) * 128],
                      s.xt[:, j * T + i * 128:j * T + i * 128 + 128])
                     for j in range(DC)]
            self.bf_transpose_batch(ctx, pairs, depth=2)
        ctx.wait("t", "act")

        self.snap(ctx, "l3xn", s.xt[:, :])
        self.barrier(ctx)
        wT = self.dscr_wt
        ctx.wait("g", "qs")
        ctx.dma("g", out=self.dscr_w[0:3072, 0:1024], in_=inp["L3_attn_W_qkva"])
        ctx.wait("q", "qg")
        ctx.wait("q", "dve")
        ctx.wait("q", "act")
        ctx.wait("q", "pe")
        for j in range(DC):
            sl = (j % 2) * 4096
            if j >= 2:
                ctx.wait("q", "qs", self._l3w[j - 2])
            ctx.dma("q", out=s.sbB[:, sl:sl + 3072],
                    in_=self.dscr_w[0:3072, j * 128:(j + 1) * 128], transpose=True)
            ctx.wait("q", "qs")
            if j == 0:
                self.snap(ctx, "l3pp", s.sbB[:, 0:3072])
            ft = ctx.dma("q", out=wT[j * 128:(j + 1) * 128, :],
                         in_=s.sbB[:, sl:sl + 3072])
            if not hasattr(self, "_l3w"):
                self._l3w = {}
            self._l3w[j] = ft
        self._l3w = {}
        ctx.wait("q", "qs")
        ctx.wait("g", "qs")
        if "l3wt" in DBG_TAGS:
            dbg = self.dbg_out("dbg_l3wt", [128, 3072], BF16)
            ctx.dma("q", out=dbg[:, :], in_=wT[0:128, :])
            for lst in ("t", "s", "v", "g"):
                ctx.wait(lst, "qs")

        self.barrier(ctx)
        for c in range(DC):
            ctx.wait("s", "dve")
            ctx.wait("g", "pe")
            for mi, col in enumerate((c * 128, 1024 + c * 128, 2048 + c * 128)):
                ctx.dma("g", out=s.wbuf[:, mi * 1024:(mi + 1) * 1024]
                        .rearrange("p (j n) -> p j n", n=128),
                        in_=wT[:, col:col + 128]
                        .rearrange("(j p) n -> p j n", p=P))
            ctx.wait("t", "qg")
            for mi in range(3):
                def hmm(t_, mi=mi):
                    last = None
                    for tc in range(4):
                        for j in range(DC):
                            last = t_.matmul(
                                s.PA[:, tc * 512:(tc + 1) * 512],
                                lhsT=s.wbuf[:, mi * 1024 + j * 128:
                                            mi * 1024 + (j + 1) * 128],
                                rhs=s.xt[:, j * T + tc * 512:j * T + (tc + 1) * 512],
                                start=(j == 0), stop=(j == DC - 1))
                    return last
                pe_t = ctx.op("t", hmm, inc="pe")
                ctx.wait("s", "pe", pe_t)
                if mi == 0:
                    a1 = ctx.op("s", lambda a, c=c: a.copy(
                        s.sbA[:, c * 2048:(c + 1) * 2048], s.PA[:, :]), inc="act")
                elif mi == 1:
                    a1 = ctx.op("s", lambda a: a.copy(
                        s.sbB[:, 0:2048], s.PA[:, :]), inc="act")
                else:
                    a1 = ctx.op("s", lambda a: a.activation(
                        s.sbB[:, 2048:4096], s.PA[:, :], AF.Sigmoid), inc="act")
                ctx.wait("t", "act", a1)
            if c == 0:
                self.snap(ctx, "l3w", s.wbuf[:, 0:3072])
                self.snap(ctx, "l3q", s.sbA[:, 0:2048])
                self.snap(ctx, "l3kv", s.sbB[:, 0:2048])
                self.snap(ctx, "l3a", s.sbB[:, 2048:4096])
            ctx.wait("v", "act")
            ctx.op("v", lambda v: v.tensor_tensor_scan(
                s.sbB[:, 4096:6144], s.sbB[:, 2048:4096], s.sbB[:, 0:2048],
                0.0, op0=ALU.mult, op1=ALU.add), inc="dve")
            ctx.wait("v", "dve")
            ctx.op("v", lambda v, c=c: v.tensor_tensor(
                s.sbA[:, c * 2048:(c + 1) * 2048],
                s.sbA[:, c * 2048:(c + 1) * 2048], s.sbB[:, 4096:6144],
                op=ALU.mult), inc="dve")
            if c == 0:
                self.snap(ctx, "l3y", s.sbB[:, 4096:6144])
                self.snap(ctx, "l3o", s.sbA[:, 0:2048])
            ctx.wait("g", "dve")

        self.barrier(ctx)
        self.snap(ctx, "l3oA", s.sbA[:, 1024:1280])
        ctx.wait("t", "dve")
        pairs = []
        for i in range(NT):
            for j in range(DC):
                pairs.append((s.sbA[:, j * 2048 + i * 128:j * 2048 + (i + 1) * 128],
                              s.sbB[:, i * 1024 + j * 128:i * 1024 + (j + 1) * 128]))
        self.bf_transpose_batch(ctx, pairs, depth=2)
        ctx.wait("v", "act")
        for i in range(NT):
            ctx.op("v", lambda v, i=i: v.tensor_tensor(
                s.x_cur[:, i * D:(i + 1) * D], s.x_cur[:, i * D:(i + 1) * D],
                s.sbB[:, i * 1024:(i + 1) * 1024], op=ALU.add), inc="dve")
        self.ln_batch(ctx, [(s.x_cur[:, i * D:(i + 1) * D],
                             s.x_cur[:, i * D:(i + 1) * D]) for i in range(NT)],
                      sq_scratch=s.wbuf[:, 0:1024])


    def emit_L3_ffn(self, ctx):
        s = self
        inp = self.inp
        ctx.wait("t", "qs", self.c_x_loaded)
        ctx.wait("t", "dve", self.c_dve_consts)
        ctx.wait("v", "qs", self.c_x_loaded)
        ctx.wait("s", "qs", self.c_x_loaded)
        self.transpose_to_xt(ctx, s.x_cur)
        ctx.wait("t", "act")

        def l3f_load(e):
            sl = (e % 2) * 8192
            ctx.dma("g", out=s.wbuf[:, sl:sl + 4096]
                    .rearrange("p (j nt c) -> p j nt c", nt=4, c=128),
                    in_=inp["L3_ffn_w1"][e].rearrange("(j p) (nt c) -> p j nt c",
                                                      p=P, c=128))
            return ctx.dma("g", out=s.wbuf[:, sl + 4096:sl + 8192]
                           .rearrange("p (hc c) -> p hc c", c=1024),
                           in_=inp["L3_ffn_w2"][e].rearrange("(hc p) c -> p hc c",
                                                             p=P))
        cload = {0: l3f_load(0), 1: l3f_load(1)}
        ctx.wait("v", "pe")
        ctx.op("v", lambda v: v.memset(s.sbA[:], 0.0), inc="dve")

        ctx.op("v", lambda v: v.memset(s.sbB[:, 0:1024], 0.0), inc="dve")
        ctx.wait("g", "dve")
        ctx.dma("g", out=s.sbB[0:6, 0:1024], in_=inp["L3_ffn_proj_w"])
        ctx.wait("t", "qg")
        ctx.wait("t", "dve")
        pairs = [(s.sbB[:, j * 128:(j + 1) * 128],
                  s.sbB[:, 1024 + j * 128:1024 + (j + 1) * 128]) for j in range(DC)]
        self.bf_transpose_batch(ctx, pairs, depth=2)
        ctx.wait("v", "act")
        ctx.op("v", lambda v: v.memset(s.cc[:, 2312:2376], 0.0), inc="dve")
        ctx.wait("v", "dve")
        for j in range(DC):
            ctx.op("v", lambda v, j=j: v.tensor_copy(
                s.cc[:, 2312 + j * 6:2312 + (j + 1) * 6],
                s.sbB[:, 1024 + j * 128:1024 + j * 128 + 6]), inc="dve")
        ctx.wait("t", "dve")

        for e2 in range(6):
            self.load_bcast(ctx, inp["L3_ffn_anchors"][e2], s.sf[:, 400 + e2 * 8:
                                                                400 + e2 * 8 + 6], 6)
        ctx.dma("q", out=s.sf[0:1, 448:449], in_=inp["L3_ffn_routing_temp"].unsqueeze(0))
        ctx.wait("v", "qs")
        ctx.op("v", lambda v: v.tensor_scalar(
            s.sf[0:1, 449:450], s.sf[0:1, 448:449], 0.1, None, op0=ALU.max),
            inc="dve")
        ctx.wait("v", "dve")
        ctx.op("v", lambda v: v.reciprocal(s.sf[0:1, 450:451], s.sf[0:1, 449:450]),
               inc="dve")
        ctx.wait("v", "dve")
        ctx.op("v", lambda v: v.tensor_scalar(
            s.sf[0:1, 451:452], s.sf[0:1, 450:451], 0.5, None, op0=ALU.mult),
            inc="dve")
        ctx.wait("q", "dve")
        ctx.dma("q", out=self.dscr_s[0:1, 0:1], in_=s.sf[0:1, 451:452])
        ctx.wait("q", "qs")
        ctx.dma("q", out=s.sf[:, 452:453],
                in_=self.dscr_s[0:1, 0:1].broadcast_to([P, 1]))
        ctx.wait("v", "qs")

        self.barrier(ctx)
        ctx.op("v", lambda v: v.memset(s.sf[:, 0:NT * 8], -1e30), inc="dve")
        ctx.wait("v", "dve")
        ctx.wait("t", "dve")
        for i in range(NT):
            def zmm(t_, i=i):
                last = None
                for j in range(DC):
                    last = t_.matmul(s.PD[:, 0:6],
                                     lhsT=s.xt[:, j * T + i * 128:j * T + (i + 1) * 128],
                                     rhs=s.cc[:, 2312 + j * 6:2312 + (j + 1) * 6],
                                     start=(j == 0), stop=(j == DC - 1))
                return last
            pe_t = ctx.op("t", zmm, inc="pe")
            ctx.wait("s", "pe", pe_t)
            ctx.wait("s", "dve")
            a1 = ctx.op("s", lambda a: a.activation(
                s.sf[:, 856:862], s.PD[:, 0:6], AF.Tanh, bias=0.0,
                scale=1.0 / 0.3), inc="act")
            ctx.wait("v", "act", a1)
            ctx.wait("t", "act", a1)
            for e2 in range(6):
                ctx.op("v", lambda v, e2=e2: v.tensor_tensor(
                    s.sf[:, 862:868], s.sf[:, 856:862],
                    s.sf[:, 400 + e2 * 8:400 + e2 * 8 + 6], op=ALU.mult),
                    inc="dve")
                ctx.wait("v", "dve")
                ctx.op("v", lambda v, e2=e2: v.reduce_sum(
                    s.sf[:, 868 + e2:869 + e2], s.sf[:, 862:868], axis=AX.X),
                    inc="dve")
                ctx.wait("v", "dve")
            ctx.op("v", lambda v, i=i: v.tensor_scalar(
                s.sf[:, i * 8:i * 8 + 6], s.sf[:, 868:874], -6.0, s.sf[:, 452:453],
                op0=ALU.add, op1=ALU.mult), inc="dve")
            ctx.wait("v", "dve")
        self.snap(ctx, "l3lg", s.sf[:, 0:128])
        self.emit_topk_softmax(ctx, nexp=6, logit_off=0, wfull_off=128, kk=2)
        self.snap(ctx, "l3wf", s.sf[:, 128:256])

        ra_act = ctx.cnt["act"]
        C3 = 0.044715 * np.sqrt(2.0 / np.pi)
        C1 = np.sqrt(2.0 / np.pi)
        bevring = {}
        first_bev = True
        for e in range(6):
            sl = (e % 2) * 8192
            ctx.wait("t", "qg", cload[e])
            if e == 0:
                ctx.wait("t", "act", ra_act)
            d2ring = {}
            for at in range(16):
                tc, hc = at // 4, at % 4
                pr = at % 2
                sA = s.sbB[:, 8192 + pr * 2048:8192 + pr * 2048 + 512]
                sB = s.sbB[:, 8192 + pr * 2048 + 512:8192 + pr * 2048 + 1024]
                bank = [s.PD, s.PE_][pr]
                if at >= 2:
                    ctx.wait("t", "dve", d2ring[at - 2])

                def amm(t_, hc=hc, tc=tc, sl=sl, bank=bank):
                    last = None
                    for j in range(DC):
                        last = t_.matmul(
                            bank[:, :],
                            lhsT=s.wbuf[:, sl + j * 512 + hc * 128:
                                        sl + j * 512 + (hc + 1) * 128],
                            rhs=s.xt[:, j * T + tc * 512:j * T + (tc + 1) * 512],
                            start=(j == 0), stop=(j == DC - 1))
                    return last
                pe_t = ctx.op("t", amm, inc="pe")
                ctx.wait("s", "pe", pe_t)
                a1 = ctx.op("s", lambda a, bank=bank, sA=sA: a.activation(
                    sA, bank[:, :], AF.Square), inc="act")
                ctx.wait("v", "act", a1)
                t1 = ctx.op("v", lambda v, sA=sA, sB=sB: v.tensor_scalar(
                    sB, sA, C3, C1, op0=ALU.mult, op1=ALU.add), inc="dve")
                ctx.wait("v", "dve", t1)
                d1 = ctx.op("v", lambda v, bank=bank, sB=sB: v.tensor_tensor(
                    sB, sB, bank[:, :], op=ALU.mult), inc="dve")
                ctx.wait("s", "dve", d1)
                a2 = ctx.op("s", lambda a, sA=sA, sB=sB: a.activation(
                    sA, sB, AF.Tanh), inc="act")
                ctx.wait("v", "act", a2)
                t2 = ctx.op("v", lambda v, sA=sA, sB=sB: v.tensor_scalar(
                    sB, sA, 0.5, 0.5, op0=ALU.mult, op1=ALU.add), inc="dve")
                ctx.wait("v", "dve", t2)
                d2 = ctx.op("v", lambda v, tc=tc, hc=hc, bank=bank, sB=sB:
                            v.tensor_tensor(
                                s.sbB[:, tc * 2048 + hc * 512:
                                      tc * 2048 + (hc + 1) * 512], sB,
                                bank[:, :], op=ALU.mult), inc="dve")
                d2ring[at] = d2
            ctx.wait("t", "dve")
            for bg in range(32):
                tc, g = bg // 8, bg % 8
                tt = tc * 4 + g // 2
                nh = g % 2
                gl = e * 32 + bg

                def bmm(t_, tc=tc, g=g, nh=nh, sl=sl, bg=bg):
                    last = None
                    for hc in range(4):
                        last = t_.matmul(
                            s.PA[:, (bg % 4) * 512:((bg % 4) + 1) * 512],
                            lhsT=s.sbB[:, tc * 2048 + hc * 512 + (g // 2) * 128:
                                       tc * 2048 + hc * 512 + (g // 2) * 128 + 128],
                            rhs=s.wbuf[:, sl + 4096 + hc * 1024 + nh * 512:
                                       sl + 4096 + hc * 1024 + nh * 512 + 512],
                            start=(hc == 0), stop=(hc == 3))
                    return last
                if gl >= 4:
                    ctx.wait("t", "dve", bevring[gl - 4])
                pe_t = ctx.op("t", bmm, inc="pe")
                ctx.wait("v", "pe", pe_t)
                if first_bev:
                    ctx.wait("v", "act")
                    first_bev = False
                wcol = s.sf[:, 128 + tt * 8 + e:128 + tt * 8 + e + 1]
                dst = s.sbA[:, tt * 1024 + nh * 512:tt * 1024 + (nh + 1) * 512]
                bevring[gl] = ctx.op("v", lambda v, dst=dst, wcol=wcol, bg=bg:
                                     v.scalar_tensor_tensor(
                                         dst,
                                         s.PA[:, (bg % 4) * 512:((bg % 4) + 1) * 512],
                                         wcol, dst, op0=ALU.mult, op1=ALU.add),
                                     inc="dve")
            if e + 2 < 6:
                ctx.wait("g", "pe")
                cload[e + 2] = l3f_load(e + 2)

        self.barrier(ctx)
        ctx.wait("v", "dve")
        for i in range(NT):
            ctx.op("v", lambda v, i=i: v.tensor_tensor(
                s.x_cur[:, i * D:(i + 1) * D], s.x_cur[:, i * D:(i + 1) * D],
                s.sbA[:, i * 1024:(i + 1) * 1024], op=ALU.add), inc="dve")
        self.ln_batch(ctx, [(s.x_cur[:, i * D:(i + 1) * D],
                             s.x_cur[:, i * D:(i + 1) * D]) for i in range(NT)],
                      sq_scratch=s.wbuf[:, 0:1024])


    def emit_L2_ffn(self, ctx):
        s = self
        inp = self.inp
        E2, CAP2, KK = 32, 1024, 4
        ctx.wait("t", "qs", self.c_x_loaded)
        ctx.wait("t", "dve", self.c_dve_consts)
        ctx.wait("v", "qs", self.c_x_loaded)
        ctx.wait("s", "qs", self.c_x_loaded)
        self.transpose_to_xt(ctx, s.x_cur)
        ctx.wait("t", "act")

        if L2FFN_PHASE < 0.4:
            return
        ctx.op("g", lambda g: g.iota(s.ccf[:, 280:312].bitcast(I32),
                                     pattern=[[1, 32]], base=0,
                                     channel_multiplier=0), inc="pool")
        ctx.wait("v", "pool")
        ctx.op("v", lambda v: v.tensor_copy(s.ccf[:, 320:352],
                                            s.ccf[:, 280:312].bitcast(I32)),
               inc="dve")

        if L2FFN_PHASE < 1:
            return
        self.barrier(ctx)
        self.emit_logits_hl(ctx, inp["L2_ffn_Wg"], nexp=32, bias_off=None,
                            out_stride=32)
        if L2FFN_PHASE < 1.2:
            return
        for i in range(NT):
            lg = s.sf[:, i * 32:(i + 1) * 32]
            ctx.op("v", lambda v, lg=lg: v.reduce_max(
                s.sf[:, 560:561], lg, axis=AX.X), inc="dve")
            ctx.wait("v", "dve")
            d1 = ctx.op("v", lambda v: v.tensor_scalar(
                s.sf[:, 561:562], s.sf[:, 560:561], -1.0, None, op0=ALU.mult),
                inc="dve")
            ctx.wait("s", "dve", d1)
            a1 = ctx.op("s", lambda a, lg=lg: a.activation(
                s.sf[:, 768:800], lg, AF.Exp, bias=s.sf[:, 561:562], scale=1.0,
                accum_out=s.sf[:, 562:563]), inc="act")
            ctx.wait("v", "act", a1)
            ctx.op("v", lambda v: v.reciprocal(s.sf[:, 563:564], s.sf[:, 562:563]),
                   inc="dve")
            ctx.wait("v", "dve")
            ctx.op("v", lambda v, lg=lg: v.tensor_scalar(
                lg, s.sf[:, 768:800], s.sf[:, 563:564], None, op0=ALU.mult),
                inc="dve")
            ctx.wait("s", "dve")
        if L2FFN_PHASE < 1.5:
            return
        ctx.op("v", lambda v: v.memset(s.sbB[:, 0:2048], 0.0), inc="dve")
        ctx.wait("v", "dve")
        B2 = 768
        for i in range(NT):
            lg = s.sf[:, i * 32:(i + 1) * 32]
            for k in range(KK):
                u = k * 16 + i
                oh = s.sbB[:, u * 32:(u + 1) * 32]
                mv = s.sf[:, 640 + k * 16 + i:640 + k * 16 + i + 1]
                ev = s.sf[:, 704 + k * 16 + i:704 + k * 16 + i + 1]

                def vv(emit):
                    ctx.op("v", emit, inc="dve")
                    ctx.wait("v", "dve")
                vv(lambda v, mv=mv, lg=lg: v.reduce_max(mv, lg, axis=AX.X))
                vv(lambda v, lg=lg, mv=mv: v.tensor_scalar(
                    s.sf[:, B2:B2 + 32], lg, mv, None, op0=ALU.is_equal))
                vv(lambda v: v.tensor_tensor_scan(
                    s.sf[:, B2 + 32:B2 + 64], s.sf[:, B2:B2 + 32],
                    s.sf[:, B2:B2 + 32], 0.0, op0=ALU.add, op1=ALU.bypass))
                vv(lambda v: v.tensor_scalar(
                    s.sf[:, B2 + 32:B2 + 64], s.sf[:, B2 + 32:B2 + 64], 1.0,
                    None, op0=ALU.is_equal))
                vv(lambda v: v.tensor_tensor(
                    s.sf[:, B2:B2 + 32], s.sf[:, B2:B2 + 32],
                    s.sf[:, B2 + 32:B2 + 64], op=ALU.mult))
                vv(lambda v, oh=oh: v.tensor_copy(oh, s.sf[:, B2:B2 + 32]))
                vv(lambda v: v.tensor_tensor(
                    s.sf[:, B2 + 32:B2 + 64], s.sf[:, B2:B2 + 32],
                    s.ccf[:, 320:352], op=ALU.mult))
                vv(lambda v, ev=ev: v.reduce_sum(ev, s.sf[:, B2 + 32:B2 + 64],
                                                 axis=AX.X))
                vv(lambda v, lg=lg: v.scalar_tensor_tensor(
                    lg, s.sf[:, B2:B2 + 32], -1e30, lg,
                    op0=ALU.mult, op1=ALU.add))

        if L2FFN_PHASE < 2:
            return
        self.snap(ctx, "f2oh", s.sbB[:, 0:2048])
        self.snap(ctx, "f2ws", s.sf[:, 640:768])
        self.barrier(ctx)
        ctx.wait("t", "dve")
        SC = s.sbA.bitcast(F32)
        ctx.op("g", lambda g: g.iota(s.ccf[:, 352:480].bitcast(I32),
                                     pattern=[[1, 128]], base=0,
                                     channel_multiplier=0), inc="pool")
        ctx.wait("v", "pool")
        ctx.op("v", lambda v: v.tensor_copy(s.cb[:, 0:128],
                                            s.ccf[:, 352:480].bitcast(I32)),
               inc="dve")
        ctx.wait("v", "dve")
        ctx.op("v", lambda v: v.tensor_scalar(
            s.cb[:, 0:128], s.cb[:, 0:128], s.ccf[:, 273:274], None,
            op0=ALU.is_gt), inc="dve")
        ctx.wait("v", "dve")
        ctx.op("v", lambda v: v.tensor_copy(s.sbB[:, 4224:4352], s.cb[:, 0:128]),
               inc="dve")
        ctx.op("v", lambda v: v.memset(s.sbB[:, 4128:4160], 0.0), inc="dve")
        tri_done = ctx.op("v", lambda v: v.tensor_scalar(
            s.sf[:, 838:839], s.ccf[:, 273:274], 1.0, float(E2 * CAP2),
            op0=ALU.mult, op1=ALU.add), inc="dve")
        dcl = None
        for kc in range(4):
            bank = [s.PD, s.PE_][kc % 2]
            if kc >= 2:
                ctx.wait("t", "dve", dcl)

            def cmm(t_, kc=kc, bank=bank):
                return t_.matmul(bank[0:1, 0:512], lhsT=s.cc[:, 2304:2305],
                                 rhs=s.sbB[:, kc * 512:(kc + 1) * 512],
                                 start=True, stop=True)
            pe_t = ctx.op("t", cmm, inc="pe")
            ctx.wait("v", "pe", pe_t)
            dcl = ctx.op("v", lambda v, kc=kc, bank=bank: v.tensor_copy(
                SC[0:1, kc * 512:(kc + 1) * 512], bank[0:1, 0:512]), inc="dve")
        ctx.wait("q", "dve", dcl)
        cq = ctx.dma("q", out=self.dscr_f.rearrange("a b -> (a b)")
                     [65536:67584].unsqueeze(0), in_=SC[0:1, 0:2048])
        ctx.wait("g", "qs", cq)
        ctx.wait("g", "dve")
        cg = ctx.dma("g", out=s.sbB[0:64, 4128:4160],
                     in_=self.dscr_f[64:66, 0:1024]
                     .rearrange("a (u2 e) -> (a u2) e", e=32))
        ctx.wait("t", "qg", cg)
        ctx.wait("t", "dve", tri_done)

        def bmm_(t_):
            return t_.matmul(s.PE_[:, 0:32], lhsT=s.sbB[:, 4224:4352],
                             rhs=s.sbB[:, 4128:4160], start=True, stop=True)
        pe_b = ctx.op("t", bmm_, inc="pe")
        ctx.wait("v", "pe", pe_b)
        db = ctx.op("v", lambda v: v.tensor_copy(
            s.sf[0:64, 800:832], s.PE_[0:64, 0:32]), inc="dve")
        ctx.wait("q", "dve", db)
        bq = ctx.dma("q", out=self.dscr_f[66:68, 0:1024]
                     .rearrange("a (u2 e) -> (a u2) e", e=32),
                     in_=s.sf[0:64, 800:832])
        ctx.wait("q", "qs", bq)
        bb = ctx.dma("q", out=SC[:, 2048:4096].rearrange("p (a b) -> p a b",
                                                         b=1024),
                     in_=self.dscr_f[66:68, :].unsqueeze(0)
                     .broadcast_to([P, 2, 1024]))
        self.snap(ctx, "f2cntT", s.sbB[:, 4128:4160])
        self.snap(ctx, "f2tri", s.sbB[:, 4224:4352])
        self.snap(ctx, "f2base", s.sf[:, 800:832])
        self.snap(ctx, "f2bb", SC[:, 2048:4096])
        pe_e = None
        for kc in range(4):
            def emm(t_, kc=kc):
                return t_.matmul(s.PA[:, kc * 512:(kc + 1) * 512],
                                 lhsT=s.cc[:, 2176:2304],
                                 rhs=s.sbB[:, kc * 512:(kc + 1) * 512],
                                 start=True, stop=True)
            pe_e = ctx.op("t", emm, inc="pe")
        ctx.wait("v", "pe", pe_e)
        ctx.wait("v", "qs", bb)
        ctx.op("v", lambda v: v.tensor_tensor(
            SC[:, 0:2048], s.PA[:, 0:2048], SC[:, 2048:4096], op=ALU.add),
            inc="dve")
        ctx.wait("v", "dve")
        ctx.op("v", lambda v: v.tensor_tensor(
            SC[:, 0:2048], SC[:, 0:2048], s.sbB[:, 0:2048], op=ALU.mult),
            inc="dve")
        ctx.wait("v", "dve")
        ctx.op("v", lambda v: v.reduce_sum(
            SC[:, 4096:4160], SC[:, 0:2048].rearrange("p (u e) -> p u e", e=32),
            axis=AX.X), inc="dve")
        ctx.wait("v", "dve")
        self.snap(ctx, "f2sel", SC[:, 4096:4160])
        ctx.op("v", lambda v: v.scalar_tensor_tensor(
            SC[:, 4160:4224], s.sf[:, 704:768], float(CAP2), SC[:, 4096:4160],
            op0=ALU.mult, op1=ALU.add), inc="dve")
        ctx.op("v", lambda v: v.tensor_scalar(
            SC[:, 4224:4288], SC[:, 4096:4160], float(CAP2), None,
            op0=ALU.is_ge), inc="dve")
        ctx.wait("v", "dve")
        ctx.op("v", lambda v: v.tensor_scalar(
            SC[:, 4288:4352], SC[:, 4224:4288], -1.0, 1.0,
            op0=ALU.mult, op1=ALU.add), inc="dve")
        ctx.wait("v", "dve")
        ctx.op("v", lambda v: v.tensor_tensor(
            SC[:, 4160:4224], SC[:, 4160:4224], SC[:, 4288:4352], op=ALU.mult),
            inc="dve")
        ctx.wait("v", "dve")
        ctx.op("v", lambda v: v.scalar_tensor_tensor(
            SC[:, 4160:4224], SC[:, 4224:4288], s.sf[:, 838:839],
            SC[:, 4160:4224], op0=ALU.mult, op1=ALU.add), inc="dve")
        ctx.wait("v", "dve")
        su_done = ctx.op("v", lambda v: v.tensor_copy(
            s.su[:, 0:64], SC[:, 4160:4224]), inc="dve")

        ctx.wait("g", "dve", su_done)
        self._l2s = {}
        for i in range(NT):
            slot = 2048 + (i % 2) * 1024
            if i >= 2:
                ctx.wait("v", "qg", self._l2s[i - 2])
            xd = ctx.op("v", lambda v, i=i, slot=slot: v.tensor_copy(
                s.sbB[:, slot:slot + 1024], s.x_cur[:, i * D:(i + 1) * D]),
                inc="dve")
            ctx.wait("g", "dve", xd)
            for k in range(KK):
                u = k * 16 + i
                qg_t = ctx.op("g", lambda g, u=u, slot=slot: g.indirect_dma_start(
                    out=self.dscr_a[0:E2 * CAP2 + 128, :],
                    out_offset=bass.IndirectOffsetOnAxis(ap=s.su[:, u:u + 1], axis=0),
                    in_=s.sbB[:, slot:slot + 1024], in_offset=None),
                    inc="qg", by=16)
            self._l2s[i] = qg_t
        self._l2s = {}

        if L2FFN_PHASE < 3:
            return
        self.snap(ctx, "f2su", s.su[:, 0:64])
        self.snap(ctx, "f2es", s.sf[:, 704:768])
        self.barrier(ctx)
        ctx.op("v", lambda v: v.memset(s.sbB[:, 4096:5120], 0.0), inc="dve")
        ctx.wait("g", "dve")
        ctx.dma("g", out=self.dscr_b[E2 * CAP2:E2 * CAP2 + 128, :],
                in_=s.sbB[:, 4096:5120])

        self.barrier(ctx)
        for hh2 in range(1):
            ctx.wait("g", "pe")
            ctx.dma("g", out=s.wbuf[:, 0:8192]
                    .rearrange("p (j nt c) -> p j nt c", nt=8, c=128),
                    in_=inp["L2_ffn_Ws1"][:, hh2 * 1024:(hh2 + 1) * 1024]
                    .rearrange("(j p) (nt c) -> p j nt c", p=P, c=128))
            ctx.dma("g", out=s.wbuf[:, 8192:16384]
                    .rearrange("p (j nt c) -> p j nt c", nt=8, c=128),
                    in_=inp["L2_ffn_Ws3"][:, hh2 * 1024:(hh2 + 1) * 1024]
                    .rearrange("(j p) (nt c) -> p j nt c", p=P, c=128))
            ctx.wait("t", "qg")
            for tc in range(4):
                for hc in range(8):
                    def h1mm(t_, hc=hc, tc=tc):
                        last = None
                        for j in range(DC):
                            last = t_.matmul(
                                s.PD[:, :],
                                lhsT=s.wbuf[:, j * 1024 + hc * 128:
                                            j * 1024 + (hc + 1) * 128],
                                rhs=s.xt[:, j * T + tc * 512:j * T + (tc + 1) * 512],
                                start=(j == 0), stop=(j == DC - 1))
                        for j in range(DC):
                            last = t_.matmul(
                                s.PE_[:, :],
                                lhsT=s.wbuf[:, 8192 + j * 1024 + hc * 128:
                                            8192 + j * 1024 + (hc + 1) * 128],
                                rhs=s.xt[:, j * T + tc * 512:j * T + (tc + 1) * 512],
                                start=(j == 0), stop=(j == DC - 1))
                        return last
                    p1 = ctx.op("t", h1mm, inc="pe")
                    ctx.wait("s", "pe", p1)
                    a1 = ctx.op("s", lambda a: a.activation(
                        s.sbB[:, 8192:8704], s.PD[:, :], AF.Silu), inc="act")
                    ctx.wait("v", "act", a1)
                    d1 = ctx.op("v", lambda v, hc=hc: v.tensor_tensor(
                        s.sbB[:, 4096 + hc * 512:4096 + (hc + 1) * 512],
                        s.sbB[:, 8192:8704], s.PE_[:, :], op=ALU.mult), inc="dve")
                    ctx.wait("t", "dve", d1)
                ctx.wait("g", "pe")
                ctx.dma("g", out=s.wbuf[:, 0:8192]
                        .rearrange("p (nt c) -> p nt c", c=1024),
                        in_=inp["L2_ffn_Ws2"][hh2 * 1024:(hh2 + 1) * 1024, :]
                        .rearrange("(nt p) c -> p nt c", p=P))
                ctx.wait("t", "qg")
                ctx.wait("t", "act")
                for g in range(8):
                    tt = tc * 4 + g // 2
                    nh = g % 2

                    def bmm(t_, g=g, tt=tt, nh=nh):
                        last = None
                        for hc in range(8):
                            last = t_.matmul(
                                s.PA[:, (g % 4) * 512:((g % 4) + 1) * 512],
                                lhsT=s.sbB[:, 4096 + hc * 512 + (g // 2) * 128:
                                           4096 + hc * 512 + (g // 2) * 128 + 128],
                                rhs=s.wbuf[:, hc * 1024 + nh * 512:
                                           hc * 1024 + nh * 512 + 512],
                                start=(hc == 0), stop=(hc == 7))
                        return last
                    if g >= 4:
                        ctx.wait("t", "dve")
                    pe_t = ctx.op("t", bmm, inc="pe")
                    ctx.wait("v", "pe", pe_t)
                    if hh2 == 0:
                        ctx.op("v", lambda v, g=g, tt=tt, nh=nh: v.tensor_copy(
                            s.sbA[:, tt * 1024 + nh * 512:tt * 1024 + (nh + 1) * 512],
                            s.PA[:, (g % 4) * 512:((g % 4) + 1) * 512]), inc="dve")
                    else:
                        ctx.op("v", lambda v, g=g, tt=tt, nh=nh: v.tensor_tensor(
                            s.sbA[:, tt * 1024 + nh * 512:tt * 1024 + (nh + 1) * 512],
                            s.sbA[:, tt * 1024 + nh * 512:tt * 1024 + (nh + 1) * 512],
                            s.PA[:, (g % 4) * 512:((g % 4) + 1) * 512],
                            op=ALU.add), inc="dve")
                ctx.wait("t", "dve")
                if tc < 3:
                    ctx.wait("g", "pe")
                    ctx.dma("g", out=s.wbuf[:, 0:8192]
                            .rearrange("p (j nt c) -> p j nt c", nt=8, c=128),
                            in_=inp["L2_ffn_Ws1"][:, hh2 * 1024:(hh2 + 1) * 1024]
                            .rearrange("(j p) (nt c) -> p j nt c", p=P, c=128))
                    ctx.wait("t", "qg")

        if L2FFN_PHASE < 4:
            return
        self.barrier(ctx)
        ctx.wait("q", "qg")
        W1, W2, W3 = inp["L2_ffn_W1"], inp["L2_ffn_W2"], inp["L2_ffn_W3"]

        def l2e_gather(e):
            return ctx.dma(
                "q", out=s.sbB[:, 0:8192].rearrange("p (r c) -> p r c", c=1024),
                in_=self.dscr_a[e * CAP2:(e + 1) * CAP2, :]
                .rearrange("(r p) c -> p r c", p=P))

        def l2e_w13(e):
            ctx.dma("g", out=s.wbuf[:, 0:4096]
                    .rearrange("p (j nt c) -> p j nt c", nt=4, c=128),
                    in_=W1[e].rearrange("(j p) (nt c) -> p j nt c", p=P, c=128))
            return ctx.dma("g", out=s.wbuf[:, 4096:8192]
                           .rearrange("p (j nt c) -> p j nt c", nt=4, c=128),
                           in_=W3[e].rearrange("(j p) (nt c) -> p j nt c",
                                               p=P, c=128))

        def l2e_w2(e):
            return ctx.dma("g", out=s.wbuf[:, 8192:12288]
                           .rearrange("p (hc c) -> p hc c", c=1024),
                           in_=W2[e].rearrange("(hc p) c -> p hc c", p=P))

        def l2e_transposes(e, gq):
            ctx.wait("t", "qs", gq)
            X0 = (e % 2) * 8192
            pairs = []
            for tb in range(8):
                for j in range(DC):
                    pairs.append(
                        (s.sbB[:, tb * 1024 + j * 128:tb * 1024 + (j + 1) * 128],
                         s.xt[:, X0 + j * CAP2 + tb * 128:
                              X0 + j * CAP2 + (tb + 1) * 128]))
            self.bf_transpose_batch(ctx, pairs, depth=2)

        ctx.wait("q", "pe")
        gq = l2e_gather(0)
        ctx.wait("g", "pe")
        ctx.wait("g", "dve")
        c13 = l2e_w13(0)
        c2 = l2e_w2(0)
        l2e_transposes(0, gq)
        self._l2e, self._l2eq = {}, {}
        for e in range(E2):
            X0 = (e % 2) * 8192
            ctx.wait("t", "qg", c13)
            d1ring = {}
            for ui in range(8):
                hc, mi = ui // 2, ui % 2
                m0 = mi * 512
                pr = ui % 2
                pa_a = s.PA[:, pr * 512:pr * 512 + 512]
                pa_b = s.PA[:, 1024 + pr * 512:1024 + pr * 512 + 512]
                if ui >= 2:
                    ctx.wait("t", "dve", d1ring[ui - 2])

                def amm(t_, hc=hc, m0=m0, X0=X0, pa_a=pa_a, pa_b=pa_b):
                    last = None
                    for j in range(DC):
                        last = t_.matmul(
                            pa_a,
                            lhsT=s.wbuf[:, j * 512 + hc * 128:
                                        j * 512 + (hc + 1) * 128],
                            rhs=s.xt[:, X0 + j * CAP2 + m0:X0 + j * CAP2 + m0 + 512],
                            start=(j == 0), stop=(j == DC - 1))
                    for j in range(DC):
                        last = t_.matmul(
                            pa_b,
                            lhsT=s.wbuf[:, 4096 + j * 512 + hc * 128:
                                        4096 + j * 512 + (hc + 1) * 128],
                            rhs=s.xt[:, X0 + j * CAP2 + m0:X0 + j * CAP2 + m0 + 512],
                            start=(j == 0), stop=(j == DC - 1))
                    return last
                p1 = ctx.op("t", amm, inc="pe")
                ctx.wait("s", "pe", p1)
                a1 = ctx.op("s", lambda a, pr=pr, pa_a=pa_a: a.activation(
                    s.sbB[:, 12288 + pr * 512:12288 + pr * 512 + 512], pa_a,
                    AF.Silu), inc="act")
                ctx.wait("v", "act", a1)
                d1 = ctx.op("v", lambda v, hc=hc, m0=m0, pr=pr, pa_b=pa_b:
                            v.tensor_tensor(
                                s.sbB[:, 8192 + hc * 1024 + m0:
                                      8192 + hc * 1024 + m0 + 512],
                                s.sbB[:, 12288 + pr * 512:12288 + pr * 512 + 512],
                                pa_b, op=ALU.mult), inc="dve")
                d1ring[ui] = d1
            ctx.wait("t", "dve")
            if e + 1 < E2:
                ctx.wait("g", "pe")
                c13 = l2e_w13(e + 1)
                ctx.wait("q", "pe")
                gq = l2e_gather(e + 1)
            ctx.wait("t", "qg", c2)
            for g in range(16):
                gg = e * 16 + g
                tb, nh = g // 2, g % 2

                def bmm(t_, g=g, tb=tb, nh=nh):
                    last = None
                    for hc in range(4):
                        last = t_.matmul(
                            s.PD[:, :] if g % 2 == 0 else s.PE_[:, :],
                            lhsT=s.sbB[:, 8192 + hc * 1024 + tb * 128:
                                       8192 + hc * 1024 + tb * 128 + 128],
                            rhs=s.wbuf[:, 8192 + hc * 1024 + nh * 512:
                                       8192 + hc * 1024 + nh * 512 + 512],
                            start=(hc == 0), stop=(hc == 3))
                    return last
                if gg >= 2:
                    ctx.wait("t", "act", self._l2e[gg - 2])
                pe_t = ctx.op("t", bmm, inc="pe")
                ctx.wait("s", "pe", pe_t)
                if gg >= 2:
                    ctx.wait("s", "qg", self._l2eq[gg - 2])
                a2 = ctx.op("s", lambda a, g=g: a.copy(
                    s.sbB[:, 13312 + (g % 2) * 512:13312 + (g % 2) * 512 + 512],
                    s.PD[:, :] if g % 2 == 0 else s.PE_[:, :]), inc="act")
                ctx.wait("g", "act", a2)
                qt = ctx.op("g", lambda gg_, e=e, tb=tb, nh=nh, g=g: gg_.dma_start(
                    out=self.dscr_b[e * CAP2 + tb * 128:e * CAP2 + (tb + 1) * 128,
                                    nh * 512:(nh + 1) * 512],
                    in_=s.sbB[:, 13312 + (g % 2) * 512:13312 + (g % 2) * 512 + 512]),
                    inc="qg", by=16)
                self._l2e[gg] = a2
                self._l2eq[gg] = qt
            if e + 1 < E2:
                l2e_transposes(e + 1, gq)
                ctx.wait("g", "pe")
                c2 = l2e_w2(e + 1)
        self._l2e, self._l2eq = {}, {}

        if L2FFN_PHASE < 5:
            return
        self.barrier(ctx)
        ctx.wait("g", "qg")
        for i in range(NT):
            for k in range(KK):
                u = k * 16 + i
                gslot = s.sbB[:, k * 1024:(k + 1) * 1024]
                ctx.wait("g", "dve")
                ctx.op("g", lambda g, u=u, gslot=gslot: g.indirect_dma_start(
                    out=gslot, out_offset=None,
                    in_=self.dscr_b[0:E2 * CAP2 + 128, :],
                    in_offset=bass.IndirectOffsetOnAxis(ap=s.su[:, u:u + 1], axis=0)),
                    inc="qg", by=16)
            ctx.wait("v", "qg")
            ctx.op("v", lambda v, i=i: v.tensor_tensor(
                s.cb[:, 0:1024], s.x_cur[:, i * D:(i + 1) * D],
                s.sbA[:, i * 1024:(i + 1) * 1024], op=ALU.add), inc="dve")
            for k in range(KK):
                wcol = s.sf[:, 640 + k * 16 + i:640 + k * 16 + i + 1]
                dst = (s.cb[:, 0:1024] if k < KK - 1
                       else s.x_cur[:, i * D:(i + 1) * D])
                ctx.op("v", lambda v, k=k, wcol=wcol, dst=dst: v.scalar_tensor_tensor(
                    dst, s.sbB[:, k * 1024:(k + 1) * 1024], wcol,
                    s.cb[:, 0:1024], op0=ALU.mult, op1=ALU.add), inc="dve")
        self.ln_batch(ctx, [(s.x_cur[:, i * D:(i + 1) * D],
                             s.x_cur[:, i * D:(i + 1) * D]) for i in range(NT)],
                      sq_scratch=s.wbuf[:, 0:1024])







def _weight_map(inputs, blocks):
    m = {}
    for b in blocks:
        for name, shape in BLOCK_WEIGHTS[b]:
            a = np.ascontiguousarray(np.asarray(inputs[name], np.float32))
            if a.ndim == 0:
                a = a.reshape(1)
            m[name] = a
    return m


FUSED = True


def kernel(**inputs):
    x = np.ascontiguousarray(np.asarray(inputs["L0_attn_x"], np.float32))
    if FUSED:
        prog = Prog(BLOCK_ORDER)
        nc = prog.build()
        wm = _weight_map(inputs, BLOCK_ORDER)
        in_maps = [dict(wm, x_in=x[c]) for c in range(NCORES)]
        res = run_bass_kernel_spmd(nc, in_maps, core_ids=list(range(NCORES)))
        cur = [np.asarray(res.results[c]["x_out"]) for c in range(NCORES)]
        return np.stack(cur).astype(np.float32)
    cur = [x[c] for c in range(NCORES)]
    for bname in BLOCK_ORDER:
        prog = Prog([bname])
        nc = prog.build()
        wm = _weight_map(inputs, [bname])
        in_maps = [dict(wm, x_in=cur[c]) for c in range(NCORES)]
        res = run_bass_kernel_spmd(nc, in_maps, core_ids=list(range(NCORES)))
        cur = [np.asarray(res.results[c]["x_out"]) for c in range(NCORES)]
    return np.stack(cur).astype(np.float32)


# version 08
# speedup vs baseline: 1.1446x; 1.1446x over previous
import numpy as np
from contextlib import ExitStack

import concourse.bass as bass
import concourse.mybir as mybir
from concourse.bass_utils import run_bass_kernel_spmd
from concourse.alu_op_type import AluOpType as ALU

F32 = mybir.dt.float32
BF16 = mybir.dt.bfloat16
U32 = mybir.dt.uint32
U8 = mybir.dt.uint8
I32 = mybir.dt.int32
AF = mybir.ActivationFunctionType
AX = mybir.AxisListType

P = 128
T = 2048
NT = T // P
D = 1024
DC = D // P
NHEADS = 16
DHEAD = 64
NCORES = 8

BLOCK_ORDER = ["L0_attn", "L0_ffn", "L1_attn", "L1_ffn",
               "L2_attn", "L2_ffn", "L3_attn", "L3_ffn"]

BLOCK_WEIGHTS = {
    "L0_attn": [("L0_attn_qkv_weights", (1024, 3072))],
    "L0_ffn": [("L0_ffn_ln_g", (1024,)), ("L0_ffn_ln_b", (1024,)),
               ("L0_ffn_aw1", (1024, 4096)), ("L0_ffn_ab1", (4096,)),
               ("L0_ffn_aw2", (4096, 1024)), ("L0_ffn_ab2", (1024,)),
               ("L0_ffn_gate_w", (1024, 6)), ("L0_ffn_gate_b", (6,)),
               ("L0_ffn_ew1", (6, 1024, 2048)), ("L0_ffn_eb1", (6, 2048)),
               ("L0_ffn_ew2", (6, 2048, 1024)), ("L0_ffn_eb2", (6, 1024)),
               ("L0_ffn_gens", (4, 1024, 1024)), ("L0_ffn_gc_w", (2048, 1)),
               ("L0_ffn_gc_b", (1,))],
    "L1_attn": [("L1_attn_W_attn", (1024, 3072)), ("L1_attn_b_attn", (3072,)),
                ("L1_attn_W_proj", (1024, 1024)), ("L1_attn_b_proj", (1024,))],
    "L1_ffn": [("L1_ffn_Wg", (1024, 8)), ("L1_ffn_bg", (8,)),
               ("L1_ffn_W1", (8, 1024, 4096)), ("L1_ffn_W2", (8, 4096, 1024)),
               ("L1_ffn_W3", (8, 1024, 4096))],
    "L2_attn": [("L2_attn_sf_w", (1024, 1024)), ("L2_attn_sf_b", (1024,)),
                ("L2_attn_mask", (1024, 1024)), ("L2_attn_wq", (1024, 1024)),
                ("L2_attn_bq", (1024,)), ("L2_attn_wk", (1024, 1024)),
                ("L2_attn_bk", (1024,)), ("L2_attn_wv", (1024, 1024)),
                ("L2_attn_bv", (1024,)), ("L2_attn_wo", (1024, 1024)),
                ("L2_attn_bo", (1024,)), ("L2_attn_ff1_w", (4096, 1024)),
                ("L2_attn_ff1_b", (4096,)), ("L2_attn_ff2_w", (1024, 4096)),
                ("L2_attn_ff2_b", (1024,)), ("L2_attn_g1", (1024,)),
                ("L2_attn_b1", (1024,)), ("L2_attn_g2", (1024,)),
                ("L2_attn_b2", (1024,)), ("L2_attn_g3", (1024,)),
                ("L2_attn_b3", (1024,))],
    "L2_ffn": [("L2_ffn_Wg", (1024, 32)), ("L2_ffn_W1", (32, 1024, 512)),
               ("L2_ffn_W2", (32, 512, 1024)), ("L2_ffn_W3", (32, 1024, 512)),
               ("L2_ffn_Ws1", (1024, 1024)), ("L2_ffn_Ws2", (1024, 1024)),
               ("L2_ffn_Ws3", (1024, 1024))],
    "L3_attn": [("L3_attn_W_qkva", (3072, 1024)), ("L3_attn_gamma", (1024,))],
    "L3_ffn": [("L3_ffn_proj_w", (6, 1024)), ("L3_ffn_anchors", (6, 6)),
               ("L3_ffn_routing_temp", ()), ("L3_ffn_w1", (6, 1024, 512)),
               ("L3_ffn_w2", (6, 512, 1024))],
}



class Ctx:

    ENG = {"q": "sync", "g": "gpsimd", "t": "tensor", "s": "scalar", "v": "vector"}

    def __init__(self, nc, st):
        self.nc = nc
        self.sems = {}
        self.cnt = {}
        for name in ("qs", "qg", "pe", "act", "dve", "pool"):
            self.sems[name] = st.enter_context(nc.semaphore(name))
            self.cnt[name] = 0
        self.L = {k: [] for k in self.ENG}

    def op(self, lst, emit, inc=None, by=1):
        if inc is None:
            self.L[lst].append(emit)
            return None
        sem = self.sems[inc]

        def f(eng, emit=emit, sem=sem, by=by):
            r = emit(eng)
            r.then_inc(sem, by)
        self.L[lst].append(f)
        self.cnt[inc] += by
        return self.cnt[inc]

    def wait(self, lst, sem, target=None):
        t = self.cnt[sem] if target is None else target
        assert t <= self.cnt[sem], f"forward wait on {sem}: {t} > {self.cnt[sem]}"
        s = self.sems[sem]
        self.L[lst].append(lambda eng, s=s, t=t: eng.wait_ge(s, t))

    def dma(self, lst, out, in_, **kw):
        sem = "qs" if lst == "q" else "qg"
        return self.op(lst, lambda e, out=out, in_=in_, kw=kw:
                       e.dma_start(out=out, in_=in_, **kw), inc=sem, by=16)

    def flush(self, blk):
        for name, attr in self.ENG.items():
            lst = self.L[name]
            if lst:
                def body(eng, lst=lst):
                    for f in lst:
                        f(eng)
                getattr(blk, attr)(body)


def pipelined(ctx, n, mm, ev, ev_lst="s", depth=2):
    ev_sem = {"s": "act", "v": "dve"}[ev_lst]
    ev_done = []
    ctx.wait("t", ev_sem)
    for g in range(n):
        if g >= depth:
            ctx.wait("t", ev_sem, ev_done[g - depth])
        pe_t = ctx.op("t", mm(g), inc="pe")
        ctx.wait(ev_lst, "pe", pe_t)
        ev_done.append(ctx.op(ev_lst, ev(g), inc=ev_sem))


DBG_TAGS = set()
L2FFN_PHASE = 5



class Prog:
    def __init__(self, blocks):
        self.blocks = list(blocks)
        nc = self.nc = bass.Bass("TRN2", target_bir_lowering=False)
        self.inp = {}
        self.x_in = nc.dram_tensor("x_in", [T, D], F32, kind="ExternalInput").ap()
        self.x_out = nc.dram_tensor("x_out", [T, D], F32, kind="ExternalOutput").ap()
        for b in self.blocks:
            for name, shape in BLOCK_WEIGHTS[b]:
                shp = list(shape) if shape else [1]
                self.inp[name] = nc.dram_tensor(name, shp, F32,
                                                kind="ExternalInput").ap()
        self.dbg = {}

    def barrier(self, ctx):
        for lst in ("q", "g", "t", "s", "v"):
            for sem in ("pe", "act", "dve", "pool", "qs", "qg"):
                ctx.wait(lst, sem)

    def dbg_out(self, name, shape, dt=F32):
        ap = self.nc.dram_tensor(name, list(shape), dt, kind="ExternalOutput").ap()
        self.dbg[name] = ap
        return ap

    def snap(self, ctx, tag, src_ap, dt=BF16):
        if tag not in DBG_TAGS:
            return
        n = src_ap.shape[-1]
        dst = self.dbg_out("dbg_" + tag, [P, n], src_ap.dtype)
        for sem in ("pe", "act", "dve", "pool"):
            ctx.wait("q", sem)
        ctx.dma("q", out=dst[:, :], in_=src_ap)
        for lst in ("t", "s", "v", "g"):
            ctx.wait(lst, "qs")

    def build(self):
        nc = self.nc
        with ExitStack() as st:
            def sb(name, shape, dt):
                return st.enter_context(nc.sbuf_tensor(name, shape, dt))

            def ps(name, shape, dt=F32):
                return st.enter_context(nc.psum_tensor(name, shape, dt))
            s = self
            s.x_cur = sb("x_cur", [P, NT * D], F32)
            s.sbA = sb("sbA", [P, NT * D], BF16)
            s.sbB = sb("sbB", [P, NT * D], BF16)
            s.xt = sb("xt", [P, DC * T], BF16)
            s.wbuf = sb("wbuf", [P, 16 * 1024], BF16)
            s.sf = sb("sf", [P, 896], F32)
            s.cb = sb("cb", [P, 1152], F32)
            s.cc = sb("cc", [P, 2048 + 384], BF16)
            s.su = sb("su", [P, 64], U32)
            s.idf = sb("idf", [P, P], F32)
            s.ccf = sb("ccf", [P, 512], F32)
            s.PA = ps("PA", [P, 2048])
            s.PBb = ps("PBb", [P, 1024], BF16)
            s.PCb = ps("PCb", [P, 1024], BF16)
            s.PD = ps("PD", [P, 512])
            s.PE_ = ps("PE_", [P, 512])
            s.dscr_a = nc.dram_tensor("dscr_a", [33024, 1024], BF16, kind="Internal").ap()
            s.dscr_b = nc.dram_tensor("dscr_b", [33024, 1024], BF16, kind="Internal").ap()
            s.dscr_w = nc.dram_tensor("dscr_w", [8192, 4096], BF16, kind="Internal").ap()
            s.dscr_f = nc.dram_tensor("dscr_f", [T, D], F32, kind="Internal").ap()
            s.dscr_wt = nc.dram_tensor("dscr_wt", [1024, 3072], BF16,
                                       kind="Internal").ap()
            s.dscr_s = nc.dram_tensor("dscr_s", [64, 16], F32, kind="Internal").ap()

            ctx = self.ctx = Ctx(nc, st)
            blk = st.enter_context(nc.Block())
            self.emit_consts(ctx)
            self.emit_load_x(ctx)
            for b in self.blocks:
                self.barrier(ctx)
                getattr(self, "emit_" + b)(ctx)
            self.barrier(ctx)
            self.emit_store_x(ctx)
            ctx.flush(blk)
        return nc

    def emit_consts(self, ctx):
        s = self
        ctx.op("g", lambda g: g.iota(s.ccf[:, 0:128].bitcast(I32),
                                     pattern=[[-1, 128]], base=0,
                                     channel_multiplier=1), inc="pool")
        ctx.op("g", lambda g: g.iota(s.ccf[:, 128:128 + 384].bitcast(I32),
                                     pattern=[[1, 384]], base=0,
                                     channel_multiplier=-1), inc="pool")
        ctx.op("g", lambda g: g.iota(s.sf[:, 0:512].bitcast(I32),
                                     pattern=[[1, 512]], base=0,
                                     channel_multiplier=-1), inc="pool")
        ctx.wait("v", "pool")
        ctx.op("v", lambda v: v.tensor_scalar(s.idf[:], s.ccf[:, 0:128].bitcast(I32),
                                              0.0, None, op0=ALU.is_equal), inc="dve")
        ctx.op("v", lambda v: v.tensor_scalar(s.cc[:, 2048:2176],
                                              s.ccf[:, 0:128].bitcast(I32),
                                              0.0, None, op0=ALU.is_equal), inc="dve")
        ctx.op("v", lambda v: v.tensor_scalar(s.cc[:, 2176:2304],
                                              s.ccf[:, 0:128].bitcast(I32),
                                              -1.0, None, op0=ALU.is_le), inc="dve")
        for m in range(4):
            ctx.op("v", lambda v, m=m: v.tensor_scalar(
                s.cc[:, m * 512:(m + 1) * 512], s.sf[:, 0:512].bitcast(I32),
                float(128 * m), None, op0=ALU.is_ge), inc="dve")
        ctx.op("v", lambda v: v.memset(s.cc[:, 2304:2312], 1.0), inc="dve")
        ctx.op("g", lambda g: g.iota(s.ccf[:, 272:273].bitcast(I32),
                                     pattern=[[0, 1]], base=0,
                                     channel_multiplier=1), inc="pool")
        ctx.wait("v", "pool")
        ctx.op("v", lambda v: v.tensor_copy(s.sf[:, 272:273],
                                            s.ccf[:, 272:273].bitcast(I32)),
               inc="dve")
        ctx.op("v", lambda v: v.tensor_copy(s.ccf[:, 273:274],
                                            s.ccf[:, 272:273].bitcast(I32)),
               inc="dve")
        ctx.op("g", lambda g: g.iota(s.ccf[:, 264:272].bitcast(I32),
                                     pattern=[[1, 8]], base=0,
                                     channel_multiplier=0), inc="pool")
        ctx.wait("v", "pool")
        ctx.op("v", lambda v: v.tensor_copy(s.sf[:, 264:272],
                                            s.ccf[:, 264:272].bitcast(I32)),
               inc="dve")
        ctx.op("v", lambda v: v.memset(s.ccf[:, 256:257], 1e-5), inc="dve")
        self.c_dve_consts = ctx.cnt["dve"]

    def emit_load_x(self, ctx):
        s = self
        ctx.dma("q", out=s.x_cur[:].rearrange("p (i c) -> p i c", c=D),
                in_=s.x_in.rearrange("(i p) c -> p i c", p=P))
        self.c_x_loaded = ctx.cnt["qs"]

    def emit_store_x(self, ctx):
        s = self
        ctx.wait("q", "dve")
        ctx.wait("q", "act")
        ctx.dma("q", out=s.x_out.rearrange("(i p) c -> p i c", p=P),
                in_=s.x_cur[:].rearrange("p (i c) -> p i c", c=D))
        ctx.wait("q", "qs")

    def transpose_to_xt(self, ctx, src_f32, dst=None, src_tile_off=None):
        s = self
        dst = dst if dst is not None else s.xt
        quads = [(i, j4) for i in range(NT) for j4 in range(2)]

        def mm(g):
            i, j4 = quads[g]
            bank = [s.PD, s.PE_][g % 2]

            def f(t_, i=i, j4=j4, bank=bank):
                last = None
                for jj in range(4):
                    j = j4 * 4 + jj
                    last = t_.transpose(bank[:, jj * 128:(jj + 1) * 128],
                                        src_f32[:, i * D + j * 128:i * D + j * 128 + 128],
                                        s.idf[:])
                return last
            return f

        def ev(g):
            i, j4 = quads[g]
            bank = [s.PD, s.PE_][g % 2]

            def f(e_, i=i, j4=j4, bank=bank):
                last = None
                for jj in range(4):
                    j = j4 * 4 + jj
                    last = e_.copy(dst[:, j * T + i * 128:j * T + i * 128 + 128],
                                   bank[:, jj * 128:(jj + 1) * 128])
                return last
            return f
        pipelined(ctx, len(quads), mm, ev, "s")

    def ln_plain(self, ctx, make_sum_tile, out_buf=None):
        s = self
        out_buf = out_buf if out_buf is not None else s.x_cur
        for i in range(NT):
            ctx.op("v", lambda v, i=i: make_sum_tile(v, i))
            ctx.op("v", lambda v: v.reduce_sum(s.sf[:, 520:521], s.cb[:, 0:D],
                                               axis=AX.X, negate=True), inc="dve")
            ctx.wait("v", "dve")
            d1 = ctx.op("v", lambda v: v.tensor_scalar(
                s.sf[:, 512:513], s.sf[:, 520:521], 1.0 / D, None, op0=ALU.mult),
                inc="dve")
            ctx.wait("s", "dve", d1)
            ctx.op("s", lambda a: a.activation(s.sbB[:, 0:D],
                                               s.cb[:, 0:D],
                                               AF.Square, bias=s.sf[:, 512:513],
                                               scale=1.0,
                                               accum_out=s.sf[:, 516:517]), inc="act")
            ctx.wait("s", "act")
            a1 = ctx.op("s", lambda a: a.activation(
                s.sf[:, 514:515], s.sf[:, 516:517], AF.Sqrt,
                bias=s.ccf[:, 256:257], scale=1.0 / D), inc="act")
            ctx.wait("v", "act", a1)
            ctx.op("v", lambda v: v.reciprocal(s.sf[:, 515:516], s.sf[:, 514:515]),
                   inc="dve")
            ctx.wait("v", "dve")
            ctx.op("v", lambda v, i=i: v.tensor_scalar(
                out_buf[:, i * D:(i + 1) * D], s.cb[:, 0:D], s.sf[:, 512:513],
                s.sf[:, 515:516], op0=ALU.add, op1=ALU.mult), inc="dve")


    def load_bcast(self, ctx, vec_ap, dst, n, lst="q"):
        src = vec_ap
        if len(src.shape) == 1:
            src = src.unsqueeze(0)
        ctx.dma(lst, out=dst, in_=src.broadcast_to([P, n]))

    def load_chunkcol(self, ctx, vec_ap, dst, nch, lst="q"):
        ctx.dma(lst, out=dst, in_=vec_ap.rearrange("(a p) -> p a", p=P),
                allow_slow_non_contiguous=True)

    def bf_transpose_batch(self, ctx, pairs, depth=2):
        s = self
        groups = [pairs[i:i + 8] for i in range(0, len(pairs), 8)]

        def mm(g):
            def f(t_, g=g):
                bank = [s.PBb, s.PCb][g % 2]
                last = None
                for k, (inap, _outap) in enumerate(groups[g]):
                    last = t_.transpose(bank[0:inap.shape[-1], k * 128:(k + 1) * 128],
                                        inap, s.cc[:, 2048:2176])
                return last
            return f

        def ev(g):
            def f(e_, g=g):
                bank = [s.PBb, s.PCb][g % 2]
                last = None
                for k, (inap, outap) in enumerate(groups[g]):
                    np_ = outap.shape[0] if len(outap.shape) == 2 else P
                    last = e_.copy(outap, bank[0:np_, k * 128:k * 128 + outap.shape[-1]])
                return last
            return f
        pipelined(ctx, len(groups), mm, ev, "s", depth=depth)

    def ln_tile(self, ctx, out_ap, sq_scratch=None):
        s = self
        sq = sq_scratch if sq_scratch is not None else s.sbB[:, 4096:5120]
        ctx.op("v", lambda v: v.reduce_sum(s.sf[:, 520:521], s.cb[:, 0:1024],
                                           axis=AX.X, negate=True), inc="dve")
        ctx.wait("v", "dve")
        d1 = ctx.op("v", lambda v: v.tensor_scalar(
            s.sf[:, 512:513], s.sf[:, 520:521], 1.0 / D, None, op0=ALU.mult),
            inc="dve")
        ctx.wait("s", "dve", d1)
        ctx.op("s", lambda a: a.activation(sq, s.cb[:, 0:1024], AF.Square,
                                           bias=s.sf[:, 512:513], scale=1.0,
                                           accum_out=s.sf[:, 516:517]), inc="act")
        ctx.wait("s", "act")
        a2 = ctx.op("s", lambda a: a.activation(
            s.sf[:, 514:515], s.sf[:, 516:517], AF.Sqrt,
            bias=s.ccf[:, 256:257], scale=1.0 / D), inc="act")
        ctx.wait("v", "act", a2)
        ctx.op("v", lambda v: v.reciprocal(s.sf[:, 515:516], s.sf[:, 514:515]),
               inc="dve")
        ctx.wait("v", "dve")
        ctx.op("v", lambda v, out_ap=out_ap: v.tensor_scalar(
            out_ap, s.cb[:, 0:1024], s.sf[:, 512:513], s.sf[:, 515:516],
            op0=ALU.add, op1=ALU.mult), inc="dve")

    def ln_batch(self, ctx, tiles, sq_scratch):
        s = self
        n = len(tiles)
        assert n <= 16
        for i, (src, _o) in enumerate(tiles):
            ctx.op("v", lambda v, i=i, src=src: v.reduce_sum(
                s.sf[:, 600 + i:601 + i], src, axis=AX.X, negate=True), inc="dve")
        ctx.wait("v", "dve")
        d1 = ctx.op("v", lambda v, n=n: v.tensor_scalar(
            s.sf[:, 616:616 + n], s.sf[:, 600:600 + n], 1.0 / D, None,
            op0=ALU.mult), inc="dve")
        ctx.wait("s", "dve", d1)
        for i, (src, _o) in enumerate(tiles):
            ctx.op("s", lambda a, i=i, src=src: a.activation(
                sq_scratch, src, AF.Square, bias=s.sf[:, 616 + i:617 + i],
                scale=1.0, accum_out=s.sf[:, 526 + i:527 + i]), inc="act")
        ctx.wait("s", "act")
        a2 = ctx.op("s", lambda a, n=n: a.activation(
            s.sf[:, 542:542 + n], s.sf[:, 526:526 + n], AF.Sqrt,
            bias=s.ccf[:, 256:257], scale=1.0 / D), inc="act")
        ctx.wait("v", "act", a2)
        d2 = ctx.op("v", lambda v, n=n: v.reciprocal(
            s.sf[:, 600:600 + n], s.sf[:, 542:542 + n]), inc="dve")
        ctx.wait("v", "dve", d2)
        for i, (src, out) in enumerate(tiles):
            ctx.op("v", lambda v, i=i, src=src, out=out: v.tensor_scalar(
                out, src, s.sf[:, 616 + i:617 + i], s.sf[:, 600 + i:601 + i],
                op0=ALU.add, op1=ALU.mult), inc="dve")

    def l1_w13_load(self, ctx, W1, W3, e, q):
        s = self
        ctx.dma("g", out=s.wbuf[:, 0:8192]
                .rearrange("p (j nt c) -> p j nt c", nt=8, c=128),
                in_=W1[e][:, q * 1024:(q + 1) * 1024]
                .rearrange("(j p) (nt c) -> p j nt c", p=P, c=128))
        return ctx.dma("g", out=s.wbuf[:, 8192:16384]
                       .rearrange("p (j nt c) -> p j nt c", nt=8, c=128),
                       in_=W3[e][:, q * 1024:(q + 1) * 1024]
                       .rearrange("(j p) (nt c) -> p j nt c", p=P, c=128))

    def emit_L0_attn(self, ctx):
        self.emit_attn_std(ctx, self.inp["L0_attn_qkv_weights"], bias=None,
                           kprime=True, proj=None)

    def emit_L1_attn(self, ctx):
        self.emit_attn_std(ctx, self.inp["L1_attn_W_attn"],
                           bias=self.inp["L1_attn_b_attn"], kprime=False,
                           proj=(self.inp["L1_attn_W_proj"],
                                 self.inp["L1_attn_b_proj"]))

    PADQ, PADK, PADKP, STQK, VON0, VON1, VT2, CTXB = (
        0, 2048, 4096, 6144, 8192, 10240, 12288, 14336)
    ATB = 6144

    def attn_head_core(self, ctx, h, hh, kprime, score, O, relu_eps=None,
                       qbase=None, kbase=None):
        s = self
        PADQ, PADK, PADKP, STQK, VON0, VON1, VT2, CTXB = (
            self.PADQ, self.PADK, self.PADKP, self.STQK, self.VON0, self.VON1,
            self.VT2, self.CTXB)
        ATB = self.ATB
        qbase = PADQ if qbase is None else qbase
        kbase = PADK if kbase is None else kbase
        SCORE_K = PADKP if kprime else kbase
        def kmm(g):
            def f(t_, g=g):
                bank = [s.PBb, s.PCb][g % 2]
                last = None
                for k in range(8):
                    sb = g * 8 + k
                    last = t_.transpose(bank[0:64, k * 128:(k + 1) * 128],
                                        s.sbB[:, kbase + sb * 64:kbase + sb * 64 + 64],
                                        s.cc[:, 2048:2176])
                return last
            return f

        def kev(g):
            def f(e_, g=g):
                bank = [s.PBb, s.PCb][g % 2]
                kp = s.sbB[0:64, PADKP:PADKP + 2048].rearrange(
                    "p (d sb) -> p d sb", sb=32)
                last = None
                for k in range(8):
                    sb = g * 8 + k
                    last = e_.copy(kp[:, :, sb], bank[0:64, k * 128:k * 128 + 64])
                return last
            return f
        if kprime:
            pipelined(ctx, 4, kmm, kev, "s")

        if h == 0:
            self.snap(ctx, "padKp0", s.sbB[:, PADKP:PADKP + 2048])
        von = (VON0, VON1)[hh]
        pe_av, avbank, a_ct_par = {}, {}, {}
        prev_sev_last = ctx.cnt["act"]
        for qci in range(5):
            if qci < 4:
                qc = qci
                nk = 4 * (qc + 1)
                if qc >= 1:
                    ctx.wait("s", "pe", pe_av[qc - 1])
                sev_cnt = {}
                ctx.wait("t", "act", prev_sev_last)
                for g in range(nk):
                    if g >= 4:
                        ctx.wait("t", "act", sev_cnt[g - 4])
                    pe_g = ctx.op("t", lambda t_, g=g, qc=qc: t_.matmul(
                        s.PA[:, (g % 4) * 512:((g % 4) + 1) * 512],
                        lhsT=s.sbB[:, SCORE_K + g * 128:SCORE_K + (g + 1) * 128],
                        rhs=s.sbB[:, qbase + qc * 512:qbase + (qc + 1) * 512],
                        start=True, stop=True), inc="pe")
                    ctx.wait("s", "pe", pe_g)
                    sev_cnt[g] = ctx.op("s", lambda e_, g=g: e_.activation(
                        s.wbuf[:, ATB + g * 512:ATB + (g + 1) * 512],
                        s.PA[:, (g % 4) * 512:((g % 4) + 1) * 512],
                        AF.Exp if score == "exp" else AF.Relu,
                        bias=0.0, scale=0.125), inc="act")
                prev_sev_last = sev_cnt[nk - 1]
                ctx.wait("v", "act", sev_cnt[nk - 1])
                for mm_ in range(4):
                    kt = 4 * qc + mm_
                    ctx.op("v", lambda v, kt=kt, mm_=mm_: v.tensor_tensor(
                        s.wbuf[:, ATB + kt * 512:ATB + (kt + 1) * 512],
                        s.wbuf[:, ATB + kt * 512:ATB + (kt + 1) * 512],
                        s.cc[:, mm_ * 512:(mm_ + 1) * 512], op=ALU.mult),
                        inc="dve")
                mask_dve = ctx.cnt["dve"]
                bank = s.PD if qc % 2 == 0 else s.PE_
                avbank[qc] = bank
                if nk > 4:
                    ctx.wait("t", "act", sev_cnt[nk - 5])

                    def av1(t_, nk=nk, von=von, bank=bank):
                        last = None
                        for kt in range(nk - 4):
                            last = t_.matmul(
                                bank[:, :],
                                lhsT=s.sbB[:, von + kt * 128:von + (kt + 1) * 128],
                                rhs=s.wbuf[:, ATB + kt * 512:ATB + (kt + 1) * 512],
                                start=(kt == 0), stop=False)
                        return last
                    ctx.op("t", av1)
                ctx.wait("t", "dve", mask_dve)

                def av2(t_, nk=nk, von=von, bank=bank):
                    last = None
                    for kt in range(max(0, nk - 4), nk):
                        last = t_.matmul(
                            bank[:, :],
                            lhsT=s.sbB[:, von + kt * 128:von + (kt + 1) * 128],
                            rhs=s.wbuf[:, ATB + kt * 512:ATB + (kt + 1) * 512],
                            start=(kt == 0), stop=(kt == nk - 1))
                    return last
                pe_av[qc] = ctx.op("t", av2, inc="pe")
            if qci >= 1:
                qc = qci - 1
                bank = avbank[qc]
                CT = CTXB + (qc % 2) * 1024
                sb_ = 512 + (qc % 2) * 8
                pb = s.PBb if qc % 2 == 0 else s.PCb
                ctx.wait("s", "pe", pe_av[qc])
                a_ctx = ctx.op("s", lambda a, CT=CT, bank=bank: a.copy(
                    s.sbB[:, CT:CT + 512], bank[:, :]), inc="act")
                if h == 0 and qc == 3:
                    self.snap(ctx, "ctx0", s.sbB[:, CT:CT + 512])
                if (qc % 2) in a_ct_par:
                    ctx.wait("t", "act", a_ct_par[qc % 2])
                ctx.wait("t", "act", a_ctx)
                pe_ct = ctx.op("t", lambda t_, CT=CT, pb=pb: [
                    t_.transpose(pb[:, u * 128:(u + 1) * 128],
                                 s.sbB[:, CT + u * 128:CT + (u + 1) * 128],
                                 s.cc[:, 2048:2176]) for u in range(4)][-1],
                    inc="pe")
                ctx.wait("s", "pe", pe_ct)
                a_ct = ctx.op("s", lambda a, CT=CT, pb=pb: a.copy(
                    s.sbB[:, CT + 512:CT + 1024], pb[0:128, 0:512]), inc="act")
                a_ct_par[qc % 2] = a_ct
                ctx.wait("v", "act", a_ct)
                rs_ap = s.sbB[:, CT + 512:CT + 1024].rearrange(
                    "p (u c) -> p u c", c=128)[:, :, 64]
                if relu_eps is not None:
                    ctx.op("v", lambda v, rs_ap=rs_ap, sb_=sb_: v.tensor_scalar(
                        s.sf[:, sb_ + 4:sb_ + 8], rs_ap, relu_eps, None,
                        op0=ALU.add), inc="dve")
                    ctx.wait("v", "dve")
                    ctx.op("v", lambda v, sb_=sb_: v.reciprocal(
                        s.sf[:, sb_:sb_ + 4], s.sf[:, sb_ + 4:sb_ + 8]),
                        inc="dve")
                else:
                    ctx.op("v", lambda v, rs_ap=rs_ap, sb_=sb_: v.reciprocal(
                        s.sf[:, sb_:sb_ + 4], rs_ap), inc="dve")
                ctx.wait("v", "dve")
                for u in range(4):
                    itile = qc * 4 + u
                    ctx.op("v", lambda v, u=u, itile=itile, h=h, CT=CT, sb_=sb_:
                           v.tensor_scalar(
                               O[:, itile * D + h * 64:itile * D + h * 64 + 64],
                               s.sbB[:, CT + 512 + u * 128:CT + 512 + u * 128 + 64],
                               s.sf[:, sb_ + u:sb_ + u + 1], None,
                               op0=ALU.mult), inc="dve")

    def emit_attn_std(self, ctx, wq, bias, kprime, proj):
        s = self
        ctx.wait("t", "qs", self.c_x_loaded)
        ctx.wait("t", "dve", self.c_dve_consts)
        self.transpose_to_xt(ctx, s.x_cur)
        if bias is not None:
            self.load_chunkcol(ctx, bias, s.sf[:, 0:24], 24, lst="q")
            ctx.wait("v", "qs")

        PADQ, PADK, PADKP, STQK, VON0, VON1, VT2, CTXB = (
            0, 2048, 4096, 6144, 8192, 10240, 12288, 14336)
        ATB = 6144

        ctx.op("v", lambda v: v.memset(s.sbB[:, 0:8192], 0.0), inc="dve")
        for h in range(2):
            von = (VON0, VON1)[h]
            ctx.op("v", lambda v, von=von: v.memset(s.sbB[:, von:von + 2048], 0.0),
                   inc="dve")
            for m in range(NT):
                ctx.op("v", lambda v, von=von, m=m: v.memset(
                    s.sbB[:, von + m * 128 + 64:von + m * 128 + 65], 1.0), inc="dve")
        pads_ready = ctx.cnt["dve"]

        def wsl(hp, m):
            o = (hp % 2) * 3072 + m * 1024
            return s.wbuf[:, o:o + 1024]

        O = s.sbA
        ctx.wait("t", "act")
        self.snap(ctx, "xt", s.xt[:, :])
        pair_pe_done = {}

        for hp in range(8):
            if hp >= 2:
                ctx.wait("g", "pe", pair_pe_done[hp - 2])
            for m in range(3):
                c0 = m * D + hp * 128
                ctx.dma("g", out=wsl(hp, m).rearrange("p (j n) -> p j n", n=128),
                        in_=wq[:, c0:c0 + 128].rearrange("(j p) n -> p j n", p=P))
            ctx.wait("t", "qg")

            if hp == 0:
                ctx.wait("s", "dve", pads_ready)
            ev_ring = {}
            for m in range(3):
                for tc in range(4):
                    gl = m * 4 + tc
                    if gl >= 4:
                        if bias is None:
                            ctx.wait("t", "act", ev_ring[gl - 4])
                        else:
                            ctx.wait("t", "dve", ev_ring[gl - 4])

                    def mmg(t_, hp=hp, m=m, tc=tc):
                        last = None
                        for j in range(DC):
                            last = t_.matmul(
                                s.PA[:, tc * 512:(tc + 1) * 512],
                                lhsT=wsl(hp, m)[:, j * 128:(j + 1) * 128],
                                rhs=s.xt[:, j * T + tc * 512:j * T + (tc + 1) * 512],
                                start=(j == 0), stop=(j == DC - 1))
                        return last
                    pe_t = ctx.op("t", mmg, inc="pe")
                    c0, c1 = tc * 512, (tc + 1) * 512
                    dsts = {0: [(s.sbB[0:64, PADQ + c0:PADQ + c1],
                                 s.PA[0:64, c0:c1], 0, 64),
                                (s.sbB[64:128, STQK + c0:STQK + c1],
                                 s.PA[64:128, c0:c1], 64, 64)],
                            1: [(s.sbB[0:64, PADK + c0:PADK + c1],
                                 s.PA[0:64, c0:c1], 0, 64),
                                (s.sbB[64:128, VT2 + c0:VT2 + c1],
                                 s.PA[64:128, c0:c1], 64, 64)],
                            2: [(s.wbuf[:, 14336 + c0:14336 + c1],
                                 s.PA[:, c0:c1], 0, 128)]}[m]
                    if bias is None:
                        ctx.wait("s", "pe", pe_t)
                        for dst, src, p0, pn in dsts:
                            ev_ring[gl] = ctx.op(
                                "s", lambda a, dst=dst, src=src: a.copy(dst, src),
                                inc="act")
                    else:
                        ctx.wait("v", "pe", pe_t)
                        for dst, src, p0, pn in dsts:
                            bcol = s.sf[p0:p0 + pn, m * 8 + hp:m * 8 + hp + 1]
                            ev_ring[gl] = ctx.op(
                                "v", lambda v, dst=dst, src=src, bcol=bcol:
                                v.tensor_scalar(dst, src, bcol, None, op0=ALU.add),
                                inc="dve")

            qkv_act_done = ctx.cnt["act"]
            if bias is None:
                ctx.wait("q", "act", ev_ring[11])
            else:
                ctx.wait("q", "dve", ev_ring[11])
            ctx.dma("q", out=s.sbB[0:64, STQK:STQK + 2048],
                    in_=s.sbB[64:128, STQK:STQK + 2048])
            h1q = ctx.dma("q", out=s.sbB[0:64, VT2:VT2 + 2048],
                          in_=s.sbB[64:128, VT2:VT2 + 2048])
            ctx.wait("v", "qs", h1q)
            ctx.op("v", lambda v: v.memset(s.sbB[64:128, STQK:STQK + 2048], 0.0))
            h1z = ctx.op("v", lambda v: v.memset(
                s.sbB[64:128, VT2:VT2 + 2048], 0.0), inc="dve")

            if hp == 0:
                ctx.wait("t", "dve", pads_ready)
            if bias is None:
                ctx.wait("t", "act", ev_ring[11])
            else:
                ctx.wait("t", "dve", ev_ring[11])

            def vmm(g):
                def f(t_, g=g):
                    bank = [s.PBb, s.PCb][g % 2]
                    last = None
                    for k in range(8):
                        mtile = g * 8 + k
                        last = t_.transpose(bank[:, k * 128:(k + 1) * 128],
                                            s.wbuf[:, 14336 + mtile * 128:
                                                   14336 + (mtile + 1) * 128],
                                            s.cc[:, 2048:2176])
                    return last
                return f

            def vev(g):
                def f(e_, g=g):
                    bank = [s.PBb, s.PCb][g % 2]
                    last = None
                    for k in range(8):
                        mtile = g * 8 + k
                        last = e_.copy(s.sbB[:, VON0 + mtile * 128:VON0 + mtile * 128 + 64],
                                       bank[:, k * 128:k * 128 + 64])
                        last = e_.copy(s.sbB[:, VON1 + mtile * 128:VON1 + mtile * 128 + 64],
                                       bank[:, k * 128 + 64:k * 128 + 128])
                    return last
                return f
            pipelined(ctx, 2, vmm, vev, "s")

            if hp == 0:
                self.snap(ctx, "padQ0", s.sbB[:, PADQ:PADQ + 2048])
                self.snap(ctx, "padK0", s.sbB[:, PADK:PADK + 2048])
                self.snap(ctx, "von0", s.sbB[:, VON0:VON0 + 2048])
            for hh in range(2):
                h = 2 * hp + hh
                if hh == 1:
                    ctx.wait("t", "dve", h1z)
                self.attn_head_core(ctx, h, hh, kprime=kprime, score="exp", O=O,
                                    qbase=PADQ if hh == 0 else STQK,
                                    kbase=PADK if hh == 0 else VT2)
            pair_pe_done[hp] = ctx.cnt["pe"]

        self.barrier(ctx)
        self.snap(ctx, "O", O[:, :])
        if proj is not None:
            wp, bp = proj
            ctx.wait("t", "dve")
            pairs = []
            for i in range(NT):
                for j in range(DC):
                    pairs.append((O[:, i * D + j * 128:i * D + (j + 1) * 128],
                                  s.xt[:, j * T + i * 128:j * T + i * 128 + 128]))
            self.bf_transpose_batch(ctx, pairs, depth=2)
            ctx.wait("g", "pe")
            ctx.dma("g", out=s.wbuf[:, 0:8192].rearrange("p (j n) -> p j n", n=1024),
                    in_=wp.rearrange("(j p) n -> p j n", p=P))
            self.load_bcast(ctx, bp, s.cb[:, 0:1024], D, lst="g")
            ctx.wait("t", "qg")
            ctx.wait("t", "act")
            ctx.wait("v", "qg")

            def pmm(g):
                def f(t_, g=g):
                    tt, nh = g // 2, g % 2
                    last = None
                    for j in range(8):
                        last = t_.matmul(
                            s.PA[:, (g % 4) * 512:((g % 4) + 1) * 512],
                            lhsT=s.xt[:, j * T + tt * 128:j * T + (tt + 1) * 128],
                            rhs=s.wbuf[:, j * 1024 + nh * 512:j * 1024 + (nh + 1) * 512],
                            start=(j == 0), stop=(j == 7))
                    return last
                return f

            def pev(g):
                def f(v_, g=g):
                    tt, nh = g // 2, g % 2
                    return v_.scalar_tensor_tensor(
                        O[:, tt * D + nh * 512:tt * D + (nh + 1) * 512],
                        s.PA[:, (g % 4) * 512:((g % 4) + 1) * 512], 1.0,
                        s.cb[:, nh * 512:(nh + 1) * 512], op0=ALU.mult, op1=ALU.add)
                return f
            pipelined(ctx, 32, pmm, pev, "v", depth=4)
            self.snap(ctx, "Oproj", O[:, :])

        self.barrier(ctx)
        ctx.wait("v", "act")
        for i in range(NT):
            ctx.op("v", lambda v, i=i: v.tensor_tensor(
                s.x_cur[:, i * D:(i + 1) * D], s.x_cur[:, i * D:(i + 1) * D],
                O[:, i * D:(i + 1) * D], op=ALU.add), inc="dve")
        self.ln_batch(ctx, [(s.x_cur[:, i * D:(i + 1) * D],
                             s.x_cur[:, i * D:(i + 1) * D]) for i in range(NT)],
                      sq_scratch=s.sbB[:, 0:D])

    def emit_L0_ffn(self, ctx):
        s = self
        inp = self.inp
        import math

        self.barrier(ctx)
        NTAY = 12
        gens = inp["L0_ffn_gens"]

        def mat_dma_in(lst, dram_mat, dst_off, buf=None):
            buf = buf if buf is not None else s.wbuf
            ctx.wait(lst, "pe")
            ctx.wait(lst, "qs")
            ctx.wait(lst, "qg")
            ctx.dma(lst, out=buf[:, dst_off:dst_off + 8192]
                    .rearrange("p (j n) -> p j n", n=1024),
                    in_=dram_mat.rearrange("(j p) n -> p j n", p=P))

        def matmul_1024(ctx, lhsT_buf, lhsT_off, rhs_buf, rhs_off, ev_emit):
            def mm(g):
                def f(t_, g=g):
                    last = None
                    po = (g % 2) * 1024
                    for nh in range(2):
                        for j in range(8):
                            last = t_.matmul(
                                s.PA[:, po + nh * 512:po + (nh + 1) * 512],
                                lhsT=lhsT_buf[:, lhsT_off + j * 1024 + g * 128:
                                              lhsT_off + j * 1024 + (g + 1) * 128],
                                rhs=rhs_buf[:, rhs_off + j * 1024 + nh * 512:
                                            rhs_off + j * 1024 + (nh + 1) * 512],
                                start=(j == 0), stop=(j == 7))
                    return last
                return f

            def ev(g):
                def f(e_, g=g):
                    return ev_emit(e_, g, (g % 2) * 1024)
                return f
            pipelined(ctx, 8, mm, ev, "s", depth=2)

        for gi in range(4):
            mat_dma_in("g", gens[gi], 0, buf=s.xt)
            ctx.wait("t", "qg")
            ctx.wait("v", "qg")
            ctx.op("v", lambda v: v.memset(s.sbA[:, 0:8192], 0.0), inc="dve")
            for c in range(8):
                ctx.op("v", lambda v, c=c: v.tensor_copy(
                    s.sbA[:, c * 1024 + c * 128:c * 1024 + c * 128 + 128],
                    s.cc[:, 2048:2176]))
            for c in range(8):
                ctx.op("v", lambda v, c=c: v.scalar_tensor_tensor(
                    s.sbA[:, c * 1024:(c + 1) * 1024],
                    s.xt[:, c * 1024:(c + 1) * 1024], 1.0,
                    s.sbA[:, c * 1024:(c + 1) * 1024],
                    op0=ALU.mult, op1=ALU.add), inc="dve")
            ctx.wait("t", "dve")
            prev_buf, prev_off = s.xt, 0
            cur_half = 0
            for k in range(2, NTAY + 1):
                dst_off = cur_half * 8192
                sk = (1.0 if (k % 2 == 1) else -1.0) / math.factorial(k)

                def ev_emit(e_, m, po, dst_off=dst_off):
                    return e_.copy(s.wbuf[:, dst_off + m * 1024:dst_off + (m + 1) * 1024],
                                   s.PA[:, po:po + 1024])
                matmul_1024(ctx, s.xt, 0, prev_buf, prev_off, ev_emit)
                ctx.wait("v", "act")
                for c in range(8):
                    ctx.op("v", lambda v, c=c, sk=sk, dst_off=dst_off:
                           v.scalar_tensor_tensor(
                               s.sbA[:, c * 1024:(c + 1) * 1024],
                               s.wbuf[:, dst_off + c * 1024:dst_off + (c + 1) * 1024],
                               sk, s.sbA[:, c * 1024:(c + 1) * 1024],
                               op0=ALU.mult, op1=ALU.add), inc="dve")
                prev_buf, prev_off = s.wbuf, dst_off
                cur_half ^= 1
                ctx.wait("t", "dve")
            ctx.wait("q", "dve")
            ctx.dma("q", out=self.dscr_w[0:1024, gi * 1024:(gi + 1) * 1024]
                    .rearrange("(j p) n -> p j n", p=P),
                    in_=s.sbA[:, 0:8192].rearrange("p (j n) -> p j n", n=1024))
            ctx.wait("t", "qs")
            ctx.wait("v", "qs")

        def prod(left_from_dram, left_dram_col, right_col, out_is_R):
            ctx.wait("t", "qg")
            pairs = []
            for j in range(8):
                for m in range(8):
                    pairs.append((s.wbuf[:, j * 1024 + m * 128:j * 1024 + (m + 1) * 128],
                                  s.wbuf[:, 8192 + m * 1024 + j * 128:
                                         8192 + m * 1024 + (j + 1) * 128]))
            self.bf_transpose_batch(ctx, pairs)
            mat_dma_in("g", self.dscr_w[0:1024, right_col:right_col + 1024], 0, buf=s.xt)
            ctx.wait("t", "qg")
            ctx.wait("t", "act")

            def ev_emit(e_, m, po):
                return e_.copy(s.sbA[:, m * 1024:(m + 1) * 1024], s.PA[:, po:po + 1024])
            matmul_1024(ctx, s.wbuf, 8192, s.xt, 0, ev_emit)
            ctx.wait("q", "act")
            row0 = 1024 if out_is_R else 2048
            ctx.dma("q", out=self.dscr_w[row0:row0 + 1024, 0:1024]
                    .rearrange("(j p) n -> p j n", p=P),
                    in_=s.sbA[:, 0:8192].rearrange("p (j n) -> p j n", n=1024))
            ctx.wait("t", "qs")
            ctx.wait("g", "qs")

        mat_dma_in("g", self.dscr_w[0:1024, 0:1024], 0)
        prod(False, 0, 1024, False)
        mat_dma_in("g", self.dscr_w[2048:3072, 0:1024], 0)
        prod(False, 0, 2048, False)
        mat_dma_in("g", self.dscr_w[2048:3072, 0:1024], 0)
        prod(False, 0, 3072, True)
        self.snap(ctx, "R", s.sbA[:, 0:8192])

        self.barrier(ctx)
        self.emit_gained_ln_to_xt(ctx, inp["L0_ffn_ln_g"], inp["L0_ffn_ln_b"])
        ctx.wait("t", "act")
        self.snap(ctx, "xnT", s.xt[:, :])

        self.barrier(ctx)
        ctx.dma("g", out=s.wbuf[:, 0:48].rearrange("p (j c) -> p j c", c=6),
                in_=inp["L0_ffn_gate_w"].rearrange("(j p) c -> p j c", p=P))
        self.load_bcast(ctx, inp["L0_ffn_gate_b"], s.sf[:, 256:262], 6)
        ctx.wait("t", "qg")
        ctx.wait("v", "qs")

        def gmm2(g):
            def f(t_, g=g):
                bank = [s.PD, s.PE_][g % 2]
                last = None
                for j in range(8):
                    last = t_.matmul(bank[:, 0:6],
                                     lhsT=s.xt[:, j * T + g * 128:j * T + (g + 1) * 128],
                                     rhs=s.wbuf[:, j * 6:(j + 1) * 6],
                                     start=(j == 0), stop=(j == 7))
                return last
            return f

        ctx.op("v", lambda v: v.memset(s.sf[:, 0:128], -1e30), inc="dve")
        ctx.wait("v", "dve")

        def gev(g):
            def f(e_, g=g):
                bank = [s.PD, s.PE_][g % 2]
                return e_.tensor_tensor(s.sf[:, g * 8:g * 8 + 6], bank[:, 0:6],
                                        s.sf[:, 256:262], op=ALU.add)
            return f
        pipelined(ctx, NT, gmm2, gev, "v", depth=2)
        self.snap(ctx, "lgL0", s.sf[:, 0:128])
        self.emit_topk_softmax(ctx, nexp=6, logit_off=0, wfull_off=128, kk=2)
        self.snap(ctx, "wfullL0", s.sf[:, 128:256])

        self.barrier(ctx)
        ctx.wait("v", "act")
        ctx.op("v", lambda v: v.memset(s.sbA[:], 0.0), inc="dve")
        ctx.wait("t", "dve")
        units = [(e, hh2) for e in range(6) for hh2 in range(2)]

        def px_w1(u):
            e, hh2 = units[u]
            return ctx.dma("g", out=s.wbuf[:, 0:8192]
                           .rearrange("p (j nt c) -> p j nt c", nt=8, c=128),
                           in_=inp["L0_ffn_ew1"][e, :, hh2 * 1024:(hh2 + 1) * 1024]
                           .rearrange("(j p) (nt c) -> p j nt c", p=P, c=128))

        def px_w2(u):
            e, hh2 = units[u]
            return ctx.dma("g", out=s.wbuf[:, 8192:16384]
                           .rearrange("p (nt c) -> p nt c", c=1024),
                           in_=inp["L0_ffn_ew2"][e, hh2 * 1024:(hh2 + 1) * 1024, :]
                           .rearrange("(nt p) c -> p nt c", p=P))

        def px_b(u):
            e, hh2 = units[u]
            self.load_chunkcol(ctx, inp["L0_ffn_eb1"][e, hh2 * 1024:(hh2 + 1) * 1024],
                               s.sf[:, (u % 2) * 8:(u % 2) * 8 + 8], 8, lst="g")
            if hh2 == 0:
                ctx.wait("g", "dve")
                self.load_bcast(ctx, inp["L0_ffn_eb2"][e], s.cb[:, 0:1024], D,
                                lst="g")
            return ctx.cnt["qg"]

        ctx.wait("g", "pe")
        ctx.wait("g", "dve")
        c1 = px_w1(0)
        cb_ = px_b(0)
        c2 = px_w2(0)
        for u, (e, hh2) in enumerate(units):
            c1u, c2u, cbu = c1, c2, cb_
            ctx.wait("t", "qg", c1u)
            ctx.wait("s", "qg", cbu)

            def amm(g):
                def f(t_, g=g):
                    tc, gg = g // 8, g % 8
                    bank = [s.PD, s.PE_][g % 2]
                    last = None
                    for j in range(8):
                        last = t_.matmul(
                            bank[:, :],
                            lhsT=s.wbuf[:, j * 1024 + gg * 128:j * 1024 + (gg + 1) * 128],
                            rhs=s.xt[:, j * T + tc * 512:j * T + (tc + 1) * 512],
                            start=(j == 0), stop=(j == 7))
                    return last
                return f

            def aev(g):
                def f(e_, g=g, u=u):
                    tc, gg = g // 8, g % 8
                    bank = [s.PD, s.PE_][g % 2]
                    return e_.activation(s.sbB[:, tc * 4096 + gg * 512:
                                               tc * 4096 + (gg + 1) * 512],
                                         bank[:, :], AF.Gelu,
                                         bias=s.sf[:, (u % 2) * 8 + gg:
                                                   (u % 2) * 8 + gg + 1], scale=1.0)
                return f
            pipelined(ctx, 32, amm, aev, "s", depth=2)
            if u + 1 < 12:
                ctx.wait("g", "pe")
                c1 = px_w1(u + 1)
                cb_ = px_b(u + 1)
            ctx.wait("t", "act")
            ctx.wait("t", "qg", c2u)
            ctx.wait("v", "qg", cbu)

            def bmm(g):
                def f(t_, g=g):
                    tc, gg = g // 8, g % 8
                    last = None
                    for nt in range(8):
                        last = t_.matmul(
                            s.PA[:, (g % 4) * 512:((g % 4) + 1) * 512],
                            lhsT=s.sbB[:, tc * 4096 + nt * 512 + (gg // 2) * 128:
                                       tc * 4096 + nt * 512 + (gg // 2) * 128 + 128],
                            rhs=s.wbuf[:, 8192 + nt * 1024 + (gg % 2) * 512:
                                       8192 + nt * 1024 + (gg % 2) * 512 + 512],
                            start=(nt == 0), stop=(nt == 7))
                    return last
                return f

            def bev(g):
                def f(v_, g=g, e=e, hh2=hh2):
                    tc, gg = g // 8, g % 8
                    tt = tc * 4 + gg // 2
                    nh = gg % 2
                    wcol = s.sf[:, 128 + tt * 8 + e:128 + tt * 8 + e + 1]
                    dst = s.sbA[:, tt * 1024 + nh * 512:tt * 1024 + (nh + 1) * 512]
                    last = v_.scalar_tensor_tensor(
                        dst, s.PA[:, (g % 4) * 512:((g % 4) + 1) * 512],
                        wcol, dst, op0=ALU.mult, op1=ALU.add)
                    if hh2 == 0:
                        last = v_.scalar_tensor_tensor(
                            dst, s.cb[:, nh * 512:(nh + 1) * 512], wcol, dst,
                            op0=ALU.mult, op1=ALU.add)
                    return last
                return f
            pipelined(ctx, 32, bmm, bev, "v", depth=4)
            if u + 1 < 12:
                ctx.wait("g", "pe")
                c2 = px_w2(u + 1)
        self.snap(ctx, "nonabel", s.sbA[:, :])

        self.barrier(ctx)
        pairs = []
        for i in range(NT):
            for j in range(DC):
                pairs.append((s.sbA[:, i * 1024 + j * 128:i * 1024 + (j + 1) * 128],
                              s.sbB[:, j * T + i * 128:j * T + i * 128 + 128]))
        self.bf_transpose_batch(ctx, pairs, depth=2)
        ctx.wait("g", "pe")
        mat_dma_in("g", self.dscr_w[1024:2048, 0:1024], 0)
        ctx.wait("t", "qg")
        ctx.wait("t", "act")

        def rmm(g):
            def f(t_, g=g):
                tt, nh = g // 2, g % 2
                last = None
                for j in range(8):
                    last = t_.matmul(
                        s.PA[:, (g % 4) * 512:((g % 4) + 1) * 512],
                        lhsT=s.sbB[:, j * T + tt * 128:j * T + (tt + 1) * 128],
                        rhs=s.wbuf[:, j * 1024 + nh * 512:j * 1024 + (nh + 1) * 512],
                        start=(j == 0), stop=(j == 7))
                return last
            return f
        rot_seq = []

        def rev(g):
            def f(e_, g=g):
                slot = 8192 + (g % 2) * 1024
                tt, nh = g // 2, g % 2
                return e_.copy(s.wbuf[:, slot + nh * 0:slot + 512]
                               if False else s.wbuf[:, 8192 + (g % 4) * 512:
                                                    8192 + (g % 4) * 512 + 512],
                               s.PA[:, (g % 4) * 512:((g % 4) + 1) * 512])
            return f
        def rot_after(g, act_cnt):
            tt, nh = g // 2, g % 2
            ctx.wait("q", "act", act_cnt)
            ctx.dma("q", out=self.dscr_b[tt * 128:(tt + 1) * 128,
                                         nh * 512:(nh + 1) * 512],
                    in_=s.wbuf[:, 8192 + (g % 4) * 512:8192 + (g % 4) * 512 + 512])
        ev_done = []
        ctx.wait("t", "act")
        for g in range(32):
            if g >= 4:
                ctx.wait("t", "act", ev_done[g - 4])
                ctx.wait("s", "qs", self._rot_q[g - 4])
            pe_t = ctx.op("t", rmm(g), inc="pe")
            ctx.wait("s", "pe", pe_t)
            a_t = ctx.op("s", rev(g), inc="act")
            ev_done.append(a_t)
            rot_after(g, a_t)
            if not hasattr(self, "_rot_q"):
                self._rot_q = []
            self._rot_q.append(ctx.cnt["qs"])
        self._rot_q = []

        self.barrier(ctx)
        ctx.wait("g", "qs")
        ctx.dma("g", out=self.dscr_a[0:4096, :], in_=inp["L0_ffn_aw2"])
        self.load_chunkcol(ctx, inp["L0_ffn_ab1"], s.sf[:, 0:32], 32, lst="g")
        self.load_bcast(ctx, inp["L0_ffn_ab2"], s.cb[:, 0:1024], D, lst="g")
        aw2_ready = ctx.cnt["qg"]
        aw1 = inp["L0_ffn_aw1"]
        a1_loaded = {}
        for tc in range(4):
            for ng in range(8):
                slot = (ng % 2) * 4096
                if tc == 0 or True:
                    if ng >= 2 or tc > 0:
                        ctx.wait("g", "pe")
                    ctx.dma("g", out=s.wbuf[:, slot:slot + 4096]
                            .rearrange("p (j nt c) -> p j nt c", nt=4, c=128),
                            in_=aw1[:, ng * 512:(ng + 1) * 512]
                            .rearrange("(j p) (nt c) -> p j nt c", p=P, c=128))
                ctx.wait("t", "qg")

                def amm(g):
                    def f(t_, g=g, tc=tc, slot=slot):
                        bank = [s.PD, s.PE_][g % 2]
                        last = None
                        for j in range(8):
                            last = t_.matmul(
                                bank[:, :],
                                lhsT=s.wbuf[:, slot + j * 512 + g * 128:
                                            slot + j * 512 + (g + 1) * 128],
                                rhs=s.xt[:, j * T + tc * 512:j * T + (tc + 1) * 512],
                                start=(j == 0), stop=(j == 7))
                        return last
                    return f

                def aev(g):
                    def f(e_, g=g, ng=ng):
                        bank = [s.PD, s.PE_][g % 2]
                        nt = ng * 4 + g
                        return e_.activation(s.sbB[:, nt * 512:(nt + 1) * 512],
                                             bank[:, :], AF.Gelu,
                                             bias=s.sf[:, nt:nt + 1], scale=1.0)
                    return f
                pipelined(ctx, 4, amm, aev, "s", depth=2)
            ctx.wait("t", "act")
            for tth in range(2):
                for ntc in range(4):
                    ctx.wait("g", "pe")
                    if tth == 0 and tc == 0 and ntc == 0:
                        ctx.wait("g", "qg", aw2_ready)
                    ctx.dma("g", out=s.wbuf[:, 8192:16384]
                            .rearrange("p (nt c) -> p nt c", c=1024),
                            in_=self.dscr_a[ntc * 1024:(ntc + 1) * 1024, :]
                            .rearrange("(nt p) c -> p nt c", p=P))
                    ctx.wait("t", "qg")

                    def bmm(t_, tc=tc, tth=tth, ntc=ntc):
                        last = None
                        for grp in range(4):
                            ttr = tth * 2 + grp // 2
                            nh = grp % 2
                            for ni in range(8):
                                nt = ntc * 8 + ni
                                last = t_.matmul(
                                    s.PA[:, grp * 512:(grp + 1) * 512],
                                    lhsT=s.sbB[:, nt * 512 + ttr * 128:
                                               nt * 512 + ttr * 128 + 128],
                                    rhs=s.wbuf[:, 8192 + ni * 1024 + nh * 512:
                                               8192 + ni * 1024 + nh * 512 + 512],
                                    start=(nt == 0), stop=(nt == 31))
                        return last
                    pe_b = ctx.op("t", bmm, inc="pe")
                ctx.wait("v", "pe", pe_b)
                for grp in range(4):
                    ttr = tth * 2 + grp // 2
                    tt = tc * 4 + ttr
                    nh = grp % 2
                    ctx.op("v", lambda v, grp=grp, tt=tt, nh=nh:
                           v.scalar_tensor_tensor(
                               s.sbA[:, tt * 1024 + nh * 512:tt * 1024 + (nh + 1) * 512],
                               s.PA[:, grp * 512:(grp + 1) * 512], 1.0,
                               s.cb[:, nh * 512:(nh + 1) * 512],
                               op0=ALU.mult, op1=ALU.add), inc="dve")
                ctx.wait("t", "dve")
        self.snap(ctx, "abel", s.sbA[:, :])

        self.barrier(ctx)
        gcw = inp["L0_ffn_gc_w"]
        ctx.wait("g", "dve")
        ctx.dma("g", out=s.sbB[:, 0:1024],
                in_=gcw[0:1024, 0:1].rearrange("n 1 -> 1 n").broadcast_to([P, 1024]))
        ctx.dma("g", out=s.sbB[:, 1024:2048],
                in_=gcw[1024:2048, 0:1].rearrange("n 1 -> 1 n").broadcast_to([P, 1024]))
        self.load_bcast(ctx, inp["L0_ffn_gc_b"], s.sf[:, 893:894], 1, lst="g")
        ctx.wait("v", "qg")
        ctx.wait("q", "qg")
        ctx.wait("q", "qs")
        ctx.wait("q", "pe")

        def rot_load(i):
            slot = 2048 + (i % 4) * 1024
            return ctx.dma("q", out=s.sbB[:, slot:slot + 1024],
                           in_=self.dscr_b[i * 128:(i + 1) * 128, :]), slot

        ql, dva = {}, {}
        for i in range(3):
            ql[i] = rot_load(i)
        for i in range(NT):
            if i + 3 < NT:
                if i >= 1:
                    ctx.wait("q", "dve", dva[i - 1])
                ql[i + 3] = rot_load(i + 3)
            ctx.wait("v", "qs", ql[i][0])
            ctx.op("v", lambda v, i=i: v.scalar_tensor_tensor(
                s.sbB[:, 6144:7168], s.sbA[:, i * 1024:(i + 1) * 1024], 1.0,
                s.sbB[:, 0:1024], op0=ALU.mult, op1=ALU.mult,
                accum_out=s.sf[:, 560 + i:561 + i]), inc="dve")
            ctx.op("v", lambda v, i=i, slot=ql[i][1]: v.scalar_tensor_tensor(
                s.sbB[:, 6144:7168], s.sbB[:, slot:slot + 1024], 1.0,
                s.sbB[:, 1024:2048], op0=ALU.mult, op1=ALU.mult,
                accum_out=s.sf[:, 576 + i:577 + i]), inc="dve")
            dva[i] = ctx.cnt["dve"]
        ctx.wait("v", "dve")
        d1 = ctx.op("v", lambda v: v.tensor_tensor(
            s.sf[:, 560:576], s.sf[:, 560:576], s.sf[:, 576:592], op=ALU.add),
            inc="dve")
        ctx.wait("s", "dve", d1)
        a1 = ctx.op("s", lambda a: a.activation(
            s.sf[:, 576:592], s.sf[:, 560:576], AF.Sigmoid,
            bias=s.sf[:, 893:894], scale=1.0), inc="act")
        ctx.wait("v", "act", a1)
        d2 = ctx.op("v", lambda v: v.tensor_scalar(
            s.sf[:, 560:576], s.sf[:, 576:592], -1.0, 1.0,
            op0=ALU.mult, op1=ALU.add), inc="dve")
        ctx.wait("v", "dve", d2)
        ctx.wait("q", "dve")
        ql2, dv2 = {}, {}
        for i in range(3):
            ql2[i] = rot_load(i)
        for i in range(NT):
            if i + 3 < NT:
                if i >= 1:
                    ctx.wait("q", "dve", dv2[i - 1])
                ql2[i + 3] = rot_load(i + 3)
            ctx.wait("v", "qs", ql2[i][0])
            ctx.op("v", lambda v, i=i, slot=ql2[i][1]: v.tensor_scalar(
                s.cb[:, 0:1024], s.sbB[:, slot:slot + 1024],
                s.sf[:, 576 + i:577 + i], None, op0=ALU.mult))
            ctx.op("v", lambda v, i=i: v.scalar_tensor_tensor(
                s.cb[:, 0:1024], s.sbA[:, i * 1024:(i + 1) * 1024],
                s.sf[:, 560 + i:561 + i], s.cb[:, 0:1024],
                op0=ALU.mult, op1=ALU.add))
            ctx.op("v", lambda v, i=i: v.scalar_tensor_tensor(
                s.x_cur[:, i * D:(i + 1) * D], s.x_cur[:, i * D:(i + 1) * D],
                2.0, s.cb[:, 0:1024], op0=ALU.mult, op1=ALU.add), inc="dve")
            dv2[i] = ctx.cnt["dve"]
        self.ln_batch(ctx, [(s.x_cur[:, i * D:(i + 1) * D],
                             s.x_cur[:, i * D:(i + 1) * D]) for i in range(NT)],
                      sq_scratch=s.sbB[:, 6144:7168])

    def emit_topk_softmax(self, ctx, nexp, logit_off, wfull_off, kk,
                          stride=None, slot_store=None):
        assert kk == 2
        s = self
        ne = stride if stride is not None else max(8, nexp)
        SB = s.sbB.bitcast(F32)
        oh1 = lambda i: SB[:, 2048 + i * 8:2048 + i * 8 + 8]
        cum = lambda i: SB[:, 2176 + i * 8:2176 + i * 8 + 8]
        oh1f = lambda i: SB[:, 2304 + i * 8:2304 + i * 8 + 8]
        l2 = lambda i: SB[:, 2432 + i * 8:2432 + i * 8 + 8]
        oh2 = lambda i: SB[:, 2560 + i * 8:2560 + i * 8 + 8]
        oh2f = lambda i: SB[:, 2688 + i * 8:2688 + i * 8 + 8]
        m1 = lambda i: SB[:, 2816 + i:2817 + i]
        m2 = lambda i: SB[:, 2832 + i:2833 + i]
        lg = lambda i: s.sf[:, logit_off + i * ne:logit_off + i * ne + 8]
        assert ne >= 8 or nexp <= 8

        def rnd(emits):
            for e in emits:
                ctx.op("v", e, inc="dve")
            ctx.wait("v", "dve")
        rnd([lambda v, a=m1(i), b=lg(i): v.reduce_max(a, b, axis=AX.X)
             for i in range(NT)])
        rnd([lambda v, a=oh1(i), b=lg(i), c=m1(i): v.tensor_scalar(
            a, b, c, None, op0=ALU.is_equal) for i in range(NT)])
        rnd([lambda v, a=cum(i), b=oh1(i): v.tensor_tensor_scan(
            a, b, b, 0.0, op0=ALU.add, op1=ALU.bypass) for i in range(NT)])
        rnd([lambda v, a=cum(i): v.tensor_scalar(a, a, 1.0, None,
                                                 op0=ALU.is_equal)
             for i in range(NT)])
        rnd([lambda v, a=oh1f(i), b=oh1(i), c=cum(i): v.tensor_tensor(
            a, b, c, op=ALU.mult) for i in range(NT)])
        rnd([lambda v, a=l2(i), b=oh1f(i), c=lg(i): v.scalar_tensor_tensor(
            a, b, -1e30, c, op0=ALU.mult, op1=ALU.add) for i in range(NT)])
        rnd([lambda v, a=m2(i), b=l2(i): v.reduce_max(a, b, axis=AX.X)
             for i in range(NT)])
        rnd([lambda v, a=oh2(i), b=l2(i), c=m2(i): v.tensor_scalar(
            a, b, c, None, op0=ALU.is_equal) for i in range(NT)])
        rnd([lambda v, a=cum(i), b=oh2(i): v.tensor_tensor_scan(
            a, b, b, 0.0, op0=ALU.add, op1=ALU.bypass) for i in range(NT)])
        rnd([lambda v, a=cum(i): v.tensor_scalar(a, a, 1.0, None,
                                                 op0=ALU.is_equal)
             for i in range(NT)])
        rnd([lambda v, a=oh2f(i), b=oh2(i), c=cum(i): v.tensor_tensor(
            a, b, c, op=ALU.mult) for i in range(NT)])
        d = ctx.op("v", lambda v: v.tensor_tensor(
            SB[:, 2848:2864], SB[:, 2832:2848], SB[:, 2816:2832],
            op=ALU.subtract), inc="dve")
        ctx.wait("s", "dve", d)
        a_ = ctx.op("s", lambda a2: a2.activation(
            SB[:, 2896:2912], SB[:, 2848:2864], AF.Exp), inc="act")
        ctx.wait("v", "act", a_)
        ctx.op("v", lambda v: v.tensor_scalar(
            SB[:, 2864:2880], SB[:, 2896:2912], 1.0, None, op0=ALU.add),
            inc="dve")
        ctx.wait("v", "dve")
        ctx.op("v", lambda v: v.reciprocal(SB[:, 2864:2880], SB[:, 2864:2880]),
               inc="dve")
        ctx.wait("v", "dve")
        ctx.op("v", lambda v: v.tensor_tensor(
            SB[:, 2880:2896], SB[:, 2896:2912], SB[:, 2864:2880], op=ALU.mult),
            inc="dve")
        ctx.wait("v", "dve")
        w1 = lambda i: SB[:, 2864 + i:2865 + i]
        w2 = lambda i: SB[:, 2880 + i:2881 + i]
        if slot_store is None:
            for i in range(NT):
                wf = s.sf[:, wfull_off + i * ne:wfull_off + i * ne + 8]
                ctx.op("v", lambda v, a=wf, b=oh1f(i), c=w1(i): v.tensor_scalar(
                    a, b, c, None, op0=ALU.mult), inc="dve")
                ctx.op("v", lambda v, a=wf, b=oh2f(i), c=w2(i):
                       v.scalar_tensor_tensor(a, b, c, a, op0=ALU.mult,
                                              op1=ALU.add), inc="dve")
            ctx.wait("v", "dve")
        else:
            oh_buf, ws0, ws1, es0, es1 = slot_store
            for i in range(NT):
                for si, (ohf, wv, wsb, esb) in enumerate(
                        ((oh1f(i), w1(i), ws0, es0), (oh2f(i), w2(i), ws1, es1))):
                    tmp8 = SB[:, 2912 + si * 128 + i * 8:2912 + si * 128 + i * 8 + 8]
                    ctx.op("v", lambda v,
                           d=oh_buf[:, (si * NT + i) * 8:(si * NT + i) * 8 + 8],
                           b=ohf: v.tensor_copy(d, b), inc="dve")
                    ctx.op("v", lambda v, d=s.sf[:, wsb + i:wsb + i + 1], b=wv:
                           v.tensor_copy(d, b), inc="dve")
                    ctx.op("v", lambda v, a=tmp8, b=ohf: v.tensor_tensor(
                        a, b, s.sf[:, 264:272], op=ALU.mult), inc="dve")
            ctx.wait("v", "dve")
            for i in range(NT):
                for si, esb in enumerate((es0, es1)):
                    tmp8 = SB[:, 2912 + si * 128 + i * 8:2912 + si * 128 + i * 8 + 8]
                    ctx.op("v", lambda v, d=s.sf[:, esb + i:esb + i + 1], a=tmp8:
                           v.reduce_sum(d, a, axis=AX.X), inc="dve")
            ctx.wait("v", "dve")

    def emit_logits_hl(self, ctx, w_ap, nexp, bias_off, out_stride=8,
                       pad_neg=True):
        s = self
        ne8 = out_stride
        nw = 8 * nexp
        ctx.wait("q", "dve")
        ctx.dma("q", out=s.sf[:, 0:nw].rearrange("p (j c) -> p j c", c=nexp),
                in_=w_ap.rearrange("(j p) c -> p j c", p=P))
        ctx.wait("g", "pe")
        ctx.dma("g", out=s.wbuf[:, 0:nw].rearrange("p (j c) -> p j c", c=nexp),
                in_=w_ap.rearrange("(j p) c -> p j c", p=P))
        ctx.wait("v", "qs")
        ctx.wait("v", "qg")
        ctx.op("v", lambda v: v.tensor_tensor(
            s.wbuf[:, nw:2 * nw], s.sf[:, 0:nw], s.wbuf[:, 0:nw],
            op=ALU.subtract), inc="dve")
        ctx.wait("v", "dve")
        if pad_neg:
            ctx.op("v", lambda v: v.memset(s.sf[:, 0:NT * ne8], -1e30), inc="dve")
        ctx.wait("v", "dve")
        ctx.wait("t", "dve")
        XH, XL, XLT = 9216, 10240, 11264
        for i in range(NT):
            ctx.wait("v", "act")
            ctx.op("v", lambda v, i=i: v.tensor_copy(
                s.sbB[:, XH:XH + 1024], s.x_cur[:, i * D:(i + 1) * D]))
            d1 = ctx.op("v", lambda v, i=i: v.tensor_tensor(
                s.sbB[:, XL:XL + 1024], s.x_cur[:, i * D:(i + 1) * D],
                s.sbB[:, XH:XH + 1024], op=ALU.subtract), inc="dve")
            ctx.wait("t", "dve", d1)
            pairs = [(s.sbB[:, XL + j * 128:XL + (j + 1) * 128],
                      s.sbB[:, XLT + j * 128:XLT + (j + 1) * 128])
                     for j in range(DC)]
            self.bf_transpose_batch(ctx, pairs, depth=2)
            ctx.wait("t", "act")

            def lmm(t_, i=i):
                last = None
                k = 0
                for (lh, rh) in ((0, 0), (0, nw), (1, 0)):
                    for j in range(DC):
                        lhsT = (s.xt[:, j * T + i * 128:j * T + (i + 1) * 128]
                                if lh == 0 else
                                s.sbB[:, XLT + j * 128:XLT + (j + 1) * 128])
                        last = t_.matmul(
                            s.PD[:, 0:nexp], lhsT=lhsT,
                            rhs=s.wbuf[:, rh + j * nexp:rh + (j + 1) * nexp],
                            start=(k == 0), stop=(k == 3 * DC - 1))
                        k += 1
                return last
            pe_t = ctx.op("t", lmm, inc="pe")
            ctx.wait("v", "pe", pe_t)
            if bias_off is not None:
                d2 = ctx.op("v", lambda v, i=i: v.tensor_tensor(
                    s.sf[:, i * ne8:i * ne8 + nexp], s.PD[:, 0:nexp],
                    s.sf[:, bias_off:bias_off + nexp], op=ALU.add), inc="dve")
            else:
                d2 = ctx.op("v", lambda v, i=i: v.tensor_copy(
                    s.sf[:, i * ne8:i * ne8 + nexp], s.PD[:, 0:nexp]), inc="dve")
            ctx.wait("t", "dve", d2)

    def emit_gained_ln_to_xt(self, ctx, g_ap, b_ap):
        s = self
        ctx.wait("v", "qs", self.c_x_loaded)
        ctx.wait("s", "qs", self.c_x_loaded)
        ctx.wait("g", "dve")
        ctx.wait("g", "act")
        ctx.wait("g", "pe")
        self.load_bcast(ctx, g_ap, s.sbB[:, 0:1024], D, lst="g")
        self.load_bcast(ctx, b_ap, s.sbB[:, 1024:2048], D, lst="g")
        for i in range(NT):
            ctx.op("v", lambda v, i=i: v.reduce_sum(
                s.sf[:, 600 + i:601 + i], s.x_cur[:, i * D:(i + 1) * D],
                axis=AX.X, negate=True), inc="dve")
        ctx.wait("v", "dve")
        d1 = ctx.op("v", lambda v: v.tensor_scalar(
            s.sf[:, 616:632], s.sf[:, 600:616], 1.0 / D, None, op0=ALU.mult),
            inc="dve")
        ctx.wait("s", "dve", d1)
        for i in range(NT):
            ctx.op("s", lambda a, i=i: a.activation(
                s.sbB[:, 4096:5120], s.x_cur[:, i * D:(i + 1) * D], AF.Square,
                bias=s.sf[:, 616 + i:617 + i], scale=1.0,
                accum_out=s.sf[:, 526 + i:527 + i]), inc="act")
        ctx.wait("s", "act")
        a2 = ctx.op("s", lambda a: a.activation(
            s.sf[:, 542:558], s.sf[:, 526:542], AF.Sqrt,
            bias=s.ccf[:, 256:257], scale=1.0 / D), inc="act")
        ctx.wait("v", "act", a2)
        d2 = ctx.op("v", lambda v: v.reciprocal(
            s.sf[:, 600:616], s.sf[:, 542:558]), inc="dve")
        ctx.wait("v", "dve", d2)
        ctx.wait("v", "qg")
        pe_ring = {}
        for i in range(NT):
            xs = i % 2
            xn = lambda off=2048 + 1024 * xs: s.sbB[:, off:off + 1024]
            if i >= 2:
                ctx.wait("v", "pe", pe_ring[i - 2])
            ctx.op("v", lambda v, i=i, xn=xn: v.tensor_scalar(
                xn(), s.x_cur[:, i * D:(i + 1) * D], s.sf[:, 616 + i:617 + i],
                s.sf[:, 600 + i:601 + i], op0=ALU.add, op1=ALU.mult))
            ctx.op("v", lambda v, xn=xn: v.tensor_tensor(
                xn(), xn(), s.sbB[:, 0:1024], op=ALU.mult))
            d3 = ctx.op("v", lambda v, xn=xn: v.tensor_tensor(
                xn(), xn(), s.sbB[:, 1024:2048], op=ALU.add), inc="dve")
            ctx.wait("t", "dve", d3)
            pairs = [(xn()[:, j * 128:(j + 1) * 128],
                      s.xt[:, j * T + i * 128:j * T + i * 128 + 128])
                     for j in range(DC)]
            self.bf_transpose_batch(ctx, pairs, depth=2)
            pe_ring[i] = ctx.cnt["pe"]

    def emit_L1_ffn(self, ctx):
        s = self
        inp = self.inp
        CAP = 1280
        E = 8
        self.dscr_e = self.nc.dram_tensor("dscr_e", [E * CAP + 128, 1024],
                                          F32, kind="Internal").ap()
        ctx.wait("t", "qs", self.c_x_loaded)
        ctx.wait("t", "dve", self.c_dve_consts)
        self.transpose_to_xt(ctx, s.x_cur)

        self.barrier(ctx)
        self.load_bcast(ctx, inp["L1_ffn_bg"], s.sf[:, 256:264], 8)
        ctx.wait("v", "qs")
        ctx.wait("t", "act")
        self.emit_logits_hl(ctx, inp["L1_ffn_Wg"], nexp=8, bias_off=256)
        ctx.op("v", lambda v: v.memset(s.sbB[:, 0:256], 0.0), inc="dve")
        ctx.wait("v", "dve")
        self.emit_topk_softmax(ctx, nexp=8, logit_off=0, wfull_off=None, kk=2,
                               slot_store=(s.sbB, 280, 296, 312, 328))
        self.snap(ctx, "l1oh", s.sbB[:, 0:256])

        self.barrier(ctx)
        ctx.wait("t", "dve")
        SC = s.sbA.bitcast(F32)
        ctx.op("g", lambda g: g.iota(s.ccf[:, 352:480].bitcast(I32),
                                     pattern=[[1, 128]], base=0,
                                     channel_multiplier=0), inc="pool")
        ctx.wait("v", "pool")
        ctx.op("v", lambda v: v.tensor_copy(s.cb[:, 0:128],
                                            s.ccf[:, 352:480].bitcast(I32)),
               inc="dve")
        ctx.wait("v", "dve")
        ctx.op("v", lambda v: v.tensor_scalar(
            s.cb[:, 0:128], s.cb[:, 0:128], s.ccf[:, 273:274], None,
            op0=ALU.is_gt), inc="dve")
        ctx.wait("v", "dve")
        ctx.op("v", lambda v: v.tensor_copy(s.sbB[:, 512:640], s.cb[:, 0:128]),
               inc="dve")
        tri_done = ctx.op("v", lambda v: v.memset(s.sbB[:, 640:648], 0.0),
                          inc="dve")
        ctx.op("v", lambda v: v.tensor_scalar(
            s.sf[:, 838:839], s.ccf[:, 273:274], 1.0, float(E * CAP),
            op0=ALU.mult, op1=ALU.add), inc="dve")
        def cmm(t_):
            return t_.matmul(s.PD[0:1, 0:256], lhsT=s.cc[:, 2304:2305],
                             rhs=s.sbB[:, 0:256], start=True, stop=True)
        pe_c = ctx.op("t", cmm, inc="pe")
        ctx.wait("v", "pe", pe_c)
        dc = ctx.op("v", lambda v: v.tensor_copy(
            s.sf[0:1, 344:600], s.PD[0:1, 0:256]), inc="dve")
        self.snap(ctx, "l1cnt", s.sf[:, 344:600])
        ctx.wait("q", "dve", dc)
        cq = ctx.dma("q", out=self.dscr_s.rearrange("a b -> (a b)")[256:512]
                     .unsqueeze(0), in_=s.sf[0:1, 344:600])
        ctx.wait("g", "qs", cq)
        ctx.wait("g", "dve")
        cg = ctx.dma("g", out=s.sbB[0:32, 640:648],
                     in_=self.dscr_s[16:32, 0:16]
                     .rearrange("a (u2 e) -> (a u2) e", e=8))
        ctx.wait("t", "qg", cg)
        ctx.wait("t", "dve", tri_done)
        def bmm_(t_):
            return t_.matmul(s.PE_[:, 0:8], lhsT=s.sbB[:, 512:640],
                             rhs=s.sbB[:, 640:648], start=True, stop=True)
        pe_b = ctx.op("t", bmm_, inc="pe")
        ctx.wait("v", "pe", pe_b)
        db = ctx.op("v", lambda v: v.tensor_copy(
            s.sf[0:32, 884:892], s.PE_[0:32, 0:8]), inc="dve")
        ctx.wait("q", "dve", db)
        bq = ctx.dma("q", out=self.dscr_s[0:32, 0:8], in_=s.sf[0:32, 884:892])
        ctx.wait("q", "qs", bq)
        bb = ctx.dma("q", out=s.sf[:, 344:600].rearrange("p (u e) -> p u e", e=8),
                     in_=self.dscr_s[0:32, 0:8].unsqueeze(0)
                     .broadcast_to([P, 32, 8]))
        self.snap(ctx, "l1base", s.sf[:, 344:600])
        def emm(t_):
            return t_.matmul(s.PA[:, 0:256], lhsT=s.cc[:, 2176:2304],
                             rhs=s.sbB[:, 0:256], start=True, stop=True)
        pe_e = ctx.op("t", emm, inc="pe")
        ctx.wait("v", "pe", pe_e)
        ctx.wait("v", "qs", bb)
        ctx.op("v", lambda v: v.tensor_tensor(
            SC[:, 0:256], s.PA[:, 0:256], s.sf[:, 344:600], op=ALU.add),
            inc="dve")
        ctx.wait("v", "dve")
        ctx.op("v", lambda v: v.tensor_tensor(
            SC[:, 0:256], SC[:, 0:256], s.sbB[:, 0:256], op=ALU.mult), inc="dve")
        ctx.wait("v", "dve")
        ctx.op("v", lambda v: v.reduce_sum(
            SC[:, 256:288], SC[:, 0:256].rearrange("p (u e) -> p u e", e=8),
            axis=AX.X), inc="dve")
        ctx.wait("v", "dve")
        ctx.op("v", lambda v: v.scalar_tensor_tensor(
            SC[:, 288:320], s.sf[:, 312:344], float(CAP), SC[:, 256:288],
            op0=ALU.mult, op1=ALU.add), inc="dve")
        ctx.op("v", lambda v: v.tensor_scalar(
            SC[:, 320:352], SC[:, 256:288], float(CAP), None,
            op0=ALU.is_ge), inc="dve")
        ctx.wait("v", "dve")
        ctx.op("v", lambda v: v.tensor_scalar(
            SC[:, 352:384], SC[:, 320:352], -1.0, 1.0,
            op0=ALU.mult, op1=ALU.add), inc="dve")
        ctx.wait("v", "dve")
        ctx.op("v", lambda v: v.tensor_tensor(
            SC[:, 288:320], SC[:, 288:320], SC[:, 352:384], op=ALU.mult),
            inc="dve")
        ctx.wait("v", "dve")
        ctx.op("v", lambda v: v.scalar_tensor_tensor(
            SC[:, 288:320], SC[:, 320:352], s.sf[:, 838:839], SC[:, 288:320],
            op0=ALU.mult, op1=ALU.add), inc="dve")
        ctx.wait("v", "dve")
        su_done = ctx.op("v", lambda v: v.tensor_copy(
            s.su[:, 0:32], SC[:, 288:320]), inc="dve")

        xe_view = self.dscr_a.rearrange("a b -> a b")
        ctx.wait("g", "dve", su_done)
        self._l1_scat = {}
        for u in range(32):
            si, i = u // 16, u % 16
            slot = 2048 + (u % 2) * 1024
            if u >= 2:
                ctx.wait("v", "qg", self._l1_scat[u - 2])
            xd = ctx.op("v", lambda v, si=si, i=i, slot=slot: v.tensor_scalar(
                s.sbB[:, slot:slot + 1024], s.x_cur[:, i * D:(i + 1) * D],
                s.sf[:, (280 if si == 0 else 296) + i:
                     (280 if si == 0 else 296) + i + 1], None,
                op0=ALU.mult), inc="dve")
            ctx.wait("g", "dve", xd)
            qg_t = ctx.op("g", lambda g, u=u, slot=slot: g.indirect_dma_start(
                out=self.dscr_a[0:6144, :],
                out_offset=bass.IndirectOffsetOnAxis(ap=s.su[:, u:u + 1], axis=0),
                in_=s.sbB[:, slot:slot + 1024], in_offset=None), inc="qg", by=16)
            self._l1_scat[u] = qg_t
        self._l1_scat = {}

        self.snap(ctx, "l1su", s.su[:, 0:32])
        self.snap(ctx, "l1ws", s.sf[:, 280:344])
        self.barrier(ctx)
        ctx.op("v", lambda v: v.memset(s.sbB[:, 4096:6144], 0.0), inc="dve")
        ctx.wait("g", "dve")
        ctx.dma("g", out=self.dscr_e[E * CAP:E * CAP + 128, :],
                in_=s.sbB[:, 4096:6144].bitcast(F32))

        self.barrier(ctx)
        W1, W2, W3 = inp["L1_ffn_W1"], inp["L1_ffn_W2"], inp["L1_ffn_W3"]
        ctx.wait("q", "qg")
        for e in range(E):
            ctx.wait("q", "pe")
            ctx.dma("q", out=s.sbB[:, 0:CAP * 8].rearrange("p (r c) -> p r c", c=1024),
                    in_=self.dscr_a[e * CAP:(e + 1) * CAP, :]
                    .rearrange("(r p) c -> p r c", p=P))
            ctx.wait("t", "qs")
            pairs = []
            for tb in range(CAP // 128):
                for j in range(8):
                    pairs.append((s.sbB[:, tb * 1024 + j * 128:tb * 1024 + (j + 1) * 128],
                                  s.xt[:, j * CAP + tb * 128:j * CAP + (tb + 1) * 128]))
            self.bf_transpose_batch(ctx, pairs, depth=2)
            if e == 0:
                ctx.wait("g", "pe")
                self._l1c13 = self.l1_w13_load(ctx, W1, W3, 0, 0)
            for q in range(4):
                ctx.wait("g", "pe")
                ctx.dma("g", out=s.xt[:, 10240:16384]
                        .rearrange("p (hc c) -> p hc c", c=1024),
                        in_=W2[e][q * 1024:q * 1024 + 768, :]
                        .rearrange("(hc p) c -> p hc c", p=P))
                c2 = ctx.dma("g", out=s.sbB[:, 13312:15360]
                             .rearrange("p (hc c) -> p hc c", c=1024),
                             in_=W2[e][q * 1024 + 768:(q + 1) * 1024, :]
                             .rearrange("(hc p) c -> p hc c", p=P))
                ctx.wait("t", "qg", self._l1c13)
                ctx.wait("t", "act")
                ctx.wait("t", "dve")
                MVS = ((0, 512), (512, 512), (1024, 256))
                units = [(hc, m0, mn) for hc in range(8) for m0, mn in MVS]
                d1ring = {}
                for ui, (hc, m0, mn) in enumerate(units):
                    pr = ui % 2
                    pa_a = s.PA[:, pr * 512:pr * 512 + mn]
                    pa_b = s.PA[:, 1024 + pr * 512:1024 + pr * 512 + mn]
                    if ui >= 2:
                        ctx.wait("t", "dve", d1ring[ui - 2])

                    def h1mm(t_, hc=hc, m0=m0, mn=mn, pa_a=pa_a, pa_b=pa_b):
                        last = None
                        for j in range(8):
                            last = t_.matmul(
                                pa_a,
                                lhsT=s.wbuf[:, j * 1024 + hc * 128:
                                            j * 1024 + (hc + 1) * 128],
                                rhs=s.xt[:, j * CAP + m0:j * CAP + m0 + mn],
                                start=(j == 0), stop=(j == 7))
                        for j in range(8):
                            last = t_.matmul(
                                pa_b,
                                lhsT=s.wbuf[:, 8192 + j * 1024 + hc * 128:
                                            8192 + j * 1024 + (hc + 1) * 128],
                                rhs=s.xt[:, j * CAP + m0:j * CAP + m0 + mn],
                                start=(j == 0), stop=(j == 7))
                        return last
                    p1 = ctx.op("t", h1mm, inc="pe")
                    ctx.wait("s", "pe", p1)
                    a1 = ctx.op("s", lambda a, mn=mn, pa_a=pa_a, pr=pr: a.activation(
                        s.sbB[:, 12288 + pr * 512:12288 + pr * 512 + mn], pa_a,
                        AF.Silu), inc="act")
                    ctx.wait("v", "act", a1)
                    d1 = ctx.op("v", lambda v, hc=hc, m0=m0, mn=mn, pa_b=pa_b, pr=pr:
                                v.tensor_tensor(
                                    s.sbB[:, hc * CAP + m0:hc * CAP + m0 + mn],
                                    s.sbB[:, 12288 + pr * 512:12288 + pr * 512 + mn],
                                    pa_b, op=ALU.mult), inc="dve")
                    d1ring[ui] = d1
                ctx.wait("t", "dve")
                if not (e == E - 1 and q == 3):
                    ne, nq = (e, q + 1) if q < 3 else (e + 1, 0)
                    ctx.wait("g", "pe")
                    self._l1c13 = self.l1_w13_load(ctx, W1, W3, ne, nq)
                ctx.wait("t", "qg", c2)
                if q >= 1:
                    ctx.wait("g", "qg")
                for g in range(2 * (CAP // 128)):
                    tb, nh = g // 2, g % 2

                    def bmm(t_, tb=tb, nh=nh, g=g):
                        last = None
                        for hc in range(8):
                            rhs = (s.xt[:, 10240 + hc * 1024 + nh * 512:
                                        10240 + hc * 1024 + nh * 512 + 512]
                                   if hc < 6 else
                                   s.sbB[:, 13312 + (hc - 6) * 1024 + nh * 512:
                                         13312 + (hc - 6) * 1024 + nh * 512 + 512])
                            last = t_.matmul(
                                s.PA[:, (g % 4) * 512:((g % 4) + 1) * 512],
                                lhsT=s.sbB[:, hc * CAP + tb * 128:
                                           hc * CAP + (tb + 1) * 128],
                                rhs=rhs,
                                start=(hc == 0), stop=(hc == 7))
                        return last
                    if g >= 4:
                        ctx.wait("t", "dve", self._l1_bev[g - 4])
                    pe_t = ctx.op("t", bmm, inc="pe")
                    ctx.wait("v", "pe", pe_t)
                    if g >= 2:
                        ctx.wait("v", "qg", self._l1_bq[g - 2])
                    dv = ctx.op("v", lambda v, g=g: v.tensor_copy(
                        s.cb[:, (g % 2) * 512:((g % 2) + 1) * 512],
                        s.PA[:, (g % 4) * 512:((g % 4) + 1) * 512]), inc="dve")
                    if not hasattr(self, "_l1_bev"):
                        self._l1_bev, self._l1_bq = {}, {}
                    self._l1_bev[g] = dv
                    ctx.wait("g", "dve", dv)
                    if q == 0:
                        qt = ctx.op("g", lambda gg, e=e, tb=tb, nh=nh, g=g:
                                    gg.dma_start(
                                        out=self.dscr_e[e * CAP + tb * 128:
                                                        e * CAP + (tb + 1) * 128,
                                                        nh * 512:(nh + 1) * 512],
                                        in_=s.cb[:, (g % 2) * 512:
                                                 ((g % 2) + 1) * 512]),
                                    inc="qg", by=16)
                    else:
                        qt = ctx.op("g", lambda gg, e=e, tb=tb, nh=nh, g=g:
                                    gg.dma_start(
                                        out=self.dscr_e[e * CAP + tb * 128:
                                                        e * CAP + (tb + 1) * 128,
                                                        nh * 512:(nh + 1) * 512],
                                        in_=s.cb[:, (g % 2) * 512:
                                                 ((g % 2) + 1) * 512],
                                        accum_op=ALU.add), inc="qg", by=16)
                    self._l1_bq[g] = qt
                self._l1_bev, self._l1_bq = {}, {}

        if "l1eo" in DBG_TAGS:
            dbg = self.dbg_out("dbg_l1eo", [1280, 1024], F32)
            ctx.wait("q", "qg")
            ctx.dma("q", out=dbg[:, :], in_=self.dscr_e[0:1280, :])
            for lst in ("t", "s", "v", "g"):
                ctx.wait(lst, "qs")
        if "l1xe" in DBG_TAGS:
            dbg2 = self.dbg_out("dbg_l1xe", [1280, 1024], BF16)
            ctx.wait("q", "qg")
            ctx.dma("q", out=dbg2[:, :], in_=self.dscr_a[0:1280, :])
            for lst in ("t", "s", "v", "g"):
                ctx.wait(lst, "qs")
        if L2FFN_PHASE < 5:
            return
        self.barrier(ctx)
        ctx.wait("g", "qg")
        for i in range(NT):
            for si in range(2):
                u = si * 16 + i
                slot = (2 * i + si) % 4
                gslot = s.sbB[:, slot * 2048:slot * 2048 + 2048].bitcast(F32)
                ctx.wait("g", "dve")
                ctx.op("g", lambda g, u=u, gslot=gslot: g.indirect_dma_start(
                    out=gslot, out_offset=None, in_=self.dscr_e[0:6144, :],
                    in_offset=bass.IndirectOffsetOnAxis(ap=s.su[:, u:u + 1], axis=0)),
                    inc="qg", by=16)
            ctx.wait("v", "qg")
            s0 = s.sbB[:, ((2 * i) % 4) * 2048:((2 * i) % 4) * 2048 + 2048].bitcast(F32)
            s1 = s.sbB[:, ((2 * i + 1) % 4) * 2048:((2 * i + 1) % 4) * 2048 + 2048].bitcast(F32)
            ctx.op("v", lambda v, s0=s0, s1=s1: v.tensor_tensor(
                s.cb[:, 0:1024], s0, s1, op=ALU.add))
            ctx.op("v", lambda v, i=i: v.scalar_tensor_tensor(
                s.x_cur[:, i * D:(i + 1) * D], s.x_cur[:, i * D:(i + 1) * D],
                2.0, s.cb[:, 0:1024], op0=ALU.mult, op1=ALU.add), inc="dve")
        self.ln_batch(ctx, [(s.x_cur[:, i * D:(i + 1) * D],
                             s.x_cur[:, i * D:(i + 1) * D]) for i in range(NT)],
                      sq_scratch=s.sbB[:, 8192:9216])


    def emit_L2_attn(self, ctx):
        s = self
        inp = self.inp
        PADQ, PADK, VON0, VON1, VT2 = (self.PADQ, self.PADK, self.VON0,
                                       self.VON1, self.VT2)
        ctx.wait("t", "qs", self.c_x_loaded)
        ctx.wait("t", "dve", self.c_dve_consts)

        ctx.wait("q", "dve")
        ctx.wait("q", "act")
        ctx.dma("q", out=self.dscr_f.rearrange("(i p) c -> p i c", p=P),
                in_=s.x_cur[:].rearrange("p (i c) -> p i c", c=D))

        self.barrier(ctx)
        self.emit_gained_ln_to_xt(ctx, inp["L2_attn_g1"], inp["L2_attn_b1"])
        ctx.wait("t", "act")

        self.barrier(ctx)
        ctx.wait("g", "pe")
        ctx.dma("g", out=s.wbuf[:, 0:8192].rearrange("p (j n) -> p j n", n=1024),
                in_=inp["L2_attn_sf_w"].rearrange("(j p) n -> p j n", p=P))
        ctx.dma("g", out=s.wbuf[:, 8192:16384].rearrange("p (j n) -> p j n", n=1024),
                in_=inp["L2_attn_mask"].rearrange("(j p) n -> p j n", p=P))
        self.load_bcast(ctx, inp["L2_attn_sf_b"], s.cb[:, 0:1024], D, lst="g")
        ctx.wait("v", "qg")
        ctx.op("v", lambda v: v.tensor_tensor(
            s.wbuf[:, 0:8192], s.wbuf[:, 0:8192], s.wbuf[:, 8192:16384],
            op=ALU.mult), inc="dve")
        ctx.wait("q", "dve")
        ctx.wait("q", "qs")
        ctx.dma("q", out=self.dscr_w[0:1024, 0:1024]
                .rearrange("(j p) n -> p j n", p=P),
                in_=s.wbuf[:, 0:8192].rearrange("p (j n) -> p j n", n=1024))
        ctx.wait("q", "qs")
        for j in range(DC):
            ctx.dma("q", out=s.wbuf[:, j * 1024:(j + 1) * 1024],
                    in_=self.dscr_w[0:1024, j * 128:(j + 1) * 128], transpose=True)
        ctx.wait("t", "qs")
        self.snap(ctx, "l2sfmT", s.wbuf[:, 0:8192])
        self.snap(ctx, "l2xn1", s.xt[:, :])

        def pmm(g):
            def f(t_, g=g):
                tt, nh = g // 2, g % 2
                last = None
                for j in range(DC):
                    last = t_.matmul(
                        s.PA[:, (g % 4) * 512:((g % 4) + 1) * 512],
                        lhsT=s.xt[:, j * T + tt * 128:j * T + (tt + 1) * 128],
                        rhs=s.wbuf[:, j * 1024 + nh * 512:j * 1024 + (nh + 1) * 512],
                        start=(j == 0), stop=(j == DC - 1))
                return last
            return f

        def pev(g):
            def f(v_, g=g):
                tt, nh = g // 2, g % 2
                sl = s.sbB[:, (g % 4) * 512:((g % 4) + 1) * 512]
                v_.scalar_tensor_tensor(sl, s.PA[:, (g % 4) * 512:((g % 4) + 1) * 512],
                                        1.0, s.cb[:, nh * 512:(nh + 1) * 512],
                                        op0=ALU.mult, op1=ALU.add)
                return v_.tensor_tensor(
                    s.x_cur[:, tt * D + nh * 512:tt * D + (nh + 1) * 512],
                    s.x_cur[:, tt * D + nh * 512:tt * D + (nh + 1) * 512],
                    sl, op=ALU.add)
            return f
        pipelined(ctx, 32, pmm, pev, "v", depth=4)

        self.snap(ctx, "l2p2", s.x_cur[:, :])
        self.barrier(ctx)
        self.emit_gained_ln_to_xt(ctx, inp["L2_attn_g2"], inp["L2_attn_b2"])
        ctx.wait("t", "act")

        self.barrier(ctx)
        qkvT = self.dscr_a.rearrange("a b -> (a b)")
        P4 = (("L2_attn_wq", "L2_attn_bq"), ("L2_attn_wk", "L2_attn_bk"),
              ("L2_attn_wv", "L2_attn_bv"))

        def p4_cast(m):
            r0 = 1024 + (m % 2) * 1024
            ctx.dma("g", out=self.dscr_w[r0:r0 + 1024, 0:1024], in_=inp[P4[m][0]])
            self.load_chunkcol(ctx, inp[P4[m][1]], s.sf[:, m * 8:m * 8 + 8], 8,
                               lst="g")
            return ctx.cnt["qg"]

        def p4_trload(m):
            r0 = 1024 + (m % 2) * 1024
            W0 = (m % 2) * 8192
            for j in range(DC):
                ctx.dma("q", out=s.wbuf[:, W0 + j * 1024:W0 + (j + 1) * 1024],
                        in_=self.dscr_w[r0:r0 + 1024, j * 128:(j + 1) * 128],
                        transpose=True)
            return ctx.cnt["qs"]

        ctx.wait("g", "qs")
        cg = {0: p4_cast(0)}
        ctx.wait("q", "qg", cg[0])
        ctx.wait("q", "pe")
        ct = {0: p4_trload(0)}
        self._l2q = {}
        dvring = {}
        for m in range(3):
            W0 = (m % 2) * 8192
            if m + 1 < 3:
                ctx.wait("g", "qs", ct[m - 1] if m > 0 else 0)
                cg[m + 1] = p4_cast(m + 1)
                ctx.wait("q", "qg", cg[m + 1])
                ctx.wait("q", "pe")
                ct[m + 1] = p4_trload(m + 1)
            ctx.wait("t", "qs", ct[m])
            self._p4ct2 = ct[m]
            ctx.wait("v", "qg", cg[m])
            for hp in range(8):
                ctx.wait("t", "dve")
                def qmm(t_, hp=hp, W0=W0):
                    last = None
                    for tc in range(4):
                        for j in range(DC):
                            last = t_.matmul(
                                s.PA[:, tc * 512:(tc + 1) * 512],
                                lhsT=s.wbuf[:, W0 + j * 1024 + hp * 128:
                                            W0 + j * 1024 + (hp + 1) * 128],
                                rhs=s.xt[:, j * T + tc * 512:j * T + (tc + 1) * 512],
                                start=(j == 0), stop=(j == DC - 1))
                    return last
                pe_t = ctx.op("t", qmm, inc="pe")
                ctx.wait("v", "pe", pe_t)
                gl = m * 8 + hp
                if gl >= 2:
                    ctx.wait("v", "qs", self._l2q[gl - 2])
                slot = (hp % 2) * 2048
                dv = ctx.op("v", lambda v, slot=slot, m=m, hp=hp: v.tensor_scalar(
                    s.sbB[:, slot:slot + 2048], s.PA[:, 0:2048],
                    s.sf[:, m * 8 + hp:m * 8 + hp + 1], None, op0=ALU.add),
                    inc="dve")
                ctx.wait("q", "dve", dv)
                qt = ctx.dma("q", out=qkvT[(m * 1024 + hp * 128) * 2048:
                                           (m * 1024 + (hp + 1) * 128) * 2048]
                             .rearrange("(p t) -> p t", p=P),
                             in_=s.sbB[:, slot:slot + 2048])
                self._l2q[gl] = qt
        self._l2q = {}

        self.barrier(ctx)
        ctx.dma("g", out=self.dscr_w[4096:8192, 0:1024], in_=inp["L2_attn_ff1_w"])
        ctx.dma("g", out=self.dscr_w[3072:4096, 0:4096], in_=inp["L2_attn_ff2_w"])
        self._p8cg = ctx.cnt["qg"]
        ctx.wait("v", "qs")
        ctx.op("v", lambda v: v.memset(s.sbB[:, 0:8192], 0.0), inc="dve")
        for h2 in range(2):
            von = (VON0, VON1)[h2]
            ctx.op("v", lambda v, von=von: v.memset(s.sbB[:, von:von + 2048], 0.0),
                   inc="dve")
            last = None
            for mt in range(NT):
                last = ctx.op("v", lambda v, von=von, mt=mt: v.memset(
                    s.sbB[:, von + mt * 128 + 64:von + mt * 128 + 65], 1.0),
                    inc="dve")
        ctx.wait("q", "dve")
        ctx.wait("q", "qs")
        ctx.wait("t", "dve")
        O = s.sbA
        for hp in range(8):
            ctx.wait("q", "pe")
            ctx.dma("q", out=s.sbB[:, VT2:VT2 + 2048],
                    in_=qkvT[(2048 + hp * 128) * 2048:(2048 + (hp + 1) * 128) * 2048]
                    .rearrange("(p t) -> p t", p=P))
            ctx.wait("t", "qs")

            def vmm(g):
                def f(t_, g=g):
                    bank = [s.PBb, s.PCb][g % 2]
                    last = None
                    for k in range(8):
                        mtile = g * 8 + k
                        last = t_.transpose(bank[:, k * 128:(k + 1) * 128],
                                            s.sbB[:, VT2 + mtile * 128:
                                                   VT2 + (mtile + 1) * 128],
                                            s.cc[:, 2048:2176])
                    return last
                return f

            def vev(g):
                def f(e_, g=g):
                    bank = [s.PBb, s.PCb][g % 2]
                    last = None
                    for k in range(8):
                        mtile = g * 8 + k
                        last = e_.copy(s.sbB[:, VON0 + mtile * 128:VON0 + mtile * 128 + 64],
                                       bank[:, k * 128:k * 128 + 64])
                        last = e_.copy(s.sbB[:, VON1 + mtile * 128:VON1 + mtile * 128 + 64],
                                       bank[:, k * 128 + 64:k * 128 + 128])
                    return last
                return f
            pipelined(ctx, 2, vmm, vev, "s")
            for hh in range(2):
                h = 2 * hp + hh
                ctx.wait("q", "pe")
                ctx.dma("q", out=s.sbB[0:64, PADQ:PADQ + 2048],
                        in_=qkvT[(h * 64) * 2048:(h * 64 + 64) * 2048]
                        .rearrange("(p t) -> p t", p=64))
                ctx.dma("q", out=s.sbB[0:64, PADK:PADK + 2048],
                        in_=qkvT[(1024 + h * 64) * 2048:(1024 + h * 64 + 64) * 2048]
                        .rearrange("(p t) -> p t", p=64))
                ctx.wait("t", "qs")
                if h == 0:
                    self.snap(ctx, "l2padQ", s.sbB[:, PADQ:PADQ + 2048])
                    self.snap(ctx, "l2padK", s.sbB[:, PADK:PADK + 2048])
                    self.snap(ctx, "l2von", s.sbB[:, VON0:VON0 + 2048])
                self.attn_head_core(ctx, h, hh, kprime=False, score="relu", O=O,
                                    relu_eps=1e-9)

        self.snap(ctx, "l2O", O[:, :])
        self.barrier(ctx)
        ctx.wait("t", "dve")
        pairs = []
        for i in range(NT):
            for j in range(DC):
                pairs.append((O[:, i * D + j * 128:i * D + (j + 1) * 128],
                              s.xt[:, j * T + i * 128:j * T + i * 128 + 128]))
        self.bf_transpose_batch(ctx, pairs, depth=2)
        ctx.wait("g", "qs")
        ctx.dma("g", out=self.dscr_w[1024:2048, 0:1024], in_=inp["L2_attn_wo"])
        self.load_bcast(ctx, inp["L2_attn_bo"], s.cb[:, 0:1024], D, lst="g")
        ctx.wait("q", "qg")
        for j in range(DC):
            ctx.dma("q", out=s.wbuf[:, j * 1024:(j + 1) * 1024],
                    in_=self.dscr_w[1024:2048, j * 128:(j + 1) * 128], transpose=True)
        ctx.wait("t", "qs")
        ctx.wait("t", "act")
        ctx.wait("v", "qg")
        pipelined(ctx, 32, pmm, pev, "v", depth=4)

        self.snap(ctx, "l2p6", s.x_cur[:, :])
        self.barrier(ctx)
        fl = self.dscr_b.rearrange("a b -> (a b)")
        ff1T = fl[0:1024 * 4096].rearrange("(r c) -> r c", c=4096)
        ff2T = fl[1024 * 4096:2 * 1024 * 4096].rearrange("(r c) -> r c", c=1024)
        self.load_chunkcol(ctx, inp["L2_attn_ff1_b"], s.sf[:, 32:64], 32, lst="g")
        self.load_bcast(ctx, inp["L2_attn_ff2_b"], s.cb[:, 0:1024], D, lst="g")
        ctx.wait("q", "pe")
        ctx.wait("q", "dve")
        ctx.wait("q", "act")
        ctx.wait("q", "qg", self._p8cg)
        ctx.wait("g", "qs")
        self._l2f = {}
        for j in range(DC):
            sl = 8192 + (j % 2) * 4096
            if j >= 2:
                ctx.wait("q", "qg", self._l2f[j - 2])
            ctx.dma("q", out=s.sbB[:, sl:sl + 4096],
                    in_=self.dscr_w[4096:8192, j * 128:(j + 1) * 128],
                    transpose=True)
            ctx.wait("g", "qs")
            ft = ctx.dma("g", out=ff1T[j * 128:(j + 1) * 128, :]
                         .rearrange("p c -> p c"), in_=s.sbB[:, sl:sl + 4096])
            self._l2f[j] = ft
        ctx.wait("q", "qg")
        for hc in range(32):
            sl = 8192 + (hc % 2) * 1024
            if hc >= 2:
                ctx.wait("q", "qg", self._l2f[hc - 2])
            ctx.dma("q", out=s.sbB[:, sl:sl + 1024],
                    in_=self.dscr_w[3072:4096, hc * 128:(hc + 1) * 128],
                    transpose=True)
            ctx.wait("g", "qs")
            ft = ctx.dma("g", out=ff2T[hc * 128:(hc + 1) * 128, :],
                         in_=s.sbB[:, sl:sl + 1024])
            self._l2f[hc] = ft
        self._l2f = {}
        self.emit_gained_ln_to_xt(ctx, inp["L2_attn_g3"], inp["L2_attn_b3"])
        ctx.wait("t", "act")

        self.barrier(ctx)
        ctx.wait("q", "qg")
        ctx.wait("t", "qs")
        ctx.wait("t", "qg")
        ctx.wait("s", "qg")
        ctx.wait("v", "qg")
        ct1 = {}
        pe_after = {-1: ctx.cnt["pe"]}

        def p8_ff1load(k):
            tc, ng = k // 8, k % 8
            slot = (k % 2) * 4096
            return ctx.dma("q", out=s.wbuf[:, slot:slot + 4096]
                           .rearrange("p (j nt c) -> p j nt c", nt=4, c=128),
                           in_=ff1T[:, ng * 512:(ng + 1) * 512]
                           .rearrange("(j p) (nt c) -> p j nt c", p=P, c=128))

        def p8_ff2load(k):
            ntc = k % 4
            dst = (s.wbuf[:, 8192:16384] if k % 2 == 0
                   else s.sbA[:, 8192:16384])
            return ctx.dma("q", out=dst.rearrange("p (nt c) -> p nt c", c=1024),
                           in_=ff2T[ntc * 1024:(ntc + 1) * 1024, :]
                           .rearrange("(nt p) c -> p nt c", p=P))

        ct1[0] = p8_ff1load(0)
        ct2 = {}
        for tc in range(4):
            if tc > 0:
                ctx.wait("s", "pe")
            for ng in range(8):
                k = tc * 8 + ng
                slot = (k % 2) * 4096
                if k + 1 < 32:
                    ctx.wait("q", "pe", pe_after[k - 1])
                    ct1[k + 1] = p8_ff1load(k + 1)
                ctx.wait("t", "qs", ct1[k])
                if k == 0:
                    self.snap(ctx, "l2w1c0", s.wbuf[:, 0:4096])

                def amm(g):
                    def f(t_, g=g, tc=tc, slot=slot):
                        bank = [s.PD, s.PE_][g % 2]
                        last = None
                        for j in range(DC):
                            last = t_.matmul(
                                bank[:, :],
                                lhsT=s.wbuf[:, slot + j * 512 + g * 128:
                                            slot + j * 512 + (g + 1) * 128],
                                rhs=s.xt[:, j * T + tc * 512:j * T + (tc + 1) * 512],
                                start=(j == 0), stop=(j == DC - 1))
                        return last
                    return f

                def aev(g):
                    def f(e_, g=g, ng=ng):
                        bank = [s.PD, s.PE_][g % 2]
                        nt = ng * 4 + g
                        return e_.activation(s.sbB[:, nt * 512:(nt + 1) * 512],
                                             bank[:, :], AF.Relu,
                                             bias=s.sf[:, 32 + nt:32 + nt + 1],
                                             scale=1.0)
                    return f
                pipelined(ctx, 4, amm, aev, "s", depth=2)
                pe_after[k] = ctx.cnt["pe"]
            ctx.wait("t", "act")
            if tc == 0:
                self.snap(ctx, "l2stg0", s.sbB[:, 0:16384])
                ctx.wait("q", "pe", pe_after[-1])
                ct2[0] = p8_ff2load(0)
            for tth in range(2):
                pe_b = None
                for ntc in range(4):
                    k2 = tc * 8 + tth * 4 + ntc
                    if k2 + 1 < 32:
                        ctx.wait("q", "pe")
                        ct2[k2 + 1] = p8_ff2load(k2 + 1)
                    ctx.wait("t", "qs", ct2[k2])
                    ften = s.wbuf if k2 % 2 == 0 else s.sbA

                    def bmm(t_, tc=tc, tth=tth, ntc=ntc, ften=ften):
                        last = None
                        for grp in range(4):
                            ttr = tth * 2 + grp // 2
                            nh = grp % 2
                            for ni in range(8):
                                nt = ntc * 8 + ni
                                last = t_.matmul(
                                    s.PA[:, grp * 512:(grp + 1) * 512],
                                    lhsT=s.sbB[:, nt * 512 + ttr * 128:
                                               nt * 512 + ttr * 128 + 128],
                                    rhs=ften[:, 8192 + ni * 1024 + nh * 512:
                                             8192 + ni * 1024 + nh * 512 + 512],
                                    start=(nt == 0), stop=(nt == 31))
                        return last
                    pe_b = ctx.op("t", bmm, inc="pe")
                ctx.wait("v", "pe", pe_b)
                for grp in range(4):
                    ttr = tth * 2 + grp // 2
                    tt = tc * 4 + ttr
                    nh = grp % 2
                    ctx.op("v", lambda v, grp=grp, tt=tt, nh=nh:
                           v.scalar_tensor_tensor(
                               s.sbA[:, (grp % 2) * 512:((grp % 2) + 1) * 512],
                               s.PA[:, grp * 512:(grp + 1) * 512], 1.0,
                               s.cb[:, nh * 512:(nh + 1) * 512],
                               op0=ALU.mult, op1=ALU.add))
                    ctx.op("v", lambda v, grp=grp, tt=tt, nh=nh: v.tensor_tensor(
                        s.x_cur[:, tt * D + nh * 512:tt * D + (nh + 1) * 512],
                        s.x_cur[:, tt * D + nh * 512:tt * D + (nh + 1) * 512],
                        s.sbA[:, (grp % 2) * 512:((grp % 2) + 1) * 512],
                        op=ALU.add), inc="dve")
                ctx.wait("t", "dve")

        self.snap(ctx, "l2p8", s.x_cur[:, :])
        self.barrier(ctx)
        ctx.wait("q", "qs")
        ctx.wait("q", "act")
        ctx.wait("q", "pe")
        for i in range(NT):
            sl = (i % 2) * 2048
            ctx.wait("q", "dve")
            ctx.dma("q", out=s.sbB[:, sl:sl + 2048].bitcast(F32),
                    in_=self.dscr_f[i * 128:(i + 1) * 128, :])
            ctx.wait("v", "qs")
            ctx.op("v", lambda v, i=i, sl=sl: v.tensor_tensor(
                s.x_cur[:, i * D:(i + 1) * D], s.sbB[:, sl:sl + 2048].bitcast(F32),
                s.x_cur[:, i * D:(i + 1) * D], op=ALU.add), inc="dve")
        self.ln_batch(ctx, [(s.x_cur[:, i * D:(i + 1) * D],
                             s.x_cur[:, i * D:(i + 1) * D]) for i in range(NT)],
                      sq_scratch=s.sbB[:, 8192:9216])


    def emit_L3_attn(self, ctx):
        s = self
        inp = self.inp
        ctx.wait("t", "qs", self.c_x_loaded)
        ctx.wait("t", "dve", self.c_dve_consts)
        ctx.wait("v", "qs", self.c_x_loaded)
        ctx.wait("s", "qs", self.c_x_loaded)

        self.barrier(ctx)
        self.load_bcast(ctx, inp["L3_attn_gamma"], s.cb[:, 0:1024], D, lst="g")
        ctx.wait("v", "qg")
        for i in range(NT):
            xs = 2048 + (i % 2) * 1024
            xn = s.sbB[:, xs:xs + 1024]
            ctx.op("s", lambda a, i=i: a.activation(
                s.sbB[:, 4096:5120], s.x_cur[:, i * D:(i + 1) * D], AF.Square,
                bias=0.0, scale=1.0, accum_out=s.sf[:, 516:517]), inc="act")
            ctx.wait("s", "act")
            a1 = ctx.op("s", lambda a: a.activation(
                s.sf[:, 514:515], s.sf[:, 516:517], AF.Sqrt, bias=0.0, scale=1.0),
                inc="act")
            ctx.wait("v", "act", a1)
            ctx.op("v", lambda v: v.tensor_scalar(
                s.sf[:, 513:514], s.sf[:, 514:515], 1e-12, None, op0=ALU.max),
                inc="dve")
            ctx.wait("v", "dve")
            ctx.op("v", lambda v: v.reciprocal(s.sf[:, 515:516], s.sf[:, 513:514]),
                   inc="dve")
            ctx.wait("v", "dve")
            ctx.op("v", lambda v: v.tensor_scalar(
                s.sf[:, 512:513], s.sf[:, 515:516], 32.0, None, op0=ALU.mult),
                inc="dve")
            ctx.wait("v", "dve")
            ctx.op("v", lambda v, i=i, xn=xn: v.tensor_scalar(
                xn, s.x_cur[:, i * D:(i + 1) * D], s.sf[:, 512:513], None,
                op0=ALU.mult))
            d2 = ctx.op("v", lambda v, xn=xn: v.tensor_tensor(
                xn, xn, s.cb[:, 0:1024], op=ALU.mult), inc="dve")
            ctx.wait("t", "dve", d2)
            pairs = [(xn[:, j * 128:(j + 1) * 128],
                      s.xt[:, j * T + i * 128:j * T + i * 128 + 128])
                     for j in range(DC)]
            self.bf_transpose_batch(ctx, pairs, depth=2)
        ctx.wait("t", "act")

        self.snap(ctx, "l3xn", s.xt[:, :])
        self.barrier(ctx)
        wT = self.dscr_wt
        ctx.wait("g", "qs")
        ctx.dma("g", out=self.dscr_w[0:3072, 0:1024], in_=inp["L3_attn_W_qkva"])
        ctx.wait("q", "qg")
        ctx.wait("q", "dve")
        ctx.wait("q", "act")
        ctx.wait("q", "pe")
        for j in range(DC):
            sl = (j % 2) * 4096
            if j >= 2:
                ctx.wait("q", "qs", self._l3w[j - 2])
            ctx.dma("q", out=s.sbB[:, sl:sl + 3072],
                    in_=self.dscr_w[0:3072, j * 128:(j + 1) * 128], transpose=True)
            ctx.wait("q", "qs")
            if j == 0:
                self.snap(ctx, "l3pp", s.sbB[:, 0:3072])
            ft = ctx.dma("q", out=wT[j * 128:(j + 1) * 128, :],
                         in_=s.sbB[:, sl:sl + 3072])
            if not hasattr(self, "_l3w"):
                self._l3w = {}
            self._l3w[j] = ft
        self._l3w = {}
        ctx.wait("q", "qs")
        ctx.wait("g", "qs")
        if "l3wt" in DBG_TAGS:
            dbg = self.dbg_out("dbg_l3wt", [128, 3072], BF16)
            ctx.dma("q", out=dbg[:, :], in_=wT[0:128, :])
            for lst in ("t", "s", "v", "g"):
                ctx.wait(lst, "qs")

        self.barrier(ctx)
        for c in range(DC):
            ctx.wait("s", "dve")
            ctx.wait("g", "pe")
            for mi, col in enumerate((c * 128, 1024 + c * 128, 2048 + c * 128)):
                ctx.dma("g", out=s.wbuf[:, mi * 1024:(mi + 1) * 1024]
                        .rearrange("p (j n) -> p j n", n=128),
                        in_=wT[:, col:col + 128]
                        .rearrange("(j p) n -> p j n", p=P))
            ctx.wait("t", "qg")
            for mi in range(3):
                def hmm(t_, mi=mi):
                    last = None
                    for tc in range(4):
                        for j in range(DC):
                            last = t_.matmul(
                                s.PA[:, tc * 512:(tc + 1) * 512],
                                lhsT=s.wbuf[:, mi * 1024 + j * 128:
                                            mi * 1024 + (j + 1) * 128],
                                rhs=s.xt[:, j * T + tc * 512:j * T + (tc + 1) * 512],
                                start=(j == 0), stop=(j == DC - 1))
                    return last
                pe_t = ctx.op("t", hmm, inc="pe")
                ctx.wait("s", "pe", pe_t)
                if mi == 0:
                    a1 = ctx.op("s", lambda a, c=c: a.copy(
                        s.sbA[:, c * 2048:(c + 1) * 2048], s.PA[:, :]), inc="act")
                elif mi == 1:
                    a1 = ctx.op("s", lambda a: a.copy(
                        s.sbB[:, 0:2048], s.PA[:, :]), inc="act")
                else:
                    a1 = ctx.op("s", lambda a: a.activation(
                        s.sbB[:, 2048:4096], s.PA[:, :], AF.Sigmoid), inc="act")
                ctx.wait("t", "act", a1)
            if c == 0:
                self.snap(ctx, "l3w", s.wbuf[:, 0:3072])
                self.snap(ctx, "l3q", s.sbA[:, 0:2048])
                self.snap(ctx, "l3kv", s.sbB[:, 0:2048])
                self.snap(ctx, "l3a", s.sbB[:, 2048:4096])
            ctx.wait("v", "act")
            ctx.op("v", lambda v: v.tensor_tensor_scan(
                s.sbB[:, 4096:6144], s.sbB[:, 2048:4096], s.sbB[:, 0:2048],
                0.0, op0=ALU.mult, op1=ALU.add), inc="dve")
            ctx.wait("v", "dve")
            ctx.op("v", lambda v, c=c: v.tensor_tensor(
                s.sbA[:, c * 2048:(c + 1) * 2048],
                s.sbA[:, c * 2048:(c + 1) * 2048], s.sbB[:, 4096:6144],
                op=ALU.mult), inc="dve")
            if c == 0:
                self.snap(ctx, "l3y", s.sbB[:, 4096:6144])
                self.snap(ctx, "l3o", s.sbA[:, 0:2048])
            ctx.wait("g", "dve")

        self.barrier(ctx)
        self.snap(ctx, "l3oA", s.sbA[:, 1024:1280])
        ctx.wait("t", "dve")
        pairs = []
        for i in range(NT):
            for j in range(DC):
                pairs.append((s.sbA[:, j * 2048 + i * 128:j * 2048 + (i + 1) * 128],
                              s.sbB[:, i * 1024 + j * 128:i * 1024 + (j + 1) * 128]))
        self.bf_transpose_batch(ctx, pairs, depth=2)
        ctx.wait("v", "act")
        for i in range(NT):
            ctx.op("v", lambda v, i=i: v.tensor_tensor(
                s.x_cur[:, i * D:(i + 1) * D], s.x_cur[:, i * D:(i + 1) * D],
                s.sbB[:, i * 1024:(i + 1) * 1024], op=ALU.add), inc="dve")
        self.ln_batch(ctx, [(s.x_cur[:, i * D:(i + 1) * D],
                             s.x_cur[:, i * D:(i + 1) * D]) for i in range(NT)],
                      sq_scratch=s.wbuf[:, 0:1024])


    def emit_L3_ffn(self, ctx):
        s = self
        inp = self.inp
        ctx.wait("t", "qs", self.c_x_loaded)
        ctx.wait("t", "dve", self.c_dve_consts)
        ctx.wait("v", "qs", self.c_x_loaded)
        ctx.wait("s", "qs", self.c_x_loaded)
        self.transpose_to_xt(ctx, s.x_cur)
        ctx.wait("t", "act")

        def l3f_load(e):
            sl = (e % 2) * 8192
            ctx.dma("g", out=s.wbuf[:, sl:sl + 4096]
                    .rearrange("p (j nt c) -> p j nt c", nt=4, c=128),
                    in_=inp["L3_ffn_w1"][e].rearrange("(j p) (nt c) -> p j nt c",
                                                      p=P, c=128))
            return ctx.dma("g", out=s.wbuf[:, sl + 4096:sl + 8192]
                           .rearrange("p (hc c) -> p hc c", c=1024),
                           in_=inp["L3_ffn_w2"][e].rearrange("(hc p) c -> p hc c",
                                                             p=P))
        cload = {0: l3f_load(0), 1: l3f_load(1)}
        ctx.wait("v", "pe")
        ctx.op("v", lambda v: v.memset(s.sbA[:], 0.0), inc="dve")

        ctx.op("v", lambda v: v.memset(s.sbB[:, 0:1024], 0.0), inc="dve")
        ctx.wait("g", "dve")
        ctx.dma("g", out=s.sbB[0:6, 0:1024], in_=inp["L3_ffn_proj_w"])
        ctx.wait("t", "qg")
        ctx.wait("t", "dve")
        pairs = [(s.sbB[:, j * 128:(j + 1) * 128],
                  s.sbB[:, 1024 + j * 128:1024 + (j + 1) * 128]) for j in range(DC)]
        self.bf_transpose_batch(ctx, pairs, depth=2)
        ctx.wait("v", "act")
        ctx.op("v", lambda v: v.memset(s.cc[:, 2312:2376], 0.0), inc="dve")
        ctx.wait("v", "dve")
        for j in range(DC):
            ctx.op("v", lambda v, j=j: v.tensor_copy(
                s.cc[:, 2312 + j * 6:2312 + (j + 1) * 6],
                s.sbB[:, 1024 + j * 128:1024 + j * 128 + 6]), inc="dve")
        ctx.wait("t", "dve")

        for e2 in range(6):
            self.load_bcast(ctx, inp["L3_ffn_anchors"][e2], s.sf[:, 400 + e2 * 8:
                                                                400 + e2 * 8 + 6], 6)
        ctx.dma("q", out=s.sf[0:1, 448:449], in_=inp["L3_ffn_routing_temp"].unsqueeze(0))
        ctx.wait("v", "qs")
        ctx.op("v", lambda v: v.tensor_scalar(
            s.sf[0:1, 449:450], s.sf[0:1, 448:449], 0.1, None, op0=ALU.max),
            inc="dve")
        ctx.wait("v", "dve")
        ctx.op("v", lambda v: v.reciprocal(s.sf[0:1, 450:451], s.sf[0:1, 449:450]),
               inc="dve")
        ctx.wait("v", "dve")
        ctx.op("v", lambda v: v.tensor_scalar(
            s.sf[0:1, 451:452], s.sf[0:1, 450:451], 0.5, None, op0=ALU.mult),
            inc="dve")
        ctx.wait("q", "dve")
        ctx.dma("q", out=self.dscr_s[0:1, 0:1], in_=s.sf[0:1, 451:452])
        ctx.wait("q", "qs")
        ctx.dma("q", out=s.sf[:, 452:453],
                in_=self.dscr_s[0:1, 0:1].broadcast_to([P, 1]))
        ctx.wait("v", "qs")

        self.barrier(ctx)
        ctx.op("v", lambda v: v.memset(s.sf[:, 0:NT * 8], -1e30), inc="dve")
        ctx.wait("v", "dve")
        ctx.wait("t", "dve")
        for i in range(NT):
            def zmm(t_, i=i):
                last = None
                for j in range(DC):
                    last = t_.matmul(s.PD[:, 0:6],
                                     lhsT=s.xt[:, j * T + i * 128:j * T + (i + 1) * 128],
                                     rhs=s.cc[:, 2312 + j * 6:2312 + (j + 1) * 6],
                                     start=(j == 0), stop=(j == DC - 1))
                return last
            pe_t = ctx.op("t", zmm, inc="pe")
            ctx.wait("s", "pe", pe_t)
            ctx.wait("s", "dve")
            a1 = ctx.op("s", lambda a: a.activation(
                s.sf[:, 856:862], s.PD[:, 0:6], AF.Tanh, bias=0.0,
                scale=1.0 / 0.3), inc="act")
            ctx.wait("v", "act", a1)
            ctx.wait("t", "act", a1)
            for e2 in range(6):
                ctx.op("v", lambda v, e2=e2: v.tensor_tensor(
                    s.sf[:, 862:868], s.sf[:, 856:862],
                    s.sf[:, 400 + e2 * 8:400 + e2 * 8 + 6], op=ALU.mult),
                    inc="dve")
                ctx.wait("v", "dve")
                ctx.op("v", lambda v, e2=e2: v.reduce_sum(
                    s.sf[:, 868 + e2:869 + e2], s.sf[:, 862:868], axis=AX.X),
                    inc="dve")
                ctx.wait("v", "dve")
            ctx.op("v", lambda v, i=i: v.tensor_scalar(
                s.sf[:, i * 8:i * 8 + 6], s.sf[:, 868:874], -6.0, s.sf[:, 452:453],
                op0=ALU.add, op1=ALU.mult), inc="dve")
            ctx.wait("v", "dve")
        self.snap(ctx, "l3lg", s.sf[:, 0:128])
        self.emit_topk_softmax(ctx, nexp=6, logit_off=0, wfull_off=128, kk=2)
        self.snap(ctx, "l3wf", s.sf[:, 128:256])

        ra_act = ctx.cnt["act"]
        C3 = 0.044715 * np.sqrt(2.0 / np.pi)
        C1 = np.sqrt(2.0 / np.pi)
        bevring = {}
        first_bev = True
        for e in range(6):
            sl = (e % 2) * 8192
            ctx.wait("t", "qg", cload[e])
            if e == 0:
                ctx.wait("t", "act", ra_act)
            d2ring = {}
            for at in range(16):
                tc, hc = at // 4, at % 4
                pr = at % 2
                sA = s.sbB[:, 8192 + pr * 2048:8192 + pr * 2048 + 512]
                sB = s.sbB[:, 8192 + pr * 2048 + 512:8192 + pr * 2048 + 1024]
                bank = [s.PD, s.PE_][pr]
                if at >= 2:
                    ctx.wait("t", "dve", d2ring[at - 2])

                def amm(t_, hc=hc, tc=tc, sl=sl, bank=bank):
                    last = None
                    for j in range(DC):
                        last = t_.matmul(
                            bank[:, :],
                            lhsT=s.wbuf[:, sl + j * 512 + hc * 128:
                                        sl + j * 512 + (hc + 1) * 128],
                            rhs=s.xt[:, j * T + tc * 512:j * T + (tc + 1) * 512],
                            start=(j == 0), stop=(j == DC - 1))
                    return last
                pe_t = ctx.op("t", amm, inc="pe")
                ctx.wait("s", "pe", pe_t)
                a1 = ctx.op("s", lambda a, bank=bank, sA=sA: a.activation(
                    sA, bank[:, :], AF.Square), inc="act")
                ctx.wait("v", "act", a1)
                t1 = ctx.op("v", lambda v, sA=sA, sB=sB: v.tensor_scalar(
                    sB, sA, C3, C1, op0=ALU.mult, op1=ALU.add), inc="dve")
                ctx.wait("v", "dve", t1)
                d1 = ctx.op("v", lambda v, bank=bank, sB=sB: v.tensor_tensor(
                    sB, sB, bank[:, :], op=ALU.mult), inc="dve")
                ctx.wait("s", "dve", d1)
                a2 = ctx.op("s", lambda a, sA=sA, sB=sB: a.activation(
                    sA, sB, AF.Tanh), inc="act")
                ctx.wait("v", "act", a2)
                t2 = ctx.op("v", lambda v, sA=sA, sB=sB: v.tensor_scalar(
                    sB, sA, 0.5, 0.5, op0=ALU.mult, op1=ALU.add), inc="dve")
                ctx.wait("v", "dve", t2)
                d2 = ctx.op("v", lambda v, tc=tc, hc=hc, bank=bank, sB=sB:
                            v.tensor_tensor(
                                s.sbB[:, tc * 2048 + hc * 512:
                                      tc * 2048 + (hc + 1) * 512], sB,
                                bank[:, :], op=ALU.mult), inc="dve")
                d2ring[at] = d2
            ctx.wait("t", "dve")
            for bg in range(32):
                tc, g = bg // 8, bg % 8
                tt = tc * 4 + g // 2
                nh = g % 2
                gl = e * 32 + bg

                def bmm(t_, tc=tc, g=g, nh=nh, sl=sl, bg=bg):
                    last = None
                    for hc in range(4):
                        last = t_.matmul(
                            s.PA[:, (bg % 4) * 512:((bg % 4) + 1) * 512],
                            lhsT=s.sbB[:, tc * 2048 + hc * 512 + (g // 2) * 128:
                                       tc * 2048 + hc * 512 + (g // 2) * 128 + 128],
                            rhs=s.wbuf[:, sl + 4096 + hc * 1024 + nh * 512:
                                       sl + 4096 + hc * 1024 + nh * 512 + 512],
                            start=(hc == 0), stop=(hc == 3))
                    return last
                if gl >= 4:
                    ctx.wait("t", "dve", bevring[gl - 4])
                pe_t = ctx.op("t", bmm, inc="pe")
                ctx.wait("v", "pe", pe_t)
                if first_bev:
                    ctx.wait("v", "act")
                    first_bev = False
                wcol = s.sf[:, 128 + tt * 8 + e:128 + tt * 8 + e + 1]
                dst = s.sbA[:, tt * 1024 + nh * 512:tt * 1024 + (nh + 1) * 512]
                bevring[gl] = ctx.op("v", lambda v, dst=dst, wcol=wcol, bg=bg:
                                     v.scalar_tensor_tensor(
                                         dst,
                                         s.PA[:, (bg % 4) * 512:((bg % 4) + 1) * 512],
                                         wcol, dst, op0=ALU.mult, op1=ALU.add),
                                     inc="dve")
            if e + 2 < 6:
                ctx.wait("g", "pe")
                cload[e + 2] = l3f_load(e + 2)

        self.barrier(ctx)
        ctx.wait("v", "dve")
        for i in range(NT):
            ctx.op("v", lambda v, i=i: v.tensor_tensor(
                s.x_cur[:, i * D:(i + 1) * D], s.x_cur[:, i * D:(i + 1) * D],
                s.sbA[:, i * 1024:(i + 1) * 1024], op=ALU.add), inc="dve")
        self.ln_batch(ctx, [(s.x_cur[:, i * D:(i + 1) * D],
                             s.x_cur[:, i * D:(i + 1) * D]) for i in range(NT)],
                      sq_scratch=s.wbuf[:, 0:1024])


    def emit_L2_ffn(self, ctx):
        s = self
        inp = self.inp
        E2, CAP2, KK = 32, 1024, 4
        ctx.wait("t", "qs", self.c_x_loaded)
        ctx.wait("t", "dve", self.c_dve_consts)
        ctx.wait("v", "qs", self.c_x_loaded)
        ctx.wait("s", "qs", self.c_x_loaded)
        self.transpose_to_xt(ctx, s.x_cur)
        ctx.wait("t", "act")

        if L2FFN_PHASE < 0.4:
            return
        ctx.op("g", lambda g: g.iota(s.ccf[:, 280:312].bitcast(I32),
                                     pattern=[[1, 32]], base=0,
                                     channel_multiplier=0), inc="pool")
        ctx.wait("v", "pool")
        ctx.op("v", lambda v: v.tensor_copy(s.ccf[:, 320:352],
                                            s.ccf[:, 280:312].bitcast(I32)),
               inc="dve")

        if L2FFN_PHASE < 1:
            return
        self.barrier(ctx)
        self.emit_logits_hl(ctx, inp["L2_ffn_Wg"], nexp=32, bias_off=None,
                            out_stride=32)
        if L2FFN_PHASE < 1.2:
            return
        for i in range(NT):
            lg = s.sf[:, i * 32:(i + 1) * 32]
            ctx.op("v", lambda v, lg=lg: v.reduce_max(
                s.sf[:, 560:561], lg, axis=AX.X), inc="dve")
            ctx.wait("v", "dve")
            d1 = ctx.op("v", lambda v: v.tensor_scalar(
                s.sf[:, 561:562], s.sf[:, 560:561], -1.0, None, op0=ALU.mult),
                inc="dve")
            ctx.wait("s", "dve", d1)
            a1 = ctx.op("s", lambda a, lg=lg: a.activation(
                s.sf[:, 768:800], lg, AF.Exp, bias=s.sf[:, 561:562], scale=1.0,
                accum_out=s.sf[:, 562:563]), inc="act")
            ctx.wait("v", "act", a1)
            ctx.op("v", lambda v: v.reciprocal(s.sf[:, 563:564], s.sf[:, 562:563]),
                   inc="dve")
            ctx.wait("v", "dve")
            ctx.op("v", lambda v, lg=lg: v.tensor_scalar(
                lg, s.sf[:, 768:800], s.sf[:, 563:564], None, op0=ALU.mult),
                inc="dve")
            ctx.wait("s", "dve")
        if L2FFN_PHASE < 1.5:
            return
        ctx.op("v", lambda v: v.memset(s.sbB[:, 0:2048], 0.0), inc="dve")
        ctx.wait("v", "dve")
        B2 = 768
        for i in range(NT):
            lg = s.sf[:, i * 32:(i + 1) * 32]
            for k in range(KK):
                u = k * 16 + i
                oh = s.sbB[:, u * 32:(u + 1) * 32]
                mv = s.sf[:, 640 + k * 16 + i:640 + k * 16 + i + 1]
                ev = s.sf[:, 704 + k * 16 + i:704 + k * 16 + i + 1]

                def vv(emit):
                    ctx.op("v", emit, inc="dve")
                    ctx.wait("v", "dve")
                vv(lambda v, mv=mv, lg=lg: v.reduce_max(mv, lg, axis=AX.X))
                vv(lambda v, lg=lg, mv=mv: v.tensor_scalar(
                    s.sf[:, B2:B2 + 32], lg, mv, None, op0=ALU.is_equal))
                vv(lambda v: v.tensor_tensor_scan(
                    s.sf[:, B2 + 32:B2 + 64], s.sf[:, B2:B2 + 32],
                    s.sf[:, B2:B2 + 32], 0.0, op0=ALU.add, op1=ALU.bypass))
                vv(lambda v: v.tensor_scalar(
                    s.sf[:, B2 + 32:B2 + 64], s.sf[:, B2 + 32:B2 + 64], 1.0,
                    None, op0=ALU.is_equal))
                vv(lambda v: v.tensor_tensor(
                    s.sf[:, B2:B2 + 32], s.sf[:, B2:B2 + 32],
                    s.sf[:, B2 + 32:B2 + 64], op=ALU.mult))
                vv(lambda v, oh=oh: v.tensor_copy(oh, s.sf[:, B2:B2 + 32]))
                vv(lambda v: v.tensor_tensor(
                    s.sf[:, B2 + 32:B2 + 64], s.sf[:, B2:B2 + 32],
                    s.ccf[:, 320:352], op=ALU.mult))
                vv(lambda v, ev=ev: v.reduce_sum(ev, s.sf[:, B2 + 32:B2 + 64],
                                                 axis=AX.X))
                vv(lambda v, lg=lg: v.scalar_tensor_tensor(
                    lg, s.sf[:, B2:B2 + 32], -1e30, lg,
                    op0=ALU.mult, op1=ALU.add))

        if L2FFN_PHASE < 2:
            return
        self.snap(ctx, "f2oh", s.sbB[:, 0:2048])
        self.snap(ctx, "f2ws", s.sf[:, 640:768])
        self.barrier(ctx)
        ctx.wait("t", "dve")
        SC = s.sbA.bitcast(F32)
        ctx.op("g", lambda g: g.iota(s.ccf[:, 352:480].bitcast(I32),
                                     pattern=[[1, 128]], base=0,
                                     channel_multiplier=0), inc="pool")
        ctx.wait("v", "pool")
        ctx.op("v", lambda v: v.tensor_copy(s.cb[:, 0:128],
                                            s.ccf[:, 352:480].bitcast(I32)),
               inc="dve")
        ctx.wait("v", "dve")
        ctx.op("v", lambda v: v.tensor_scalar(
            s.cb[:, 0:128], s.cb[:, 0:128], s.ccf[:, 273:274], None,
            op0=ALU.is_gt), inc="dve")
        ctx.wait("v", "dve")
        ctx.op("v", lambda v: v.tensor_copy(s.sbB[:, 4224:4352], s.cb[:, 0:128]),
               inc="dve")
        ctx.op("v", lambda v: v.memset(s.sbB[:, 4128:4160], 0.0), inc="dve")
        tri_done = ctx.op("v", lambda v: v.tensor_scalar(
            s.sf[:, 838:839], s.ccf[:, 273:274], 1.0, float(E2 * CAP2),
            op0=ALU.mult, op1=ALU.add), inc="dve")
        dcl = None
        for kc in range(4):
            bank = [s.PD, s.PE_][kc % 2]
            if kc >= 2:
                ctx.wait("t", "dve", dcl)

            def cmm(t_, kc=kc, bank=bank):
                return t_.matmul(bank[0:1, 0:512], lhsT=s.cc[:, 2304:2305],
                                 rhs=s.sbB[:, kc * 512:(kc + 1) * 512],
                                 start=True, stop=True)
            pe_t = ctx.op("t", cmm, inc="pe")
            ctx.wait("v", "pe", pe_t)
            dcl = ctx.op("v", lambda v, kc=kc, bank=bank: v.tensor_copy(
                SC[0:1, kc * 512:(kc + 1) * 512], bank[0:1, 0:512]), inc="dve")
        ctx.wait("q", "dve", dcl)
        cq = ctx.dma("q", out=self.dscr_f.rearrange("a b -> (a b)")
                     [65536:67584].unsqueeze(0), in_=SC[0:1, 0:2048])
        ctx.wait("g", "qs", cq)
        ctx.wait("g", "dve")
        cg = ctx.dma("g", out=s.sbB[0:64, 4128:4160],
                     in_=self.dscr_f[64:66, 0:1024]
                     .rearrange("a (u2 e) -> (a u2) e", e=32))
        ctx.wait("t", "qg", cg)
        ctx.wait("t", "dve", tri_done)

        def bmm_(t_):
            return t_.matmul(s.PE_[:, 0:32], lhsT=s.sbB[:, 4224:4352],
                             rhs=s.sbB[:, 4128:4160], start=True, stop=True)
        pe_b = ctx.op("t", bmm_, inc="pe")
        ctx.wait("v", "pe", pe_b)
        db = ctx.op("v", lambda v: v.tensor_copy(
            s.sf[0:64, 800:832], s.PE_[0:64, 0:32]), inc="dve")
        ctx.wait("q", "dve", db)
        bq = ctx.dma("q", out=self.dscr_f[66:68, 0:1024]
                     .rearrange("a (u2 e) -> (a u2) e", e=32),
                     in_=s.sf[0:64, 800:832])
        ctx.wait("q", "qs", bq)
        bb = ctx.dma("q", out=SC[:, 2048:4096].rearrange("p (a b) -> p a b",
                                                         b=1024),
                     in_=self.dscr_f[66:68, :].unsqueeze(0)
                     .broadcast_to([P, 2, 1024]))
        self.snap(ctx, "f2cntT", s.sbB[:, 4128:4160])
        self.snap(ctx, "f2tri", s.sbB[:, 4224:4352])
        self.snap(ctx, "f2base", s.sf[:, 800:832])
        self.snap(ctx, "f2bb", SC[:, 2048:4096])
        pe_e = None
        for kc in range(4):
            def emm(t_, kc=kc):
                return t_.matmul(s.PA[:, kc * 512:(kc + 1) * 512],
                                 lhsT=s.cc[:, 2176:2304],
                                 rhs=s.sbB[:, kc * 512:(kc + 1) * 512],
                                 start=True, stop=True)
            pe_e = ctx.op("t", emm, inc="pe")
        ctx.wait("v", "pe", pe_e)
        ctx.wait("v", "qs", bb)
        ctx.op("v", lambda v: v.tensor_tensor(
            SC[:, 0:2048], s.PA[:, 0:2048], SC[:, 2048:4096], op=ALU.add),
            inc="dve")
        ctx.wait("v", "dve")
        ctx.op("v", lambda v: v.tensor_tensor(
            SC[:, 0:2048], SC[:, 0:2048], s.sbB[:, 0:2048], op=ALU.mult),
            inc="dve")
        ctx.wait("v", "dve")
        ctx.op("v", lambda v: v.reduce_sum(
            SC[:, 4096:4160], SC[:, 0:2048].rearrange("p (u e) -> p u e", e=32),
            axis=AX.X), inc="dve")
        ctx.wait("v", "dve")
        self.snap(ctx, "f2sel", SC[:, 4096:4160])
        ctx.op("v", lambda v: v.scalar_tensor_tensor(
            SC[:, 4160:4224], s.sf[:, 704:768], float(CAP2), SC[:, 4096:4160],
            op0=ALU.mult, op1=ALU.add), inc="dve")
        ctx.op("v", lambda v: v.tensor_scalar(
            SC[:, 4224:4288], SC[:, 4096:4160], float(CAP2), None,
            op0=ALU.is_ge), inc="dve")
        ctx.wait("v", "dve")
        ctx.op("v", lambda v: v.tensor_scalar(
            SC[:, 4288:4352], SC[:, 4224:4288], -1.0, 1.0,
            op0=ALU.mult, op1=ALU.add), inc="dve")
        ctx.wait("v", "dve")
        ctx.op("v", lambda v: v.tensor_tensor(
            SC[:, 4160:4224], SC[:, 4160:4224], SC[:, 4288:4352], op=ALU.mult),
            inc="dve")
        ctx.wait("v", "dve")
        ctx.op("v", lambda v: v.scalar_tensor_tensor(
            SC[:, 4160:4224], SC[:, 4224:4288], s.sf[:, 838:839],
            SC[:, 4160:4224], op0=ALU.mult, op1=ALU.add), inc="dve")
        ctx.wait("v", "dve")
        su_done = ctx.op("v", lambda v: v.tensor_copy(
            s.su[:, 0:64], SC[:, 4160:4224]), inc="dve")

        ctx.wait("g", "dve", su_done)
        self._l2s = {}
        for i in range(NT):
            slot = 2048 + (i % 2) * 1024
            if i >= 2:
                ctx.wait("v", "qg", self._l2s[i - 2])
            xd = ctx.op("v", lambda v, i=i, slot=slot: v.tensor_copy(
                s.sbB[:, slot:slot + 1024], s.x_cur[:, i * D:(i + 1) * D]),
                inc="dve")
            ctx.wait("g", "dve", xd)
            for k in range(KK):
                u = k * 16 + i
                qg_t = ctx.op("g", lambda g, u=u, slot=slot: g.indirect_dma_start(
                    out=self.dscr_a[0:E2 * CAP2 + 128, :],
                    out_offset=bass.IndirectOffsetOnAxis(ap=s.su[:, u:u + 1], axis=0),
                    in_=s.sbB[:, slot:slot + 1024], in_offset=None),
                    inc="qg", by=16)
            self._l2s[i] = qg_t
        self._l2s = {}

        if L2FFN_PHASE < 3:
            return
        self.snap(ctx, "f2su", s.su[:, 0:64])
        self.snap(ctx, "f2es", s.sf[:, 704:768])
        self.barrier(ctx)
        ctx.op("v", lambda v: v.memset(s.sbB[:, 4096:5120], 0.0), inc="dve")
        ctx.wait("g", "dve")
        ctx.dma("g", out=self.dscr_b[E2 * CAP2:E2 * CAP2 + 128, :],
                in_=s.sbB[:, 4096:5120])

        self.barrier(ctx)
        for hh2 in range(1):
            ctx.wait("g", "pe")
            ctx.dma("g", out=s.wbuf[:, 0:8192]
                    .rearrange("p (j nt c) -> p j nt c", nt=8, c=128),
                    in_=inp["L2_ffn_Ws1"][:, hh2 * 1024:(hh2 + 1) * 1024]
                    .rearrange("(j p) (nt c) -> p j nt c", p=P, c=128))
            ctx.dma("g", out=s.wbuf[:, 8192:16384]
                    .rearrange("p (j nt c) -> p j nt c", nt=8, c=128),
                    in_=inp["L2_ffn_Ws3"][:, hh2 * 1024:(hh2 + 1) * 1024]
                    .rearrange("(j p) (nt c) -> p j nt c", p=P, c=128))
            ctx.wait("t", "qg")
            for tc in range(4):
                for hc in range(8):
                    def h1mm(t_, hc=hc, tc=tc):
                        last = None
                        for j in range(DC):
                            last = t_.matmul(
                                s.PD[:, :],
                                lhsT=s.wbuf[:, j * 1024 + hc * 128:
                                            j * 1024 + (hc + 1) * 128],
                                rhs=s.xt[:, j * T + tc * 512:j * T + (tc + 1) * 512],
                                start=(j == 0), stop=(j == DC - 1))
                        for j in range(DC):
                            last = t_.matmul(
                                s.PE_[:, :],
                                lhsT=s.wbuf[:, 8192 + j * 1024 + hc * 128:
                                            8192 + j * 1024 + (hc + 1) * 128],
                                rhs=s.xt[:, j * T + tc * 512:j * T + (tc + 1) * 512],
                                start=(j == 0), stop=(j == DC - 1))
                        return last
                    p1 = ctx.op("t", h1mm, inc="pe")
                    ctx.wait("s", "pe", p1)
                    a1 = ctx.op("s", lambda a: a.activation(
                        s.sbB[:, 8192:8704], s.PD[:, :], AF.Silu), inc="act")
                    ctx.wait("v", "act", a1)
                    d1 = ctx.op("v", lambda v, hc=hc: v.tensor_tensor(
                        s.sbB[:, 4096 + hc * 512:4096 + (hc + 1) * 512],
                        s.sbB[:, 8192:8704], s.PE_[:, :], op=ALU.mult), inc="dve")
                    ctx.wait("t", "dve", d1)
                ctx.wait("g", "pe")
                ctx.dma("g", out=s.wbuf[:, 0:8192]
                        .rearrange("p (nt c) -> p nt c", c=1024),
                        in_=inp["L2_ffn_Ws2"][hh2 * 1024:(hh2 + 1) * 1024, :]
                        .rearrange("(nt p) c -> p nt c", p=P))
                ctx.wait("t", "qg")
                ctx.wait("t", "act")
                for g in range(8):
                    tt = tc * 4 + g // 2
                    nh = g % 2

                    def bmm(t_, g=g, tt=tt, nh=nh):
                        last = None
                        for hc in range(8):
                            last = t_.matmul(
                                s.PA[:, (g % 4) * 512:((g % 4) + 1) * 512],
                                lhsT=s.sbB[:, 4096 + hc * 512 + (g // 2) * 128:
                                           4096 + hc * 512 + (g // 2) * 128 + 128],
                                rhs=s.wbuf[:, hc * 1024 + nh * 512:
                                           hc * 1024 + nh * 512 + 512],
                                start=(hc == 0), stop=(hc == 7))
                        return last
                    if g >= 4:
                        ctx.wait("t", "dve")
                    pe_t = ctx.op("t", bmm, inc="pe")
                    ctx.wait("v", "pe", pe_t)
                    if hh2 == 0:
                        ctx.op("v", lambda v, g=g, tt=tt, nh=nh: v.tensor_copy(
                            s.sbA[:, tt * 1024 + nh * 512:tt * 1024 + (nh + 1) * 512],
                            s.PA[:, (g % 4) * 512:((g % 4) + 1) * 512]), inc="dve")
                    else:
                        ctx.op("v", lambda v, g=g, tt=tt, nh=nh: v.tensor_tensor(
                            s.sbA[:, tt * 1024 + nh * 512:tt * 1024 + (nh + 1) * 512],
                            s.sbA[:, tt * 1024 + nh * 512:tt * 1024 + (nh + 1) * 512],
                            s.PA[:, (g % 4) * 512:((g % 4) + 1) * 512],
                            op=ALU.add), inc="dve")
                ctx.wait("t", "dve")
                if tc < 3:
                    ctx.wait("g", "pe")
                    ctx.dma("g", out=s.wbuf[:, 0:8192]
                            .rearrange("p (j nt c) -> p j nt c", nt=8, c=128),
                            in_=inp["L2_ffn_Ws1"][:, hh2 * 1024:(hh2 + 1) * 1024]
                            .rearrange("(j p) (nt c) -> p j nt c", p=P, c=128))
                    ctx.wait("t", "qg")

        if L2FFN_PHASE < 4:
            return
        self.barrier(ctx)
        ctx.wait("q", "qg")
        W1, W2, W3 = inp["L2_ffn_W1"], inp["L2_ffn_W2"], inp["L2_ffn_W3"]

        def l2e_gather(e):
            return ctx.dma(
                "q", out=s.sbB[:, 0:8192].rearrange("p (r c) -> p r c", c=1024),
                in_=self.dscr_a[e * CAP2:(e + 1) * CAP2, :]
                .rearrange("(r p) c -> p r c", p=P))

        def l2e_w13(e):
            ctx.dma("g", out=s.wbuf[:, 0:4096]
                    .rearrange("p (j nt c) -> p j nt c", nt=4, c=128),
                    in_=W1[e].rearrange("(j p) (nt c) -> p j nt c", p=P, c=128))
            return ctx.dma("g", out=s.wbuf[:, 4096:8192]
                           .rearrange("p (j nt c) -> p j nt c", nt=4, c=128),
                           in_=W3[e].rearrange("(j p) (nt c) -> p j nt c",
                                               p=P, c=128))

        def l2e_w2(e):
            return ctx.dma("g", out=s.wbuf[:, 8192:12288]
                           .rearrange("p (hc c) -> p hc c", c=1024),
                           in_=W2[e].rearrange("(hc p) c -> p hc c", p=P))

        def l2e_transposes(e, gq):
            ctx.wait("t", "qs", gq)
            X0 = (e % 2) * 8192
            pairs = []
            for tb in range(8):
                for j in range(DC):
                    pairs.append(
                        (s.sbB[:, tb * 1024 + j * 128:tb * 1024 + (j + 1) * 128],
                         s.xt[:, X0 + j * CAP2 + tb * 128:
                              X0 + j * CAP2 + (tb + 1) * 128]))
            self.bf_transpose_batch(ctx, pairs, depth=2)

        ctx.wait("q", "pe")
        gq = l2e_gather(0)
        ctx.wait("g", "pe")
        ctx.wait("g", "dve")
        c13 = l2e_w13(0)
        c2 = l2e_w2(0)
        l2e_transposes(0, gq)
        self._l2e, self._l2eq = {}, {}
        for e in range(E2):
            X0 = (e % 2) * 8192
            ctx.wait("t", "qg", c13)
            d1ring = {}
            for ui in range(8):
                hc, mi = ui // 2, ui % 2
                m0 = mi * 512
                pr = ui % 2
                pa_a = s.PA[:, pr * 512:pr * 512 + 512]
                pa_b = s.PA[:, 1024 + pr * 512:1024 + pr * 512 + 512]
                if ui >= 2:
                    ctx.wait("t", "dve", d1ring[ui - 2])

                def amm(t_, hc=hc, m0=m0, X0=X0, pa_a=pa_a, pa_b=pa_b):
                    last = None
                    for j in range(DC):
                        last = t_.matmul(
                            pa_a,
                            lhsT=s.wbuf[:, j * 512 + hc * 128:
                                        j * 512 + (hc + 1) * 128],
                            rhs=s.xt[:, X0 + j * CAP2 + m0:X0 + j * CAP2 + m0 + 512],
                            start=(j == 0), stop=(j == DC - 1))
                    for j in range(DC):
                        last = t_.matmul(
                            pa_b,
                            lhsT=s.wbuf[:, 4096 + j * 512 + hc * 128:
                                        4096 + j * 512 + (hc + 1) * 128],
                            rhs=s.xt[:, X0 + j * CAP2 + m0:X0 + j * CAP2 + m0 + 512],
                            start=(j == 0), stop=(j == DC - 1))
                    return last
                p1 = ctx.op("t", amm, inc="pe")
                ctx.wait("s", "pe", p1)
                a1 = ctx.op("s", lambda a, pr=pr, pa_a=pa_a: a.activation(
                    s.sbB[:, 12288 + pr * 512:12288 + pr * 512 + 512], pa_a,
                    AF.Silu), inc="act")
                ctx.wait("v", "act", a1)
                d1 = ctx.op("v", lambda v, hc=hc, m0=m0, pr=pr, pa_b=pa_b:
                            v.tensor_tensor(
                                s.sbB[:, 8192 + hc * 1024 + m0:
                                      8192 + hc * 1024 + m0 + 512],
                                s.sbB[:, 12288 + pr * 512:12288 + pr * 512 + 512],
                                pa_b, op=ALU.mult), inc="dve")
                d1ring[ui] = d1
            ctx.wait("t", "dve")
            if e + 1 < E2:
                ctx.wait("g", "pe")
                c13 = l2e_w13(e + 1)
                ctx.wait("q", "pe")
                gq = l2e_gather(e + 1)
            ctx.wait("t", "qg", c2)
            for g in range(16):
                gg = e * 16 + g
                tb, nh = g // 2, g % 2

                def bmm(t_, g=g, tb=tb, nh=nh):
                    last = None
                    for hc in range(4):
                        last = t_.matmul(
                            s.PD[:, :] if g % 2 == 0 else s.PE_[:, :],
                            lhsT=s.sbB[:, 8192 + hc * 1024 + tb * 128:
                                       8192 + hc * 1024 + tb * 128 + 128],
                            rhs=s.wbuf[:, 8192 + hc * 1024 + nh * 512:
                                       8192 + hc * 1024 + nh * 512 + 512],
                            start=(hc == 0), stop=(hc == 3))
                    return last
                if gg >= 2:
                    ctx.wait("t", "act", self._l2e[gg - 2])
                pe_t = ctx.op("t", bmm, inc="pe")
                ctx.wait("s", "pe", pe_t)
                if gg >= 2:
                    ctx.wait("s", "qg", self._l2eq[gg - 2])
                a2 = ctx.op("s", lambda a, g=g: a.copy(
                    s.sbB[:, 13312 + (g % 2) * 512:13312 + (g % 2) * 512 + 512],
                    s.PD[:, :] if g % 2 == 0 else s.PE_[:, :]), inc="act")
                ctx.wait("g", "act", a2)
                qt = ctx.op("g", lambda gg_, e=e, tb=tb, nh=nh, g=g: gg_.dma_start(
                    out=self.dscr_b[e * CAP2 + tb * 128:e * CAP2 + (tb + 1) * 128,
                                    nh * 512:(nh + 1) * 512],
                    in_=s.sbB[:, 13312 + (g % 2) * 512:13312 + (g % 2) * 512 + 512]),
                    inc="qg", by=16)
                self._l2e[gg] = a2
                self._l2eq[gg] = qt
            if e + 1 < E2:
                l2e_transposes(e + 1, gq)
                ctx.wait("g", "pe")
                c2 = l2e_w2(e + 1)
        self._l2e, self._l2eq = {}, {}

        if L2FFN_PHASE < 5:
            return
        self.barrier(ctx)
        ctx.wait("g", "qg")
        for i in range(NT):
            for k in range(KK):
                u = k * 16 + i
                gslot = s.sbB[:, k * 1024:(k + 1) * 1024]
                ctx.wait("g", "dve")
                ctx.op("g", lambda g, u=u, gslot=gslot: g.indirect_dma_start(
                    out=gslot, out_offset=None,
                    in_=self.dscr_b[0:E2 * CAP2 + 128, :],
                    in_offset=bass.IndirectOffsetOnAxis(ap=s.su[:, u:u + 1], axis=0)),
                    inc="qg", by=16)
            ctx.wait("v", "qg")
            ctx.op("v", lambda v, i=i: v.tensor_tensor(
                s.cb[:, 0:1024], s.x_cur[:, i * D:(i + 1) * D],
                s.sbA[:, i * 1024:(i + 1) * 1024], op=ALU.add), inc="dve")
            for k in range(KK):
                wcol = s.sf[:, 640 + k * 16 + i:640 + k * 16 + i + 1]
                dst = (s.cb[:, 0:1024] if k < KK - 1
                       else s.x_cur[:, i * D:(i + 1) * D])
                ctx.op("v", lambda v, k=k, wcol=wcol, dst=dst: v.scalar_tensor_tensor(
                    dst, s.sbB[:, k * 1024:(k + 1) * 1024], wcol,
                    s.cb[:, 0:1024], op0=ALU.mult, op1=ALU.add), inc="dve")
        self.ln_batch(ctx, [(s.x_cur[:, i * D:(i + 1) * D],
                             s.x_cur[:, i * D:(i + 1) * D]) for i in range(NT)],
                      sq_scratch=s.wbuf[:, 0:1024])







def _weight_map(inputs, blocks):
    m = {}
    for b in blocks:
        for name, shape in BLOCK_WEIGHTS[b]:
            a = np.ascontiguousarray(np.asarray(inputs[name], np.float32))
            if a.ndim == 0:
                a = a.reshape(1)
            m[name] = a
    return m


FUSED = True


def kernel(**inputs):
    x = np.ascontiguousarray(np.asarray(inputs["L0_attn_x"], np.float32))
    if FUSED:
        prog = Prog(BLOCK_ORDER)
        nc = prog.build()
        wm = _weight_map(inputs, BLOCK_ORDER)
        in_maps = [dict(wm, x_in=x[c]) for c in range(NCORES)]
        res = run_bass_kernel_spmd(nc, in_maps, core_ids=list(range(NCORES)))
        cur = [np.asarray(res.results[c]["x_out"]) for c in range(NCORES)]
        return np.stack(cur).astype(np.float32)
    cur = [x[c] for c in range(NCORES)]
    for bname in BLOCK_ORDER:
        prog = Prog([bname])
        nc = prog.build()
        wm = _weight_map(inputs, [bname])
        in_maps = [dict(wm, x_in=cur[c]) for c in range(NCORES)]
        res = run_bass_kernel_spmd(nc, in_maps, core_ids=list(range(NCORES)))
        cur = [np.asarray(res.results[c]["x_out"]) for c in range(NCORES)]
    return np.stack(cur).astype(np.float32)
